# version 01
# baseline (speedup 1.0000x reference)
import numpy as np
from contextlib import ExitStack

import concourse.bass as bass
import concourse.mybir as mybir
from concourse.bass_utils import run_bass_kernel_spmd

F32 = mybir.dt.float32
BF16 = mybir.dt.bfloat16
AF = mybir.ActivationFunctionType
ALU = mybir.AluOpType
AX = mybir.AxisListType

NCORES = 8
DM = 1024
CT = DM // 128
T = 2048
CH = 512
NCH = T // CH
NH = 16
HD = 64
EPS = 1e-5

ENGS = ("tensor", "scalar", "vector", "sync", "gpsimd")


class Prog:
    def __init__(self, nc):
        self.nc = nc
        self.ops = {e: [] for e in ENGS}
        self.sems = {}
        self.lastwait = {}

    def sem(self, name):
        if name not in self.sems:
            h = self.nc.alloc_semaphore(name=name)
            self.sems[name] = [h, 0]
        return name

    def wait(self, eng, tok):
        if tok is None:
            return
        sname, val = tok
        if val <= 0:
            return
        if self.lastwait.get((eng, sname), -1) >= val:
            return
        self.lastwait[(eng, sname)] = val
        self.ops[eng].append(("w", self.sems[sname][0], val, sname))

    def waits(self, eng, toks):
        best = {}
        for t in toks:
            if t is None:
                continue
            if t[0] not in best or best[t[0]] < t[1]:
                best[t[0]] = t[1]
        for s, v in best.items():
            self.wait(eng, (s, v))

    def op(self, eng, fn, inc=None, amt=1):
        if eng in ("vector", "scalar"):
            sname = "s_" + eng
            self.wait(eng, (sname, self.sems[sname][1]))
            inc = sname
        tok = None
        h = None
        if inc is not None:
            self.sem(inc)
            s = self.sems[inc]
            s[1] += amt
            tok = (inc, s[1])
            h = s[0]
        self.ops[eng].append(("c", fn, h, amt))
        return tok

    def dma(self, eng, out, in_, sem):
        self.sem(sem)
        s = self.sems[sem]
        s[1] += 16
        tok = (sem, s[1])
        self.ops[eng].append(("d", out, in_, s[0]))
        return tok

    def cur(self, semname):
        if semname not in self.sems:
            return None
        return (semname, self.sems[semname][1])

    def barrier(self):
        for e in ENGS:
            self.op(e, lambda p: p.nop(), inc="bar_" + e)
        toks = [self.cur("bar_" + e) for e in ENGS]
        toks += [self.cur(s) for s in list(self.sems) if not s.startswith("bar_")]
        for e in ENGS:
            for t in toks:
                self.wait(e, t)

    def check(self):
        vals = {name: 0 for name in self.sems}
        idx = {e: 0 for e in ENGS}
        h2n = {id(v[0]): k for k, v in self.sems.items()}
        progress = True
        while progress:
            progress = False
            for e in ENGS:
                while idx[e] < len(self.ops[e]):
                    it = self.ops[e][idx[e]]
                    if it[0] == "w":
                        if vals[it[3]] < it[2]:
                            break
                    elif it[0] == "d":
                        vals[h2n[id(it[3])]] += 16
                    else:
                        if it[2] is not None:
                            vals[h2n[id(it[2])]] += it[3]
                    idx[e] += 1
                    progress = True
        stuck = {e: (idx[e], len(self.ops[e])) for e in ENGS if idx[e] < len(self.ops[e])}
        if stuck:
            det = []
            for e, (i, n) in stuck.items():
                it = self.ops[e][i]
                det.append(f"{e}@{i}/{n} waiting {it[3]}>={it[2]} (now {vals[it[3]]})")
            raise RuntimeError("deadlock: " + "; ".join(det))

    def build(self):
        self.check()
        with self.nc.Block() as blk:
            for eng in ENGS:
                if not self.ops[eng]:
                    continue
                deco = getattr(blk, eng)

                def body(proxy, _e=eng):
                    for it in self.ops[_e]:
                        if it[0] == "w":
                            proxy.wait_ge(it[1], it[2])
                        elif it[0] == "d":
                            proxy.dma_start(it[1], it[2]).then_inc(it[3], 16)
                        else:
                            ins = it[1](proxy)
                            if it[2] is not None:
                                ins.then_inc(it[2], it[3])

                deco(body)


class Ctx:
    def __init__(self, nc):
        self.nc = nc
        self.P = Prog(nc)
        for e in ENGS:
            self.P.sem("s_" + e)
        self.es = ExitStack()
        self.uid = 0

    def name(self, base):
        self.uid += 1
        return f"{base}_{self.uid}"

    def sbuf(self, es, shape, dtype, base="t"):
        return es.enter_context(self.nc.sbuf_tensor(self.name(base), shape, dtype))

    def psum(self, es, shape, dtype, base="p"):
        return es.enter_context(self.nc.psum_tensor(self.name(base), shape, dtype))


def emit_common_setup(C, es):
    nc, P = C.nc, C.P
    C.resid = [C.sbuf(es, [128, T], F32, "resid") for _ in range(CT)]
    C.ones_f = C.sbuf(es, [128, 128], F32, "onesf")
    C.ones_b = C.sbuf(es, [128, 128], BF16, "onesb")
    C.id_b = C.sbuf(es, [128, 128], BF16, "idb")

    t2 = P.dma("gpsimd", C.id_b[:], C.ins["c_ident"], "d_misc2")
    P.op("vector", lambda p: p.memset(C.ones_f[:], 1.0))
    P.op("vector", lambda p: p.memset(C.ones_b[:], 1.0), inc="s_vector")
    P.wait("vector", t2)

    for c in range(CT):
        P.dma("sync", C.resid[c][:], C.ins["resid_in"][c * 128:(c + 1) * 128, :], "d_resid")


def emit_ln(C, es, make_xn=True, xn_es=None, stats_es=None):
    nc, P = C.nc, C.P
    if make_xn:
        C.xn = [C.sbuf(xn_es, [128, T], BF16, "xn") for _ in range(CT)]
    if stats_es is None:
        stats_es = es
    C.mrow = C.sbuf(stats_es, [1, T], F32, "mrow")
    C.rrow = C.sbuf(stats_es, [1, T], F32, "rrow")
    les = ExitStack()
    sq = [C.sbuf(les, [128, CH], F32, "lnsq") for _ in range(2)]
    ps_s = C.psum(les, [128, CH], F32, "lnpsS")
    ps_q = C.psum(les, [128, CH], F32, "lnpsQ")
    ps_m = C.psum(les, [128, CH], F32, "lnpsM")
    ps_r = C.psum(les, [128, CH], F32, "lnpsR")
    trow = C.sbuf(les, [1, CH], F32, "lnt1")
    trow2 = C.sbuf(les, [1, CH], F32, "lnt2")

    sq_mm = [None, None]
    prev_vec = None
    prev_sqrt = None
    prev_recip = None
    for ch in range(NCH):
        sl = slice(ch * CH, (ch + 1) * CH)
        for c in range(CT):
            P.wait("scalar", sq_mm[c % 2])
            sq_tok = P.op("scalar", lambda p, c=c, sl=sl: p.activation(
                sq[c % 2][:], C.resid[c][:, sl], AF.Square), inc="s_scalar")
            if c == 0:
                P.wait("tensor", prev_vec)
            P.op("tensor", lambda p, c=c, sl=sl: p.matmul(
                ps_s[0:1, :], C.ones_f[:, 0:1], C.resid[c][:, sl],
                start=(c == 0), stop=(c == CT - 1)))
            P.wait("tensor", sq_tok)
            sq_mm[c % 2] = P.op("tensor", lambda p, c=c: p.matmul(
                ps_q[0:1, :], C.ones_f[:, 0:1], sq[c % 2][:],
                start=(c == 0), stop=(c == CT - 1)), inc="s_tensor")
        mm_done = P.cur("s_tensor")
        P.wait("vector", mm_done)
        P.wait("vector", prev_sqrt)
        P.op("vector", lambda p, sl=sl: p.tensor_scalar(
            C.mrow[:, sl], ps_s[0:1, :], 1.0 / DM, None, ALU.mult))
        P.op("vector", lambda p: p.tensor_scalar(
            trow[:], ps_q[0:1, :], 1.0 / DM, None, ALU.mult), inc="s_vector")
        mtok = P.cur("s_vector")
        P.wait("scalar", mtok)
        P.wait("scalar", prev_recip)
        sqm = P.op("scalar", lambda p, sl=sl: p.activation(
            trow2[:], C.mrow[0:1, sl], AF.Square), inc="s_scalar")
        P.wait("vector", sqm)
        P.op("vector", lambda p: p.tensor_tensor(
            trow[:], trow[:], trow2[:], ALU.subtract), inc="s_vector")
        P.wait("scalar", P.cur("s_vector"))
        prev_sqrt = P.op("scalar", lambda p: p.activation(
            trow2[:], trow[0:1, :], AF.Sqrt, bias=np.float32(EPS)), inc="s_scalar")
        P.wait("vector", prev_sqrt)
        prev_recip = P.op("vector", lambda p, sl=sl: p.reciprocal(
            C.rrow[:, sl], trow2[:]), inc="s_vector")

        if make_xn:
            P.wait("tensor", prev_recip)
            P.op("tensor", lambda p, sl=sl: p.matmul(
                ps_m[:], C.ones_f[0:1, :], C.mrow[0:1, sl], start=True, stop=True))
            bt = P.op("tensor", lambda p, sl=sl: p.matmul(
                ps_r[:], C.ones_f[0:1, :], C.rrow[0:1, sl], start=True, stop=True),
                inc="s_tensor")
            P.wait("vector", bt)
            for c in range(CT):
                P.op("vector", lambda p, c=c, sl=sl: p.tensor_tensor(
                    sq[c % 2][:], C.resid[c][:, sl], ps_m[:], ALU.subtract))
                P.op("vector", lambda p, c=c, sl=sl: p.tensor_tensor(
                    C.xn[c][:, sl], sq[c % 2][:], ps_r[:], ALU.mult), inc="s_vector")
            sq_mm[0] = sq_mm[1] = P.cur("s_vector")
        prev_vec = P.cur("s_vector")
    les.close()


def emit_dense_to_scratch(C, es, w_name, nout, src_tiles, dst, dst_row0=0):
    nc, P = C.nc, C.P
    les = ExitStack()
    no = nout // 128
    wfull = C.sbuf(les, [128, CT * nout], BF16, "dpw")
    stage = [C.sbuf(les, [128, T], BF16, "dpst") for _ in range(2)]
    pss = [C.psum(les, [128, CH], F32, "dpps") for _ in range(2)]
    w = C.ins[w_name].rearrange("(a p) o -> p a o", p=128)
    wdst = wfull[:].rearrange("p (a o) -> p a o", o=nout)
    wtoks = [P.dma("gpsimd", wdst[:, a:a + 1, :], w[:, a:a + 1, :], f"d_dw{a % 4}")
             for a in range(CT)]
    P.waits("tensor", wtoks)

    ev_tok = [None, None]
    last_store = [None, None]
    for o in range(no):
        b = o % 2
        for ch in range(NCH):
            pb = ch % 2
            P.wait("tensor", ev_tok[pb])
            for c in range(CT):
                P.op("tensor", lambda p, c=c, ch=ch, pb=pb, o=o: p.matmul(
                    pss[pb][:], wdst[:, c, o * 128:(o + 1) * 128],
                    src_tiles[c][:, ch * CH:(ch + 1) * CH],
                    start=(c == 0), stop=(c == CT - 1)),
                    inc="s_tensor" if c == CT - 1 else None)
            P.wait("scalar", P.cur("s_tensor"))
            P.wait("scalar", last_store[b])
            ev_tok[pb] = P.op("scalar", lambda p, b=b, ch=ch, pb=pb: p.activation(
                stage[b][:, ch * CH:(ch + 1) * CH], pss[pb][:], AF.Copy), inc="s_scalar")
        P.wait("sync", ev_tok[0])
        P.wait("sync", ev_tok[1])
        r0 = dst_row0 + o * 128
        last_store[b] = P.dma("sync", dst[r0:r0 + 128, :], stage[b][:], f"d_ds{b}")
    P.waits("sync", [t for t in last_store if t is not None])
    les.close()


def emit_attention(C, es, layer, ctx_es):
    nc, P = C.nc, C.P
    les = ExitStack()
    C.ctx = [C.sbuf(ctx_es, [128, T], BF16, "ctx") for _ in range(CT)]

    qa = [C.sbuf(les, [128, T], BF16, "qa") for _ in range(CT)]
    kaug = [C.sbuf(les, [128, T], BF16, "kaug") for _ in range(2)]
    vfeat = [C.sbuf(les, [64, T], BF16, "vfeat") for _ in range(2)]
    vtok = [[C.sbuf(les, [128, 80], BF16, "vtok") for _ in range(16)] for _ in range(2)]
    sqk = C.sbuf(les, [64, T], BF16, "sqk")
    mxc = C.sbuf(les, [1, NCH], F32, "mxc")
    mx2 = C.sbuf(les, [1, 1], F32, "mx2")
    nrow = C.sbuf(les, [1, CH], F32, "nrow")
    nrow2 = C.sbuf(les, [1, CH], F32, "nrow2")
    qaug = [C.sbuf(les, [128, CH], BF16, "qaug") for _ in range(2)]
    sq2 = C.sbuf(les, [64, CH], BF16, "sq2")
    pt = [C.sbuf(les, [128, CH], BF16, "pt") for _ in range(2)]
    rb = C.sbuf(les, [128, CH], F32, "rb")

    ps_s = [C.psum(les, [128, CH], F32, "psS") for _ in range(2)]
    ps_c = C.psum(les, [128, CH], F32, "psC")
    ps_b = C.psum(les, [128, CH], F32, "psB")
    ps_n = C.psum(les, [128, CH], F32, "psN")
    ps_tr = C.psum(les, [128, 512], BF16, "psTr")

    if layer == 1:
        rt = C.sbuf(les, [64, 64], BF16, "rotT")
        rtok = P.dma("gpsimd", rt[:], C.ins["l1_rot_t"], "d_rot")
        P.wait("tensor", rtok)
        kr_ps = [C.psum(les, [128, CH], F32, "psKr") for _ in range(2)]

    def rows_q(h):
        return h * 64 if layer == 0 else h * 192

    def rows_k(h):
        return 1024 + h * 64 if layer == 0 else h * 192 + 64

    def rows_v(h):
        return 2048 + h * 64 if layer == 0 else h * 192 + 128

    for c in range(CT):
        h0, h1 = 2 * c, 2 * c + 1
        P.dma("sync", qa[c][0:64, :], C.qkv_s[rows_q(h0):rows_q(h0) + 64, :], "d_qa")
        P.dma("sync", qa[c][64:128, :], C.qkv_s[rows_q(h1):rows_q(h1) + 64, :], "d_qa")
    qa_tok = P.cur("d_qa")

    for par in range(2):
        for kt in range(16):
            P.op("vector", lambda p, par=par, kt=kt: p.memset(vtok[par][kt][:, 64:65], 1.0))
    P.op("vector", lambda p: p.memset(kaug[0][64:65, :], 1.0))
    P.op("vector", lambda p: p.memset(kaug[1][64:65, :], 1.0), inc="s_vector")
    ones_done = P.cur("s_vector")

    k_load = [None, None]
    v_load = [None, None]
    k_free = [None, None]
    v_free = [None, None]
    vt_free = [None, None]

    k_load[0] = P.dma("sync", kaug[0][0:64, :], C.qkv_s[rows_k(0):rows_k(0) + 64, :], "d_k0")
    v_load[0] = P.dma("sync", vfeat[0][:], C.qkv_s[rows_v(0):rows_v(0) + 64, :], "d_v0")

    for h in range(NH):
        par = h % 2
        if h + 1 < NH:
            np_ = (h + 1) % 2
            P.wait("sync", k_free[np_])
            k_load[np_] = P.dma("sync", kaug[np_][0:64, :],
                                C.qkv_s[rows_k(h + 1):rows_k(h + 1) + 64, :], f"d_k{np_}")
            P.wait("sync", v_free[np_])
            v_load[np_] = P.dma("sync", vfeat[np_][:],
                                C.qkv_s[rows_v(h + 1):rows_v(h + 1) + 64, :], f"d_v{np_}")

        P.wait("tensor", k_load[par])
        if layer == 1:
            kr_ev = [None, None]
            for ch in range(NCH):
                sl = slice(ch * CH, (ch + 1) * CH)
                P.wait("tensor", kr_ev[ch % 2])
                t = P.op("tensor", lambda p, par=par, sl=sl, ch=ch: p.matmul(
                    kr_ps[ch % 2][0:64, :], rt[:], kaug[par][0:64, sl],
                    start=True, stop=True), inc="s_tensor")
                P.wait("scalar", t)
                kr_ev[ch % 2] = P.op("scalar", lambda p, par=par, sl=sl, ch=ch: p.activation(
                    kaug[par][0:64, sl], kr_ps[ch % 2][0:64, :], AF.Copy), inc="s_scalar")
            P.wait("scalar", P.cur("s_scalar"))
        sct = P.op("scalar", lambda p, par=par: p.activation(
            sqk[:], kaug[par][0:64, :], AF.Square), inc="s_scalar")
        P.wait("tensor", sct)
        mx_prev = None
        for ch in range(NCH):
            P.wait("tensor", mx_prev)
            t = P.op("tensor", lambda p, ch=ch: p.matmul(
                ps_n[0:1, :], C.ones_b[0:64, 0:1], sqk[:, ch * CH:(ch + 1) * CH],
                start=True, stop=True), inc="s_tensor")
            P.wait("vector", t)
            mx_prev = P.op("vector", lambda p, ch=ch: p.tensor_reduce(
                mxc[0:1, ch:ch + 1], ps_n[0:1, :], AX.X, ALU.max), inc="s_vector")
        mxtok = P.op("vector", lambda p: p.tensor_reduce(
            mx2[:], mxc[0:1, :], AX.X, ALU.max), inc="s_vector")

        P.wait("tensor", v_load[par])
        P.wait("vector", ones_done)
        P.wait("vector", vt_free[par])
        for kt in range(16):
            slot = kt % 4
            t = P.op("tensor", lambda p, par=par, kt=kt, slot=slot: p.transpose(
                ps_tr[:, slot * 128:slot * 128 + 64],
                vfeat[par][:, kt * 128:(kt + 1) * 128], C.id_b[0:64, 0:64]),
                inc="s_tensor")
            P.wait("vector", t)
            P.op("vector", lambda p, par=par, kt=kt, slot=slot: p.tensor_copy(
                vtok[par][kt][:, 0:64], ps_tr[:, slot * 128:slot * 128 + 64]),
                inc="s_vector")
            if kt >= 3:
                P.wait("tensor", P.cur("s_vector"))
        v_free[par] = P.cur("s_tensor")
        vt_ready = P.cur("s_vector")

        for ch in range(NCH):
            sl = slice(ch * CH, (ch + 1) * CH)
            qb = ch % 2
            P.wait("vector", mxtok)
            P.wait("vector", qa_tok)
            cp = P.op("vector", lambda p, h=h, sl=sl, qb=qb: p.tensor_copy(
                qaug[qb][0:64, :], qa[h // 2][(h % 2) * 64:(h % 2) * 64 + 64, sl]),
                inc="s_vector")
            if layer == 1:
                P.wait("tensor", cp)
                t = P.op("tensor", lambda p, qb=qb: p.matmul(
                    ps_b[0:64, :], rt[:], qaug[qb][0:64, :], start=True, stop=True),
                    inc="s_tensor")
                P.wait("scalar", t)
                qrot = P.op("scalar", lambda p, qb=qb: p.activation(
                    qaug[qb][0:64, :], ps_b[0:64, :], AF.Copy), inc="s_scalar")
            else:
                qrot = cp
            P.wait("scalar", qrot)
            sct = P.op("scalar", lambda p, qb=qb: p.activation(
                sq2[:], qaug[qb][0:64, :], AF.Square), inc="s_scalar")
            P.wait("tensor", sct)
            qn = P.op("tensor", lambda p: p.matmul(
                ps_n[0:1, :], C.ones_b[0:64, 0:1], sq2[:], start=True, stop=True),
                inc="s_tensor")
            P.wait("vector", qn)
            P.op("vector", lambda p: p.tensor_scalar(
                nrow[:], ps_n[0:1, :], mx2[0:1, 0:1], None, ALU.mult), inc="s_vector")
            P.wait("scalar", P.cur("s_vector"))
            st = P.op("scalar", lambda p: p.activation(
                nrow2[:], nrow[0:1, :], AF.Sqrt), inc="s_scalar")
            P.wait("vector", st)
            rowtok = P.op("vector", lambda p, qb=qb: p.tensor_scalar(
                qaug[qb][64:65, :], nrow2[0:1, :], -1.0, None, ALU.mult), inc="s_vector")
            P.wait("tensor", rowtok)
            P.wait("tensor", vt_ready)

            smm = [None] * 16
            exp_toks = [None, None]

            def emit_s(kt):
                sb = kt % 2
                P.wait("tensor", exp_toks[sb])
                smm[kt] = P.op("tensor", lambda p, par=par, kt=kt, sb=sb, qb=qb: p.matmul(
                    ps_s[sb][:], kaug[par][0:65, kt * 128:(kt + 1) * 128],
                    qaug[qb][0:65, :], start=True, stop=True), inc="s_tensor")

            emit_s(0)
            for kt in range(16):
                if kt + 1 < 16:
                    emit_s(kt + 1)
                sb = kt % 2
                P.wait("scalar", smm[kt])
                exp_toks[sb] = P.op("scalar", lambda p, sb=sb: p.activation(
                    pt[sb][:], ps_s[sb][:], AF.Exp, scale=0.125), inc="s_scalar")
                P.wait("tensor", exp_toks[sb])
                P.op("tensor", lambda p, par=par, kt=kt, sb=sb: p.matmul(
                    ps_c[0:65, :], vtok[par][kt][:, 0:65], pt[sb][:],
                    start=(kt == 0), stop=(kt == 15)),
                    inc="s_tensor" if kt == 15 else None)
            ctx_done = P.cur("s_tensor")
            P.wait("scalar", ctx_done)
            P.op("scalar", lambda p: p.activation(nrow[:], ps_c[64:65, :], AF.Ln))
            rtok = P.op("scalar", lambda p: p.activation(
                nrow2[:], nrow[0:1, :], AF.Exp, scale=-1.0), inc="s_scalar")
            P.wait("tensor", rtok)
            bc = P.op("tensor", lambda p: p.matmul(
                ps_b[:], C.ones_f[0:1, :], nrow2[0:1, :], start=True, stop=True),
                inc="s_tensor")
            P.wait("vector", bc)
            P.op("vector", lambda p: p.tensor_copy(rb[:], ps_b[:]), inc="s_vector")
            P.op("vector", lambda p, h=h, sl=sl: p.tensor_tensor(
                C.ctx[h // 2][(h % 2) * 64:(h % 2) * 64 + 64, sl], ps_c[0:64, :],
                rb[0:64, :], ALU.mult), inc="s_vector")
            P.wait("tensor", P.cur("s_vector"))

        k_free[par] = P.cur("s_tensor")
        vt_free[par] = P.cur("s_tensor")
    les.close()


def emit_proj_residual(C, es, w_name, b_name):
    nc, P = C.nc, C.P
    les = ExitStack()
    wfull = C.sbuf(les, [128, CT * DM], BF16, "pjw")
    bb = C.sbuf(les, [128, CT], F32, "pjb")
    pss = [C.psum(les, [128, CH], F32, "pjps") for _ in range(2)]
    w = C.ins[w_name].rearrange("(a p) o -> p a o", p=128)
    wdst = wfull[:].rearrange("p (a o) -> p a o", o=DM)
    wtoks = [P.dma("gpsimd", wdst[:, a:a + 1, :], w[:, a:a + 1, :], f"d_pw{a % 4}")
             for a in range(CT)]
    btok = P.dma("sync", bb[:], C.ins[b_name], "d_pjb")
    P.waits("tensor", wtoks)
    P.wait("vector", btok)

    ev_tok = [None, None]
    for o in range(CT):
        for ch in range(NCH):
            pb = ch % 2
            P.wait("tensor", ev_tok[pb])
            for c in range(CT):
                P.op("tensor", lambda p, c=c, ch=ch, pb=pb, o=o: p.matmul(
                    pss[pb][:], wdst[:, c, o * 128:(o + 1) * 128],
                    C.ctx[c][:, ch * CH:(ch + 1) * CH],
                    start=(c == 0), stop=(c == CT - 1)),
                    inc="s_tensor" if c == CT - 1 else None)
            P.wait("vector", P.cur("s_tensor"))
            ev_tok[pb] = P.op("vector", lambda p, o=o, ch=ch, pb=pb: p.scalar_tensor_tensor(
                C.resid[o][:, ch * CH:(ch + 1) * CH], pss[pb][:], bb[:, o:o + 1],
                C.resid[o][:, ch * CH:(ch + 1) * CH], ALU.add, ALU.add), inc="s_vector")
    P.wait("tensor", P.cur("s_vector"))
    les.close()


def emit_routing_l0(C, es, route_es):
    nc, P = C.nc, C.P
    les = ExitStack()
    C.gatesT = C.sbuf(route_es, [8, T], BF16, "gatesT")
    C.esel = C.sbuf(route_es, [32, 32 * 128], BF16, "esel")
    C.id_f = C.sbuf(route_es, [128, 128], F32, "idf")
    P.dma("gpsimd", C.esel[:], C.ins["c_esel"], "d_esel_g")
    et = P.dma("sync", C.id_f[:], C.ins["c_ident"], "d_esel")
    P.wait("tensor", et)
    gw = C.sbuf(les, [128, CT * 8], BF16, "gw")
    lgT = C.sbuf(les, [8, T], F32, "lgT")
    lg_tok = C.sbuf(les, [128, 8], F32, "lgtok")
    w1 = C.sbuf(les, [128, 1], F32, "ro1")
    w2 = C.sbuf(les, [128, 1], F32, "ro2")
    msk = C.sbuf(les, [128, 8], F32, "romask")
    e1 = C.sbuf(les, [128, 8], F32, "roe1")
    ps_l = C.psum(les, [128, CH], F32, "rops")
    ps_t = C.psum(les, [128, 512], F32, "rops2")

    wv = C.ins["l0_gwt"].rearrange("(a p) e -> p a e", p=128)
    gwv = gw[:].rearrange("p (a e) -> p a e", e=8)
    wtok = P.dma("gpsimd", gwv[:], wv[:], "d_rw")
    P.wait("tensor", wtok)
    prev_ev = None
    for ch in range(NCH):
        sl = slice(ch * CH, (ch + 1) * CH)
        P.wait("tensor", prev_ev)
        for c in range(CT):
            P.op("tensor", lambda p, c=c, sl=sl: p.matmul(
                ps_l[0:8, :], gwv[:, c, :], C.xn[c][:, sl],
                start=(c == 0), stop=(c == CT - 1)),
                inc="s_tensor" if c == CT - 1 else None)
        P.wait("vector", P.cur("s_tensor"))
        prev_ev = P.op("vector", lambda p, sl=sl: p.tensor_copy(lgT[:, sl], ps_l[0:8, :]),
                       inc="s_vector")
    prev_wr = None
    prev_tr = None
    for tt in range(T // 128):
        tsl = slice(tt * 128, (tt + 1) * 128)
        P.wait("tensor", prev_wr)
        tr = P.op("tensor", lambda p, tsl=tsl: p.transpose(
            ps_t[:, 0:8], lgT[0:8, tsl], C.id_f[0:8, 0:8]), inc="s_tensor")
        P.wait("vector", tr)
        P.op("vector", lambda p: p.tensor_copy(lg_tok[:], ps_t[:, 0:8]), inc="s_vector")
        P.op("vector", lambda p: p.tensor_reduce(w1[:], lg_tok[:], AX.X, ALU.max))
        P.op("vector", lambda p: p.tensor_scalar(msk[:], lg_tok[:], w1[:, 0:1], None,
                                                 ALU.is_equal))
        P.op("vector", lambda p: p.scalar_tensor_tensor(
            msk[:], msk[:], -1e30, lg_tok[:], ALU.mult, ALU.add))
        P.op("vector", lambda p: p.tensor_reduce(w2[:], msk[:], AX.X, ALU.max))
        P.op("vector", lambda p: p.tensor_scalar(w1[:], w1[:], -1.0, None, ALU.mult),
             inc="s_vector")
        vt = P.cur("s_vector")
        P.wait("scalar", vt)
        ex = P.op("scalar", lambda p: p.activation(e1[:], lg_tok[:], AF.Exp, bias=w1[:, 0:1]),
                  inc="s_scalar")
        P.wait("vector", ex)
        P.op("vector", lambda p: p.tensor_scalar(msk[:], lg_tok[:], w2[:, 0:1], None,
                                                 ALU.is_ge))
        P.op("vector", lambda p: p.tensor_tensor(e1[:], e1[:], msk[:], ALU.mult))
        P.op("vector", lambda p: p.tensor_reduce(w2[:], e1[:], AX.X, ALU.add))
        P.op("vector", lambda p: p.reciprocal(w1[:], w2[:]))
        P.op("vector", lambda p: p.tensor_scalar(e1[:], e1[:], w1[:, 0:1], None, ALU.mult),
             inc="s_vector")
        gt = P.cur("s_vector")
        P.wait("tensor", gt)
        prev_tr = P.op("tensor", lambda p: p.transpose(
            ps_t[0:8, 8:136], e1[:], C.id_f[:]), inc="s_tensor")
        P.wait("vector", prev_tr)
        prev_wr = P.op("vector", lambda p, tsl=tsl: p.tensor_copy(
            C.gatesT[0:8, tsl], ps_t[0:8, 8:136]), inc="s_vector")
        P.wait("tensor", prev_wr)
    les.close()


def emit_moe_l0(C, es):
    nc, P = C.nc, C.P
    les = ExitStack()
    NE = 9
    NHT = 32
    HALF = T // 2
    w1b = [C.sbuf(les, [128, CT * 256], BF16, "w1b") for _ in range(2)]
    w2b = [C.sbuf(les, [128, 2 * 512], BF16, "w2b") for _ in range(2)]
    aE = [C.sbuf(les, [128, HALF], BF16, "aE") for _ in range(NHT)]
    gBs = [C.sbuf(les, [128, CH], BF16, "gBs") for _ in range(2)]
    sst = [C.sbuf(les, [128, CH], BF16, "sst") for _ in range(2)]
    pool = [C.psum(les, [128, CH], F32, "fps") for _ in range(8)]

    w1v = C.ins["l0_w1t"]
    w2v = C.ins["l0_w2t"]

    w1_tok = [None, None]
    w1_use = [None, None]
    w2_tok = [None, None]
    w2_use = [None, None]
    gB_done = [None, None]

    def load_w1(e, ht2, buf):
        P.wait("gpsimd", w1_use[buf])
        src = C.ins["l0_w1t"][e].rearrange("(a p) o -> p a o", p=128)[
            :, :, ht2 * 256:(ht2 + 1) * 256]
        dst = w1b[buf][:].rearrange("p (a o) -> p a o", o=256)
        w1_tok[buf] = P.dma("gpsimd", dst[:], src, f"d_w1{buf}")

    def load_w2(e, oh, ht2, buf):
        P.wait("gpsimd", w2_use[buf])
        src = C.ins["l0_w2t"][e][ht2 * 256:(ht2 + 1) * 256,
                                 oh * 512:(oh + 1) * 512].rearrange("(b p) o -> p b o", p=128)
        dst = w2b[buf][:].rearrange("p (b o) -> p b o", o=512)
        w2_tok[buf] = P.dma("gpsimd", dst[:], src, f"d_w2{buf}")

    a_ready = None
    ev_prev = {}
    for e in range(NE):
        for hf in range(2):
            chs = (2 * hf, 2 * hf + 1)
            if e > 0:
                for i, ch in enumerate(chs):
                    P.wait("tensor", gB_done[i])
                    t = P.op("tensor", lambda p, e=e, ch=ch, i=i: p.matmul(
                        pool[2 + i][:], C.esel[0:8, (e - 1) * 128:e * 128],
                        C.gatesT[0:8, ch * CH:(ch + 1) * CH], start=True, stop=True),
                        inc="s_tensor")
                    P.wait("vector", t)
                    gB_done[i] = P.op("vector", lambda p, i=i: p.tensor_copy(
                        gBs[i][:], pool[2 + i][:]), inc="s_vector")
                P.wait("tensor", gB_done[0])
                P.wait("tensor", gB_done[1])

            load_w1(e, 0, 0)
            for ht2 in range(16):
                buf = ht2 % 2
                if ht2 + 1 < 16:
                    load_w1(e, ht2 + 1, (ht2 + 1) % 2)
                P.wait("tensor", w1_tok[buf])
                wv = w1b[buf][:].rearrange("p (a o) -> p a o", o=256)
                for k in range(2):
                    ht = ht2 * 2 + k
                    for sub in range(2):
                        ch = chs[sub]
                        pb = (ht * 2 + sub) % 2
                        P.wait("tensor", ev_prev.get(("A", pb)))
                        for c in range(CT):
                            P.op("tensor", lambda p, c=c, k=k, ch=ch, pb=pb, wv=wv: p.matmul(
                                pool[pb][:], wv[:, c, k * 128:(k + 1) * 128],
                                C.xn[c][:, ch * CH:(ch + 1) * CH],
                                start=(c == 0), stop=(c == CT - 1)),
                                inc="s_tensor" if c == CT - 1 else None)
                        mm = P.cur("s_tensor")
                        P.wait("scalar", mm)
                        if e == 0:
                            ev_prev[("A", pb)] = P.op(
                                "scalar", lambda p, ht=ht, sub=sub, pb=pb: p.activation(
                                    aE[ht][:, sub * CH:(sub + 1) * CH], pool[pb][:], AF.Silu),
                                inc="s_scalar")
                        else:
                            st = P.op("scalar", lambda p, sub=sub, pb=pb: p.activation(
                                sst[pb][:], pool[pb][:], AF.Silu), inc="s_scalar")
                            P.wait("vector", st)
                            ev_prev[("A", pb)] = P.op(
                                "vector", lambda p, ht=ht, sub=sub, pb=pb: p.tensor_tensor(
                                    aE[ht][:, sub * CH:(sub + 1) * CH], sst[pb][:],
                                    gBs[sub][:], ALU.mult), inc="s_vector")
                w1_use[buf] = P.cur("s_tensor")
            a_done_tok = {"scalar": P.cur("s_scalar"), "vector": P.cur("s_vector")}

            for oh in range(2):
                load_w2(e, oh, 0, 0)
                for ht2 in range(16):
                    buf = ht2 % 2
                    if ht2 + 1 < 16:
                        load_w2(e, oh, ht2 + 1, (ht2 + 1) % 2)
                    P.wait("tensor", w2_tok[buf])
                    if ht2 == 0:
                        P.wait("tensor", a_done_tok["scalar"])
                        P.wait("tensor", a_done_tok["vector"])
                    w2vv = w2b[buf][:].rearrange("p (b o) -> p b o", o=512)
                    for k in range(2):
                        ht = ht2 * 2 + k
                        for sub in range(2):
                            for oi in range(4):
                                pb = sub * 4 + oi
                                if ht == 0:
                                    P.wait("tensor", ev_prev.get(("B", oh, pb)))
                                P.op("tensor", lambda p, k=k, oi=oi, sub=sub, pb=pb, ht=ht,
                                     w2vv=w2vv: p.matmul(
                                         pool[pb][:], w2vv[:, k, oi * 128:(oi + 1) * 128],
                                         aE[ht][:, sub * CH:(sub + 1) * CH],
                                         start=(ht == 0), stop=(ht == NHT - 1)),
                                     inc="s_tensor" if (k == 1 and sub == 1 and oi == 3) else None)
                    w2_use[buf] = P.cur("s_tensor")
                bdone = P.cur("s_tensor")
                P.wait("vector", bdone)
                for sub in range(2):
                    ch = chs[sub]
                    for oi in range(4):
                        pb = sub * 4 + oi
                        ot = oh * 4 + oi
                        ev_prev[("B", oh, pb)] = P.op(
                            "vector", lambda p, ot=ot, ch=ch, pb=pb: p.tensor_tensor(
                                C.resid[ot][:, ch * CH:(ch + 1) * CH], pool[pb][:],
                                C.resid[ot][:, ch * CH:(ch + 1) * CH], ALU.add),
                            inc="s_vector")
                P.wait("tensor", P.cur("s_vector"))
    les.close()


def emit_sw_spill(C, es):
    nc, P = C.nc, C.P
    les = ExitStack()
    swf = [C.sbuf(les, [128, 3072], F32, "swf") for _ in range(2)]
    scf = C.sbuf(les, [128, 3072], F32, "scf")
    ob = [C.sbuf(les, [128, 3072], BF16, "swob") for _ in range(2)]
    sc_users = None
    st_tok = [None, None]
    sw_tok = [None, None]
    for c in range(CT):
        P.wait("sync", sc_users)
        sc_tok = P.dma("sync", scf[:], C.ins["l1_sct"][c * 128:(c + 1) * 128, :], "d_sc")
        for j in range(8):
            buf = j % 2
            P.wait("sync", sw_tok[buf])
            sw_load = P.dma("sync", swf[buf][:], C.ins["l1_swt"][j, c * 128:(c + 1) * 128, :],
                            f"d_sw{buf}")
            P.wait("vector", sw_load)
            P.wait("vector", sc_tok)
            P.wait("vector", st_tok[buf])
            tt = P.op("vector", lambda p, buf=buf: p.tensor_tensor(
                ob[buf][:], swf[buf][:], scf[:], ALU.mult), inc="s_vector")
            sw_tok[buf] = tt
            P.wait("sync", tt)
            st_tok[buf] = P.dma("sync", C.swb_s[j, c * 128:(c + 1) * 128, :], ob[buf][:],
                                f"d_sb{buf}")
        sc_users = P.cur("s_vector")
    P.waits("sync", [st_tok[0], st_tok[1]])
    les.close()


def emit_spline_qkv(C, es):
    nc, P = C.nc, C.P
    les = ExitStack()
    NJ0 = 11
    bsF = [C.sbuf(les, [128, 8 * CH], BF16, "bsF") for _ in range(CT - 1)]
    bw = C.sbuf(les, [128, NJ0 * CH], BF16, "bw")

    def bs_src(c, j):
        if c < CT - 1:
            return bsF[c][:, j * CH:(j + 1) * CH]
        return bw[:, j * CH:(j + 1) * CH]
    sx = [C.sbuf(les, [128, CH], BF16, "sx") for _ in range(CT)]
    t1 = C.sbuf(les, [128, CH], BF16, "spt1")
    t2 = C.sbuf(les, [128, CH], BF16, "spt2")
    xnf = C.sbuf(les, [128, CH], F32, "spxnf")
    tmp = C.sbuf(les, [128, CH], F32, "sptmp")
    gcolall = C.sbuf(les, [128, CT * 12], F32, "gcol")
    gcol = [gcolall[:, c * 12:(c + 1) * 12] for c in range(CT)]
    rtab = [C.sbuf(les, [128, 64], F32, "rtab") for _ in range(CT)]
    sblk = [C.sbuf(les, [128, 64 * 128], BF16, "sblk") for _ in range(2)]
    bwb = [C.sbuf(les, [128, CT * 128], BF16, "bwb") for _ in range(2)]
    stage = [C.sbuf(les, [128, CH], BF16, "spst") for _ in range(2)]
    ps_m = C.psum(les, [128, CH], F32, "sppsM")
    ps_r = C.psum(les, [128, CH], F32, "sppsR")
    pss = [C.psum(les, [128, CH], F32, "spps") for _ in range(2)]

    OFF_R1 = {1: 0, 2: 20, 3: 38}
    OFF_N2 = {1: 10, 2: 29, 3: 46}

    g_tok = P.dma("sync", gcolall[:].rearrange("p (c g) -> p c g", g=12),
                  C.ins["l1_grid"].rearrange("(c p) g -> p c g", p=128), "d_grid")
    P.wait("vector", g_tok)
    for c in range(CT):
        for p_ in (1, 2, 3):
            cnt = NJ0 - p_
            P.op("vector", lambda p, c=c, p_=p_, cnt=cnt: p.tensor_tensor(
                rtab[c][:, OFF_R1[p_]:OFF_R1[p_] + cnt], gcol[c][:, p_:p_ + cnt],
                gcol[c][:, 0:cnt], ALU.subtract))
            P.op("vector", lambda p, c=c, p_=p_, cnt=cnt: p.reciprocal(
                rtab[c][:, OFF_R1[p_]:OFF_R1[p_] + cnt],
                rtab[c][:, OFF_R1[p_]:OFF_R1[p_] + cnt]))
            P.op("vector", lambda p, c=c, p_=p_, cnt=cnt: p.tensor_tensor(
                rtab[c][:, OFF_N2[p_]:OFF_N2[p_] + cnt], gcol[c][:, 1:1 + cnt],
                gcol[c][:, p_ + 1:p_ + 1 + cnt], ALU.subtract))
            P.op("vector", lambda p, c=c, p_=p_, cnt=cnt: p.reciprocal(
                rtab[c][:, OFF_N2[p_]:OFF_N2[p_] + cnt],
                rtab[c][:, OFF_N2[p_]:OFF_N2[p_] + cnt]))

    sp_tok = [None, None]
    bw_tok = [None, None]
    blk_use = [None, None]
    st_tok = [None, None]
    mm_all = None
    for ch in range(NCH):
        sl = slice(ch * CH, (ch + 1) * CH)
        P.wait("tensor", P.cur("s_vector"))
        P.op("tensor", lambda p, sl=sl: p.matmul(
            ps_m[:], C.ones_f[0:1, :], C.mrow[0:1, sl], start=True, stop=True))
        bt = P.op("tensor", lambda p, sl=sl: p.matmul(
            ps_r[:], C.ones_f[0:1, :], C.rrow[0:1, sl], start=True, stop=True),
            inc="s_tensor")
        P.wait("vector", bt)
        P.wait("vector", mm_all)
        P.wait("scalar", mm_all)
        sil_tok = None
        for c in range(CT):
            P.op("vector", lambda p, c=c, sl=sl: p.tensor_tensor(
                tmp[:], C.resid[c][:, sl], ps_m[:], ALU.subtract))
            P.wait("vector", sil_tok)
            xt = P.op("vector", lambda p: p.tensor_tensor(xnf[:], tmp[:], ps_r[:], ALU.mult))
            P.wait("scalar", xt)
            sil_tok = P.op("scalar", lambda p, c=c: p.activation(sx[c][:], xnf[:], AF.Silu))
            for j in range(NJ0):
                P.op("vector", lambda p, c=c, j=j: p.tensor_scalar(
                    t1[:], xnf[:], gcol[c][:, j:j + 1], None, ALU.is_ge))
                P.op("vector", lambda p, c=c, j=j: p.tensor_scalar(
                    t2[:], xnf[:], gcol[c][:, j + 1:j + 2], None, ALU.is_lt))
                P.op("vector", lambda p, j=j: p.tensor_tensor(
                    bw[:, j * CH:(j + 1) * CH], t1[:], t2[:], ALU.mult))
            for p_ in (1, 2, 3):
                for j in range(NJ0 - p_):
                    P.op("vector", lambda p, c=c, p_=p_, j=j: p.tensor_scalar(
                        t1[:], xnf[:], gcol[c][:, j:j + 1],
                        rtab[c][:, OFF_R1[p_] + j:OFF_R1[p_] + j + 1],
                        ALU.subtract, ALU.mult))
                    P.op("vector", lambda p, c=c, p_=p_, j=j: p.tensor_scalar(
                        t2[:], xnf[:], gcol[c][:, j + p_ + 1:j + p_ + 2],
                        rtab[c][:, OFF_N2[p_] + j:OFF_N2[p_] + j + 1],
                        ALU.subtract, ALU.mult))
                    P.op("vector", lambda p, j=j: p.tensor_tensor(
                        t1[:], t1[:], bw[:, j * CH:(j + 1) * CH], ALU.mult))
                    P.op("vector", lambda p, j=j: p.tensor_tensor(
                        t2[:], t2[:], bw[:, (j + 1) * CH:(j + 2) * CH], ALU.mult))
                    P.op("vector", lambda p, j=j: p.tensor_tensor(
                        bw[:, j * CH:(j + 1) * CH], t1[:], t2[:], ALU.add))
            if c < CT - 1:
                for j in range(8):
                    P.op("vector", lambda p, c=c, j=j: p.tensor_copy(
                        bsF[c][:, j * CH:(j + 1) * CH], bw[:, j * CH:(j + 1) * CH]))
        bs_done = P.cur("s_vector")
        sx_done = P.cur("s_scalar")

        ev_tok = [None, None]
        for o in range(24):
            buf = o % 2
            P.wait("sync", blk_use[buf])
            svw = sblk[buf][:].rearrange("p (j c o) -> p j c o", c=CT, o=128)
            for j in range(8):
                sp_tok[buf] = P.dma(
                    "sync", svw[:, j, :, :],
                    C.swb_s[j, :, o * 128:(o + 1) * 128].rearrange(
                        "(c p) o -> p c o", p=128), f"d_sp{buf}")
            P.wait("gpsimd", blk_use[buf])
            bvw = bwb[buf][:].rearrange("p (c o) -> p c o", o=128)
            bw_tok[buf] = P.dma(
                "gpsimd", bvw[:],
                C.ins["l1_bwt"][:, o * 128:(o + 1) * 128].rearrange("(c p) o -> p c o", p=128),
                f"d_bw{buf}")
            pb = o % 2
            P.wait("tensor", ev_tok[pb])
            P.wait("tensor", bw_tok[buf])
            P.wait("tensor", sx_done)
            for c in range(CT):
                P.op("tensor", lambda p, c=c, buf=buf, pb=pb: p.matmul(
                    pss[pb][:], bwb[buf][:].rearrange("p (c o) -> p c o", o=128)[:, c, :],
                    sx[c][:], start=(c == 0), stop=False))
            P.wait("tensor", sp_tok[buf])
            P.wait("tensor", bs_done)
            for j in range(8):
                for c in range(CT):
                    P.op("tensor", lambda p, j=j, c=c, buf=buf, pb=pb: p.matmul(
                        pss[pb][:], sblk[buf][:].rearrange(
                            "p (j c o) -> p j c o", c=CT, o=128)[:, j, c, :],
                        bs_src(c, j),
                        start=False, stop=(j == 7 and c == CT - 1)),
                        inc="s_tensor" if (j == 7 and c == CT - 1) else None)
            mmt = P.cur("s_tensor")
            blk_use[buf] = mmt
            P.wait("scalar", mmt)
            P.wait("scalar", st_tok[buf])
            ev = P.op("scalar", lambda p, buf=buf, pb=pb: p.activation(
                stage[buf][:], pss[pb][:], AF.Copy), inc="s_scalar")
            ev_tok[pb] = ev
            P.wait("sync", ev)
            st_tok[buf] = P.dma("sync", C.qkv_s[o * 128:(o + 1) * 128, sl], stage[buf][:],
                                f"d_sq{buf}")
        mm_all = P.cur("s_tensor")
    P.waits("sync", [st_tok[0], st_tok[1]])
    les.close()


def emit_routing_l1(C, es, route_es):
    nc, P = C.nc, C.P
    les = ExitStack()
    C.routeT = C.sbuf(route_es, [32, T], BF16, "routeT")
    C.esel = C.sbuf(route_es, [32, 32 * 128], BF16, "esel")
    C.id_f = C.sbuf(route_es, [128, 128], F32, "idf")
    P.dma("gpsimd", C.esel[:], C.ins["c_esel"], "d_esel_g")
    et = P.dma("sync", C.id_f[:], C.ins["c_ident"], "d_esel")
    P.wait("tensor", et)

    rw = C.sbuf(les, [128, CT * 32], BF16, "rw")
    lgT = C.sbuf(les, [32, T], F32, "lgT")
    br = C.sbuf(les, [1, 32], F32, "rbr")
    bB = C.sbuf(les, [128, 32], F32, "rbB")
    ltok = C.sbuf(les, [128, 32], F32, "rltok")
    sc = C.sbuf(les, [128, 32], F32, "rsc")
    sfc = C.sbuf(les, [128, 32], F32, "rsfc")
    tmp = C.sbuf(les, [128, 32], F32, "rtmp")
    tc = C.sbuf(les, [128, 32], F32, "rtc")
    msk = C.sbuf(les, [128, 32], F32, "rmsk")
    m8 = C.sbuf(les, [128, 8], F32, "rm8")
    gsc = C.sbuf(les, [128, 4], F32, "rgsc")
    g2 = C.sbuf(les, [128, 4], F32, "rg2")
    c1 = C.sbuf(les, [128, 1], F32, "rc1")
    c2 = C.sbuf(les, [128, 1], F32, "rc2")
    ps_l = C.psum(les, [128, CH], F32, "rops")
    ps_t = C.psum(les, [128, 512], F32, "rops2")

    wv = C.ins["l1_rwt"].rearrange("(a p) e -> p a e", p=128)
    rwv = rw[:].rearrange("p (a e) -> p a e", e=32)
    wtok = P.dma("gpsimd", rwv[:], wv[:], "d_rw")
    btok = P.dma("sync", br[:], C.ins["l1_rbias"], "d_rb")
    P.wait("tensor", wtok)
    P.wait("tensor", btok)
    bc = P.op("tensor", lambda p: p.matmul(
        ps_t[:, 0:32], C.ones_f[0:1, :], br[0:1, :], start=True, stop=True), inc="s_tensor")
    P.wait("vector", bc)
    bB_tok = P.op("vector", lambda p: p.tensor_copy(bB[:], ps_t[:, 0:32]))

    prev_ev = None
    for ch in range(NCH):
        sl = slice(ch * CH, (ch + 1) * CH)
        P.wait("tensor", prev_ev)
        for c in range(CT):
            P.op("tensor", lambda p, c=c, sl=sl: p.matmul(
                ps_l[0:32, :], rwv[:, c, :], C.xn[c][:, sl],
                start=(c == 0), stop=(c == CT - 1)),
                inc="s_tensor" if c == CT - 1 else None)
        P.wait("vector", P.cur("s_tensor"))
        prev_ev = P.op("vector", lambda p, sl=sl: p.tensor_copy(lgT[:, sl], ps_l[0:32, :]))

    prev_wr = None
    for tt in range(T // 128):
        tsl = slice(tt * 128, (tt + 1) * 128)
        P.wait("tensor", prev_wr)
        tr = P.op("tensor", lambda p, tsl=tsl: p.transpose(
            ps_t[0:128, 64:96], lgT[0:32, tsl], C.id_f[0:32, 0:32]), inc="s_tensor")
        P.wait("vector", tr)
        ct = P.op("vector", lambda p: p.tensor_copy(ltok[:], ps_t[0:128, 64:96]))
        P.wait("scalar", ct)
        sg = P.op("scalar", lambda p: p.activation(sc[:], ltok[:], AF.Sigmoid))
        P.wait("vector", sg)
        P.wait("vector", bB_tok)
        P.op("vector", lambda p: p.tensor_tensor(sfc[:], sc[:], bB[:], ALU.add))
        for g in range(4):
            s8 = slice(8 * g, 8 * g + 8)
            P.op("vector", lambda p, s8=s8: p.tensor_reduce(c1[:], sfc[:, s8], AX.X, ALU.max))
            P.op("vector", lambda p, s8=s8: p.tensor_scalar(
                m8[:], sfc[:, s8], c1[:, 0:1], None, ALU.is_equal))
            P.op("vector", lambda p, s8=s8: p.scalar_tensor_tensor(
                m8[:], m8[:], -1e30, sfc[:, s8], ALU.mult, ALU.add))
            P.op("vector", lambda p: p.tensor_reduce(c2[:], m8[:], AX.X, ALU.max))
            P.op("vector", lambda p, g=g: p.tensor_tensor(
                gsc[:, g:g + 1], c1[:], c2[:], ALU.add))
        P.op("vector", lambda p: p.tensor_reduce(c1[:], gsc[:], AX.X, ALU.max))
        P.op("vector", lambda p: p.tensor_scalar(g2[:], gsc[:], c1[:, 0:1], None,
                                                 ALU.is_equal))
        P.op("vector", lambda p: p.scalar_tensor_tensor(g2[:], g2[:], -1e30, gsc[:],
                                                        ALU.mult, ALU.add))
        P.op("vector", lambda p: p.tensor_reduce(c2[:], g2[:], AX.X, ALU.max))
        for g in range(4):
            s8 = slice(8 * g, 8 * g + 8)
            P.op("vector", lambda p, g=g: p.tensor_scalar(
                c1[:], gsc[:, g:g + 1], c2[:, 0:1], None, ALU.is_ge))
            P.op("vector", lambda p, s8=s8: p.tensor_scalar(
                tmp[:, s8], sfc[:, s8], c1[:, 0:1], None, ALU.mult))
        P.op("vector", lambda p: p.tensor_copy(tc[:], tmp[:]))
        for _ in range(3):
            P.op("vector", lambda p: p.tensor_reduce(c1[:], tc[:], AX.X, ALU.max))
            P.op("vector", lambda p: p.tensor_scalar(msk[:], tc[:], c1[:, 0:1], None,
                                                     ALU.is_equal))
            P.op("vector", lambda p: p.scalar_tensor_tensor(
                tc[:], msk[:], -1e30, tc[:], ALU.mult, ALU.add))
        P.op("vector", lambda p: p.tensor_reduce(c1[:], tc[:], AX.X, ALU.max))
        P.op("vector", lambda p: p.tensor_scalar(msk[:], tmp[:], c1[:, 0:1], None,
                                                 ALU.is_ge))
        P.op("vector", lambda p: p.tensor_tensor(tmp[:], sc[:], msk[:], ALU.mult))
        P.op("vector", lambda p: p.tensor_reduce(c2[:], tmp[:], AX.X, ALU.add))
        P.op("vector", lambda p: p.tensor_scalar(c2[:], c2[:], 1e-20, None,
                                                 ALU.add))
        P.op("vector", lambda p: p.reciprocal(c1[:], c2[:]))
        P.op("vector", lambda p: p.tensor_scalar(c1[:], c1[:], 2.5, None, ALU.mult))
        gt = P.op("vector", lambda p: p.tensor_scalar(tmp[:], tmp[:], c1[:, 0:1], None,
                                                      ALU.mult))
        P.wait("tensor", gt)
        tr2 = P.op("tensor", lambda p: p.transpose(
            ps_t[0:32, 96:224], tmp[:], C.id_f[:]), inc="s_tensor")
        P.wait("vector", tr2)
        prev_wr = P.op("vector", lambda p, tsl=tsl: p.tensor_copy(
            C.routeT[0:32, tsl], ps_t[0:32, 96:224]))
        P.wait("tensor", prev_wr)
    les.close()


def emit_moe_l1(C, es):
    nc, P = C.nc, C.P
    pool_es = ExitStack()
    pool = [C.psum(pool_es, [128, CH], F32, "l1ps") for _ in range(8)]

    sh_es = ExitStack()
    sgw = C.sbuf(sh_es, [128, CT * DM], BF16, "sgw")
    suw = C.sbuf(sh_es, [128, CT * DM], BF16, "suw")
    sdw = C.sbuf(sh_es, [128, CT * DM], BF16, "sdw")
    aSH = [C.sbuf(sh_es, [128, T], BF16, "aSH") for _ in range(CT)]
    sst = [C.sbuf(sh_es, [128, CH], BF16, "shst") for _ in range(2)]
    toks = []
    for nm, dst in (("l1_sgwt", sgw), ("l1_suwt", suw), ("l1_sdwt", sdw)):
        toks.append(P.dma("gpsimd", dst[:].rearrange("p (a o) -> p a o", o=DM),
                          C.ins[nm].rearrange("(a p) o -> p a o", p=128), "d_shw"))
    P.waits("tensor", toks)
    sgv = sgw[:].rearrange("p (a o) -> p a o", o=DM)
    suv = suw[:].rearrange("p (a o) -> p a o", o=DM)
    sdv = sdw[:].rearrange("p (a o) -> p a o", o=DM)

    ev_prev = [None, None]
    for ht in range(CT):
        for sub in range(NCH):
            ssl = slice(sub * CH, (sub + 1) * CH)
            pb = (ht * NCH + sub) % 2
            P.wait("tensor", ev_prev[pb])
            for c in range(CT):
                P.op("tensor", lambda p, c=c, ht=ht, ssl=ssl, pb=pb: p.matmul(
                    pool[pb][:], sgv[:, c, ht * 128:(ht + 1) * 128], C.xn[c][:, ssl],
                    start=(c == 0), stop=(c == CT - 1)))
            for c in range(CT):
                P.op("tensor", lambda p, c=c, ht=ht, ssl=ssl, pb=pb: p.matmul(
                    pool[4 + pb][:], suv[:, c, ht * 128:(ht + 1) * 128], C.xn[c][:, ssl],
                    start=(c == 0), stop=(c == CT - 1)),
                    inc="s_tensor" if c == CT - 1 else None)
            mm = P.cur("s_tensor")
            P.wait("scalar", mm)
            st = P.op("scalar", lambda p, pb=pb: p.activation(sst[pb][:], pool[pb][:],
                                                              AF.Silu))
            P.wait("vector", st)
            ev_prev[pb] = P.op("vector", lambda p, ht=ht, ssl=ssl, pb=pb: p.tensor_tensor(
                aSH[ht][:, ssl], sst[pb][:], pool[4 + pb][:], ALU.mult))
    a_done = P.cur("s_vector")
    for sub in range(NCH):
        ssl = slice(sub * CH, (sub + 1) * CH)
        for ht in range(CT):
            if ht == 0:
                P.wait("tensor", a_done)
                P.wait("tensor", P.cur("s_scalar"))
                P.wait("tensor", P.cur("s_vector"))
            for oi in range(CT):
                P.op("tensor", lambda p, ht=ht, oi=oi, ssl=ssl: p.matmul(
                    pool[oi][:], sdv[:, ht, oi * 128:(oi + 1) * 128], aSH[ht][:, ssl],
                    start=(ht == 0), stop=(ht == CT - 1)),
                    inc="s_tensor" if (ht == CT - 1 and oi == CT - 1) else None)
        bdone = P.cur("s_tensor")
        P.wait("vector", bdone)
        for oi in range(CT):
            P.op("vector", lambda p, oi=oi, ssl=ssl: p.tensor_tensor(
                C.resid[oi][:, ssl], pool[oi][:], C.resid[oi][:, ssl], ALU.add))
    P.wait("tensor", P.cur("s_vector"))
    sh_es.close()

    ex_es = ExitStack()
    NE = 32
    wg = [C.sbuf(ex_es, [128, CT * 512], BF16, "wg") for _ in range(2)]
    wu = [C.sbuf(ex_es, [128, CT * 512], BF16, "wu") for _ in range(2)]
    wd = [C.sbuf(ex_es, [128, 4 * DM], BF16, "wd") for _ in range(2)]
    aE = [C.sbuf(ex_es, [128, T], BF16, "aE1") for _ in range(4)]
    wBs = [C.sbuf(ex_es, [128, CH], BF16, "wBs") for _ in range(NCH)]
    gst = [C.sbuf(ex_es, [128, CH], BF16, "gst") for _ in range(2)]
    ust = [C.sbuf(ex_es, [128, CH], BF16, "ust") for _ in range(2)]

    w_tok = [None, None]
    w_use = [None, None]

    def load_we(e, buf):
        P.wait("gpsimd", w_use[buf])
        P.dma("gpsimd", wg[buf][:].rearrange("p (a o) -> p a o", o=512),
              C.ins["l1_gwt"][e].rearrange("(a p) o -> p a o", p=128), f"d_we{buf}")
        P.dma("gpsimd", wu[buf][:].rearrange("p (a o) -> p a o", o=512),
              C.ins["l1_uwt"][e].rearrange("(a p) o -> p a o", p=128), f"d_we{buf}")
        w_tok[buf] = P.dma("gpsimd", wd[buf][:].rearrange("p (b o) -> p b o", o=DM),
                           C.ins["l1_dwt"][e].rearrange("(b p) o -> p b o", p=128),
                           f"d_we{buf}")

    P.wait("gpsimd", P.cur("s_tensor"))
    P.wait("gpsimd", P.cur("s_vector"))
    P.wait("gpsimd", P.cur("s_scalar"))
    load_we(0, 0)
    wB_done = [None] * NCH
    ev_prev = {}
    for e in range(NE):
        buf = e % 2
        if e + 1 < NE:
            load_we(e + 1, (e + 1) % 2)
        last_copy = wB_done[NCH - 1]
        for ch in range(NCH):
            P.wait("tensor", last_copy)
            t = P.op("tensor", lambda p, e=e, ch=ch: p.matmul(
                pool[2][:], C.esel[0:32, e * 128:(e + 1) * 128],
                C.routeT[0:32, ch * CH:(ch + 1) * CH], start=True, stop=True),
                inc="s_tensor")
            P.wait("vector", t)
            last_copy = wB_done[ch] = P.op("vector", lambda p, ch=ch: p.tensor_copy(
                wBs[ch][:], pool[2][:]))
        wv_tok = P.cur("s_vector")
        P.wait("tensor", w_tok[buf])
        wgv = wg[buf][:].rearrange("p (a o) -> p a o", o=512)
        wuv = wu[buf][:].rearrange("p (a o) -> p a o", o=512)
        wdv = wd[buf][:].rearrange("p (b o) -> p b o", o=DM)
        for ht in range(4):
            for sub in range(NCH):
                ssl = slice(sub * CH, (sub + 1) * CH)
                pb = (ht * NCH + sub) % 2
                P.wait("tensor", ev_prev.get(("A", pb)))
                for c in range(CT):
                    P.op("tensor", lambda p, c=c, ht=ht, ssl=ssl, pb=pb, wgv=wgv: p.matmul(
                        pool[pb][:], wgv[:, c, ht * 128:(ht + 1) * 128], C.xn[c][:, ssl],
                        start=(c == 0), stop=(c == CT - 1)))
                for c in range(CT):
                    P.op("tensor", lambda p, c=c, ht=ht, ssl=ssl, pb=pb, wuv=wuv: p.matmul(
                        pool[4 + pb][:], wuv[:, c, ht * 128:(ht + 1) * 128], C.xn[c][:, ssl],
                        start=(c == 0), stop=(c == CT - 1)),
                        inc="s_tensor" if c == CT - 1 else None)
                mm = P.cur("s_tensor")
                P.wait("scalar", mm)
                st = P.op("scalar", lambda p, pb=pb: p.activation(
                    gst[pb][:], pool[pb][:], AF.Silu))
                P.wait("vector", st)
                P.wait("vector", wv_tok)
                P.op("vector", lambda p, pb=pb: p.tensor_tensor(
                    ust[pb][:], gst[pb][:], pool[4 + pb][:], ALU.mult))
                ev_prev[("A", pb)] = P.op(
                    "vector", lambda p, ht=ht, ssl=ssl, pb=pb, sub=sub: p.tensor_tensor(
                        aE[ht][:, ssl], ust[pb][:], wBs[sub][:], ALU.mult))
        a_done_v = P.cur("s_vector")
        for sub in range(NCH):
            ssl = slice(sub * CH, (sub + 1) * CH)
            for ht in range(4):
                if ht == 0:
                    P.wait("tensor", a_done_v)
                    P.wait("tensor", P.cur("s_scalar"))
                    P.wait("tensor", P.cur("s_vector"))
                for oi in range(CT):
                    P.op("tensor", lambda p, ht=ht, oi=oi, ssl=ssl, wdv=wdv: p.matmul(
                        pool[oi][:], wdv[:, ht, oi * 128:(oi + 1) * 128], aE[ht][:, ssl],
                        start=(ht == 0), stop=(ht == 3)),
                        inc="s_tensor" if (ht == 3 and oi == CT - 1) else None)
            bdone = P.cur("s_tensor")
            P.wait("vector", bdone)
            for oi in range(CT):
                P.op("vector", lambda p, oi=oi, ssl=ssl: p.tensor_tensor(
                    C.resid[oi][:, ssl], pool[oi][:], C.resid[oi][:, ssl], ALU.add))
            ev_prev[("B", sub)] = P.cur("s_vector")
        w_use[buf] = P.cur("s_tensor")
        P.wait("tensor", P.cur("s_vector"))
    ex_es.close()
    pool_es.close()



_ESEL = None


def _esel_host():
    global _ESEL
    if _ESEL is None:
        e = np.zeros((32, 32 * 128), dtype=np.float32)
        for k in range(32):
            e[k, k * 128:(k + 1) * 128] = 1.0
        _ESEL = e
    return _ESEL


GROUP_INPUTS = {
    "l0a": [("l0_wattn_t", [DM, 3072]), ("l0_wproj_t", [DM, DM]), ("l0_bproj_t", [128, CT])],
    "l0f": [("c_esel", [32, 32 * 128]), ("l0_gwt", [DM, 8]),
            ("l0_w1t", [9, DM, 4096]), ("l0_w2t", [9, 4096, DM])],
    "l1a": [("l1_bwt", [DM, 3072]), ("l1_swt", [8, DM, 3072]), ("l1_sct", [DM, 3072]),
            ("l1_grid", [DM, 12]), ("l1_rot_t", [64, 64]),
            ("l1_owt", [DM, DM]), ("l1_obt", [128, CT])],
    "l1f": [("c_esel", [32, 32 * 128]), ("l1_rwt", [DM, 32]), ("l1_rbias", [1, 32]),
            ("l1_gwt", [32, DM, 512]), ("l1_uwt", [32, DM, 512]), ("l1_dwt", [32, 512, DM]),
            ("l1_sgwt", [DM, DM]), ("l1_suwt", [DM, DM]), ("l1_sdwt", [DM, DM])],
}


def prep_host(inputs, groups):
    h = {"c_ident": np.eye(128, dtype=np.float32)}
    cat = np.concatenate
    if "l0a" in groups:
        h["l0_wattn_t"] = np.ascontiguousarray(inputs["L0_attn_W_attn"].T)
        h["l0_wproj_t"] = np.ascontiguousarray(inputs["L0_attn_W_proj"].T)
        h["l0_bproj_t"] = np.ascontiguousarray(inputs["L0_attn_b_proj"].reshape(CT, 128).T)
    if "l0f" in groups:
        h["c_esel"] = _esel_host()
        h["l0_gwt"] = np.ascontiguousarray(inputs["L0_ffn_gate_w"].T)
        w1 = cat([inputs["L0_ffn_shared_w1"][None], inputs["L0_ffn_experts_w1"]], 0)
        w2 = cat([inputs["L0_ffn_shared_w2"][None], inputs["L0_ffn_experts_w2"]], 0)
        h["l0_w1t"] = np.ascontiguousarray(w1.transpose(0, 2, 1))
        h["l0_w2t"] = np.ascontiguousarray(w2.transpose(0, 2, 1))
    if "l1a" in groups:
        h["l1_bwt"] = np.ascontiguousarray(inputs["L1_attn_base_weight"].T)
        h["l1_swt"] = np.ascontiguousarray(inputs["L1_attn_spline_weight"].transpose(2, 1, 0))
        h["l1_sct"] = np.ascontiguousarray(inputs["L1_attn_spline_scaler"].T)
        h["l1_grid"] = np.ascontiguousarray(inputs["L1_attn_grid"])
        h["l1_rot_t"] = np.ascontiguousarray(inputs["L1_attn_rotation_matrix"].T)
        h["l1_owt"] = np.ascontiguousarray(inputs["L1_attn_out_w"].T)
        h["l1_obt"] = np.ascontiguousarray(inputs["L1_attn_out_b"].reshape(CT, 128).T)
    if "l1f" in groups:
        h["c_esel"] = _esel_host()
        h["l1_rwt"] = np.ascontiguousarray(inputs["L1_ffn_router_weight"].T)
        h["l1_rbias"] = np.ascontiguousarray(inputs["L1_ffn_router_bias"].reshape(1, 32))
        h["l1_gwt"] = np.ascontiguousarray(inputs["L1_ffn_gate_w"].transpose(0, 2, 1))
        h["l1_uwt"] = np.ascontiguousarray(inputs["L1_ffn_up_w"].transpose(0, 2, 1))
        h["l1_dwt"] = np.ascontiguousarray(inputs["L1_ffn_down_w"].transpose(0, 2, 1))
        h["l1_sgwt"] = np.ascontiguousarray(inputs["L1_ffn_shared_gate_w"].T)
        h["l1_suwt"] = np.ascontiguousarray(inputs["L1_ffn_shared_up_w"].T)
        h["l1_sdwt"] = np.ascontiguousarray(inputs["L1_ffn_shared_down_w"].T)
    return h


def build_group(groups):
    nc = bass.Bass("TRN2", target_bir_lowering=False, debug=False)
    C = Ctx(nc)
    C.ins = {}
    seen = set()
    decls = [("resid_in", [DM, T]), ("c_ident", [128, 128])]
    for g in groups:
        decls += GROUP_INPUTS[g]
    for nm, shp in decls:
        if nm in seen:
            continue
        seen.add(nm)
        C.ins[nm] = nc.dram_tensor(nm, shp, F32, kind="ExternalInput").ap()
    out = nc.dram_tensor("resid_out", [DM, T], F32, kind="ExternalOutput").ap()
    if "l0a" in groups or "l1a" in groups:
        C.qkv_s = nc.dram_tensor("qkv_s", [3072, T], BF16, kind="Internal").ap()
    if "l1a" in groups:
        C.swb_s = nc.dram_tensor("swb_s", [8, DM, 3072], BF16, kind="Internal").ap()

    es = C.es
    emit_common_setup(C, es)
    C.P.barrier()
    for g in groups:
        if g == "l0a":
            xn_es = ExitStack()
            st_es = ExitStack()
            emit_ln(C, es, make_xn=True, xn_es=xn_es, stats_es=st_es)
            C.P.barrier()
            st_es.close()
            emit_dense_to_scratch(C, es, "l0_wattn_t", 3072, C.xn, C.qkv_s)
            xn_es.close()
            C.P.barrier()
            attn_es = ExitStack()
            emit_attention(C, es, layer=0, ctx_es=attn_es)
            C.P.barrier()
            emit_proj_residual(C, es, "l0_wproj_t", "l0_bproj_t")
            attn_es.close()
            C.P.barrier()
        elif g == "l0f":
            xn_es = ExitStack()
            st_es = ExitStack()
            emit_ln(C, es, make_xn=True, xn_es=xn_es, stats_es=st_es)
            C.P.barrier()
            st_es.close()
            route_es = ExitStack()
            emit_routing_l0(C, es, route_es)
            C.P.barrier()
            emit_moe_l0(C, es)
            route_es.close()
            xn_es.close()
            C.P.barrier()
        elif g == "l1a":
            st_es = ExitStack()
            emit_ln(C, es, make_xn=False, stats_es=st_es)
            C.P.barrier()
            emit_sw_spill(C, es)
            C.P.barrier()
            emit_spline_qkv(C, es)
            C.P.barrier()
            st_es.close()
            attn_es = ExitStack()
            emit_attention(C, es, layer=1, ctx_es=attn_es)
            C.P.barrier()
            emit_proj_residual(C, es, "l1_owt", "l1_obt")
            attn_es.close()
            C.P.barrier()
        elif g == "l1f":
            xn_es = ExitStack()
            st_es = ExitStack()
            emit_ln(C, es, make_xn=True, xn_es=xn_es, stats_es=st_es)
            C.P.barrier()
            st_es.close()
            route_es = ExitStack()
            emit_routing_l1(C, es, route_es)
            C.P.barrier()
            emit_moe_l1(C, es)
            route_es.close()
            xn_es.close()
            C.P.barrier()
    toks = [C.P.dma("sync", out[c * 128:(c + 1) * 128, :], C.resid[c][:], "d_out")
            for c in range(CT)]
    C.P.wait("sync", C.P.cur("d_out"))
    C.P.build()
    es.close()
    return nc


_BUILD_CACHE = {}


def _get_prog(groups):
    key = tuple(groups)
    if key not in _BUILD_CACHE:
        _BUILD_CACHE[key] = build_group(list(groups))
    return _BUILD_CACHE[key]


LAUNCH_PLAN = [["l0a"], ["l0f"], ["l1a"], ["l1f"]]


def kernel(**inputs):
    x = inputs["L0_attn_x"]
    resid = [np.ascontiguousarray(x[b].T) for b in range(NCORES)]
    for groups in LAUNCH_PLAN:
        h = prep_host(inputs, groups)
        nc = _get_prog(tuple(groups))
        in_maps = []
        for b in range(NCORES):
            m = dict(h)
            m["resid_in"] = resid[b]
            in_maps.append(m)
        res = run_bass_kernel_spmd(nc, in_maps, core_ids=list(range(NCORES)))
        resid = [r["resid_out"] for r in res.results]
    return np.stack([r.T for r in resid]).astype(np.float32)


# version 04
# speedup vs baseline: 1.0055x; 1.0055x over previous
import numpy as np
from contextlib import ExitStack

import concourse.bass as bass
import concourse.mybir as mybir
from concourse.bass_utils import run_bass_kernel_spmd

F32 = mybir.dt.float32
BF16 = mybir.dt.bfloat16
AF = mybir.ActivationFunctionType
ALU = mybir.AluOpType
AX = mybir.AxisListType

NCORES = 8
DM = 1024
CT = DM // 128
T = 2048
CH = 512
NCH = T // CH
NH = 16
HD = 64
EPS = 1e-5

ENGS = ("tensor", "scalar", "vector", "sync", "gpsimd")


class Prog:
    def __init__(self, nc):
        self.nc = nc
        self.ops = {e: [] for e in ENGS}
        self.sems = {}
        self.lastwait = {}

    def sem(self, name):
        if name not in self.sems:
            h = self.nc.alloc_semaphore(name=name)
            self.sems[name] = [h, 0]
        return name

    def wait(self, eng, tok):
        if tok is None:
            return
        sname, val = tok
        if val <= 0:
            return
        if self.lastwait.get((eng, sname), -1) >= val:
            return
        self.lastwait[(eng, sname)] = val
        self.ops[eng].append(("w", self.sems[sname][0], val, sname))

    def waits(self, eng, toks):
        best = {}
        for t in toks:
            if t is None:
                continue
            if t[0] not in best or best[t[0]] < t[1]:
                best[t[0]] = t[1]
        for s, v in best.items():
            self.wait(eng, (s, v))

    def op(self, eng, fn, inc=None, amt=1):
        if eng in ("vector", "scalar"):
            sname = "s_" + eng
            self.wait(eng, (sname, self.sems[sname][1]))
            inc = sname
        tok = None
        h = None
        if inc is not None:
            self.sem(inc)
            s = self.sems[inc]
            s[1] += amt
            tok = (inc, s[1])
            h = s[0]
        self.ops[eng].append(("c", fn, h, amt))
        return tok

    def dma(self, eng, out, in_, sem):
        self.sem(sem)
        s = self.sems[sem]
        s[1] += 16
        tok = (sem, s[1])
        self.ops[eng].append(("d", out, in_, s[0]))
        return tok

    def cur(self, semname):
        if semname not in self.sems:
            return None
        return (semname, self.sems[semname][1])

    def barrier(self):
        for e in ENGS:
            self.op(e, lambda p: p.nop(), inc="bar_" + e)
        toks = [self.cur("bar_" + e) for e in ENGS]
        toks += [self.cur(s) for s in list(self.sems) if not s.startswith("bar_")]
        for e in ENGS:
            for t in toks:
                self.wait(e, t)

    def check(self):
        vals = {name: 0 for name in self.sems}
        idx = {e: 0 for e in ENGS}
        h2n = {id(v[0]): k for k, v in self.sems.items()}
        progress = True
        while progress:
            progress = False
            for e in ENGS:
                while idx[e] < len(self.ops[e]):
                    it = self.ops[e][idx[e]]
                    if it[0] == "w":
                        if vals[it[3]] < it[2]:
                            break
                    elif it[0] == "d":
                        vals[h2n[id(it[3])]] += 16
                    else:
                        if it[2] is not None:
                            vals[h2n[id(it[2])]] += it[3]
                    idx[e] += 1
                    progress = True
        stuck = {e: (idx[e], len(self.ops[e])) for e in ENGS if idx[e] < len(self.ops[e])}
        if stuck:
            det = []
            for e, (i, n) in stuck.items():
                it = self.ops[e][i]
                det.append(f"{e}@{i}/{n} waiting {it[3]}>={it[2]} (now {vals[it[3]]})")
            raise RuntimeError("deadlock: " + "; ".join(det))

    def build(self):
        self.check()
        with self.nc.Block() as blk:
            for eng in ENGS:
                if not self.ops[eng]:
                    continue
                deco = getattr(blk, eng)

                def body(proxy, _e=eng):
                    for it in self.ops[_e]:
                        if it[0] == "w":
                            proxy.wait_ge(it[1], it[2])
                        elif it[0] == "d":
                            proxy.dma_start(it[1], it[2]).then_inc(it[3], 16)
                        else:
                            ins = it[1](proxy)
                            if it[2] is not None:
                                ins.then_inc(it[2], it[3])

                deco(body)


class Ctx:
    def __init__(self, nc):
        self.nc = nc
        self.P = Prog(nc)
        for e in ENGS:
            self.P.sem("s_" + e)
        self.es = ExitStack()
        self.uid = 0

    def name(self, base):
        self.uid += 1
        return f"{base}_{self.uid}"

    def sbuf(self, es, shape, dtype, base="t"):
        return es.enter_context(self.nc.sbuf_tensor(self.name(base), shape, dtype))

    def psum(self, es, shape, dtype, base="p"):
        return es.enter_context(self.nc.psum_tensor(self.name(base), shape, dtype))


def emit_common_setup(C, es):
    nc, P = C.nc, C.P
    C.resid = [C.sbuf(es, [128, T], F32, "resid") for _ in range(CT)]
    C.ones_f = C.sbuf(es, [128, 128], F32, "onesf")
    C.ones_b = C.sbuf(es, [128, 128], BF16, "onesb")
    C.id_b = C.sbuf(es, [128, 128], BF16, "idb")

    t2 = P.dma("gpsimd", C.id_b[:], C.ins["c_ident"], "d_misc2")
    P.op("vector", lambda p: p.memset(C.ones_f[:], 1.0))
    P.op("vector", lambda p: p.memset(C.ones_b[:], 1.0), inc="s_vector")
    P.wait("vector", t2)

    for c in range(CT):
        P.dma("sync", C.resid[c][:], C.ins["resid_in"][c * 128:(c + 1) * 128, :], "d_resid")


def emit_ln(C, es, make_xn=True, xn_es=None, stats_es=None):
    nc, P = C.nc, C.P
    if make_xn:
        C.xn = [C.sbuf(xn_es, [128, T], BF16, "xn") for _ in range(CT)]
    if stats_es is None:
        stats_es = es
    C.mrow = C.sbuf(stats_es, [1, T], F32, "mrow")
    C.rrow = C.sbuf(stats_es, [1, T], F32, "rrow")
    xn = C.xn if make_xn else None
    mrow, rrow = C.mrow, C.rrow
    les = ExitStack()
    sq = [C.sbuf(les, [128, CH], F32, "lnsq") for _ in range(2)]
    ps_s = C.psum(les, [128, CH], F32, "lnpsS")
    ps_q = C.psum(les, [128, CH], F32, "lnpsQ")
    ps_m = C.psum(les, [128, CH], F32, "lnpsM")
    ps_r = C.psum(les, [128, CH], F32, "lnpsR")
    trow = C.sbuf(les, [1, CH], F32, "lnt1")
    trow2 = C.sbuf(les, [1, CH], F32, "lnt2")

    sq_mm = [None, None]
    prev_vec = None
    prev_sqrt = None
    prev_recip = None
    for ch in range(NCH):
        sl = slice(ch * CH, (ch + 1) * CH)
        for c in range(CT):
            P.wait("scalar", sq_mm[c % 2])
            sq_tok = P.op("scalar", lambda p, c=c, sl=sl: p.activation(
                sq[c % 2][:], C.resid[c][:, sl], AF.Square), inc="s_scalar")
            if c == 0:
                P.wait("tensor", prev_vec)
            P.op("tensor", lambda p, c=c, sl=sl: p.matmul(
                ps_s[0:1, :], C.ones_f[:, 0:1], C.resid[c][:, sl],
                start=(c == 0), stop=(c == CT - 1)))
            P.wait("tensor", sq_tok)
            sq_mm[c % 2] = P.op("tensor", lambda p, c=c: p.matmul(
                ps_q[0:1, :], C.ones_f[:, 0:1], sq[c % 2][:],
                start=(c == 0), stop=(c == CT - 1)), inc="s_tensor")
        mm_done = P.cur("s_tensor")
        P.wait("vector", mm_done)
        P.wait("vector", prev_sqrt)
        P.op("vector", lambda p, sl=sl: p.tensor_scalar(
            mrow[:, sl], ps_s[0:1, :], 1.0 / DM, None, ALU.mult))
        P.op("vector", lambda p: p.tensor_scalar(
            trow[:], ps_q[0:1, :], 1.0 / DM, None, ALU.mult), inc="s_vector")
        mtok = P.cur("s_vector")
        P.wait("scalar", mtok)
        P.wait("scalar", prev_recip)
        sqm = P.op("scalar", lambda p, sl=sl: p.activation(
            trow2[:], mrow[0:1, sl], AF.Square), inc="s_scalar")
        P.wait("vector", sqm)
        P.op("vector", lambda p: p.tensor_tensor(
            trow[:], trow[:], trow2[:], ALU.subtract), inc="s_vector")
        P.wait("scalar", P.cur("s_vector"))
        prev_sqrt = P.op("scalar", lambda p: p.activation(
            trow2[:], trow[0:1, :], AF.Sqrt, bias=np.float32(EPS)), inc="s_scalar")
        P.wait("vector", prev_sqrt)
        prev_recip = P.op("vector", lambda p, sl=sl: p.reciprocal(
            rrow[:, sl], trow2[:]), inc="s_vector")

        if make_xn:
            P.wait("tensor", prev_recip)
            P.op("tensor", lambda p, sl=sl: p.matmul(
                ps_m[:], C.ones_f[0:1, :], mrow[0:1, sl], start=True, stop=True))
            bt = P.op("tensor", lambda p, sl=sl: p.matmul(
                ps_r[:], C.ones_f[0:1, :], rrow[0:1, sl], start=True, stop=True),
                inc="s_tensor")
            P.wait("vector", bt)
            for c in range(CT):
                P.op("vector", lambda p, c=c, sl=sl: p.tensor_tensor(
                    sq[c % 2][:], C.resid[c][:, sl], ps_m[:], ALU.subtract))
                P.op("vector", lambda p, c=c, sl=sl: p.tensor_tensor(
                    xn[c][:, sl], sq[c % 2][:], ps_r[:], ALU.mult), inc="s_vector")
            sq_mm[0] = sq_mm[1] = P.cur("s_vector")
        prev_vec = P.cur("s_vector")
    les.close()


def emit_dense_to_scratch(C, es, w_name, nout, src_tiles, dst, dst_row0=0):
    nc, P = C.nc, C.P
    les = ExitStack()
    no = nout // 128
    wfull = C.sbuf(les, [128, CT * nout], BF16, "dpw")
    stage = [C.sbuf(les, [128, T], BF16, "dpst") for _ in range(2)]
    pss = [C.psum(les, [128, CH], F32, "dpps") for _ in range(2)]
    w = C.ins[w_name].rearrange("(a p) o -> p a o", p=128)
    wdst = wfull[:].rearrange("p (a o) -> p a o", o=nout)
    wtoks = [P.dma("gpsimd", wdst[:, a:a + 1, :], w[:, a:a + 1, :], f"d_dw{a % 4}")
             for a in range(CT)]
    P.waits("tensor", wtoks)

    ev_tok = [None, None]
    last_store = [None, None]
    for o in range(no):
        b = o % 2
        for ch in range(NCH):
            pb = ch % 2
            P.wait("tensor", ev_tok[pb])
            for c in range(CT):
                P.op("tensor", lambda p, c=c, ch=ch, pb=pb, o=o: p.matmul(
                    pss[pb][:], wdst[:, c, o * 128:(o + 1) * 128],
                    src_tiles[c][:, ch * CH:(ch + 1) * CH],
                    start=(c == 0), stop=(c == CT - 1)),
                    inc="s_tensor" if c == CT - 1 else None)
            P.wait("scalar", P.cur("s_tensor"))
            P.wait("scalar", last_store[b])
            ev_tok[pb] = P.op("scalar", lambda p, b=b, ch=ch, pb=pb: p.activation(
                stage[b][:, ch * CH:(ch + 1) * CH], pss[pb][:], AF.Copy), inc="s_scalar")
        P.wait("sync", ev_tok[0])
        P.wait("sync", ev_tok[1])
        r0 = dst_row0 + o * 128
        last_store[b] = P.dma("sync", dst[r0:r0 + 128, :], stage[b][:], f"d_ds{b}")
    P.waits("sync", [t for t in last_store if t is not None])
    les.close()


def emit_attention(C, es, layer, ctx_es):
    nc, P = C.nc, C.P
    les = ExitStack()
    C.ctx = [C.sbuf(ctx_es, [128, T], BF16, "ctx") for _ in range(CT)]
    ctx = C.ctx

    qa = [C.sbuf(les, [128, T], BF16, "qa") for _ in range(CT)]
    kaug = [C.sbuf(les, [128, T], BF16, "kaug") for _ in range(2)]
    vfeat = [C.sbuf(les, [64, T], BF16, "vfeat") for _ in range(2)]
    vtok = [[C.sbuf(les, [128, 80], BF16, "vtok") for _ in range(16)] for _ in range(2)]
    sqk = C.sbuf(les, [64, T], BF16, "sqk")
    mxc = C.sbuf(les, [1, NCH], F32, "mxc")
    mx2 = C.sbuf(les, [1, 1], F32, "mx2")
    nrow = C.sbuf(les, [1, CH], F32, "nrow")
    nrow2 = C.sbuf(les, [1, CH], F32, "nrow2")
    qaug = [C.sbuf(les, [128, CH], BF16, "qaug") for _ in range(2)]
    sq2 = C.sbuf(les, [64, CH], BF16, "sq2")
    pt = [C.sbuf(les, [128, CH], BF16, "pt") for _ in range(2)]
    rb = C.sbuf(les, [128, CH], F32, "rb")

    ps_s = [C.psum(les, [128, CH], F32, "psS") for _ in range(2)]
    ps_c = C.psum(les, [128, CH], F32, "psC")
    ps_b = C.psum(les, [128, CH], F32, "psB")
    ps_n = C.psum(les, [128, CH], F32, "psN")
    ps_tr = C.psum(les, [128, 512], BF16, "psTr")

    if layer == 1:
        rt = C.sbuf(les, [64, 64], BF16, "rotT")
        rtok = P.dma("gpsimd", rt[:], C.ins["l1_rot_t"], "d_rot")
        P.wait("tensor", rtok)
        kr_ps = [C.psum(les, [128, CH], F32, "psKr") for _ in range(2)]

    def rows_q(h):
        return h * 64 if layer == 0 else h * 192

    def rows_k(h):
        return 1024 + h * 64 if layer == 0 else h * 192 + 64

    def rows_v(h):
        return 2048 + h * 64 if layer == 0 else h * 192 + 128

    for c in range(CT):
        h0, h1 = 2 * c, 2 * c + 1
        P.dma("sync", qa[c][0:64, :], C.qkv_s[rows_q(h0):rows_q(h0) + 64, :], "d_qa")
        P.dma("sync", qa[c][64:128, :], C.qkv_s[rows_q(h1):rows_q(h1) + 64, :], "d_qa")
    qa_tok = P.cur("d_qa")

    for par in range(2):
        for kt in range(16):
            P.op("vector", lambda p, par=par, kt=kt: p.memset(vtok[par][kt][:, 64:65], 1.0))
    P.op("vector", lambda p: p.memset(kaug[0][64:65, :], 1.0))
    P.op("vector", lambda p: p.memset(kaug[1][64:65, :], 1.0), inc="s_vector")
    ones_done = P.cur("s_vector")

    k_load = [None, None]
    v_load = [None, None]
    k_free = [None, None]
    v_free = [None, None]
    vt_free = [None, None]

    k_load[0] = P.dma("sync", kaug[0][0:64, :], C.qkv_s[rows_k(0):rows_k(0) + 64, :], "d_k0")
    v_load[0] = P.dma("sync", vfeat[0][:], C.qkv_s[rows_v(0):rows_v(0) + 64, :], "d_v0")

    for h in range(NH):
        par = h % 2
        if h + 1 < NH:
            np_ = (h + 1) % 2
            P.wait("sync", k_free[np_])
            k_load[np_] = P.dma("sync", kaug[np_][0:64, :],
                                C.qkv_s[rows_k(h + 1):rows_k(h + 1) + 64, :], f"d_k{np_}")
            P.wait("sync", v_free[np_])
            v_load[np_] = P.dma("sync", vfeat[np_][:],
                                C.qkv_s[rows_v(h + 1):rows_v(h + 1) + 64, :], f"d_v{np_}")

        P.wait("tensor", k_load[par])
        P.wait("scalar", k_load[par])
        if layer == 1:
            kr_ev = [None, None]
            for ch in range(NCH):
                sl = slice(ch * CH, (ch + 1) * CH)
                P.wait("tensor", kr_ev[ch % 2])
                t = P.op("tensor", lambda p, par=par, sl=sl, ch=ch: p.matmul(
                    kr_ps[ch % 2][0:64, :], rt[:], kaug[par][0:64, sl],
                    start=True, stop=True), inc="s_tensor")
                P.wait("scalar", t)
                kr_ev[ch % 2] = P.op("scalar", lambda p, par=par, sl=sl, ch=ch: p.activation(
                    kaug[par][0:64, sl], kr_ps[ch % 2][0:64, :], AF.Copy), inc="s_scalar")
            P.wait("scalar", P.cur("s_scalar"))
        sct = P.op("scalar", lambda p, par=par: p.activation(
            sqk[:], kaug[par][0:64, :], AF.Square), inc="s_scalar")
        P.wait("tensor", sct)
        mx_prev = None
        for ch in range(NCH):
            P.wait("tensor", mx_prev)
            t = P.op("tensor", lambda p, ch=ch: p.matmul(
                ps_n[0:1, :], C.ones_b[0:64, 0:1], sqk[:, ch * CH:(ch + 1) * CH],
                start=True, stop=True), inc="s_tensor")
            P.wait("vector", t)
            mx_prev = P.op("vector", lambda p, ch=ch: p.tensor_reduce(
                mxc[0:1, ch:ch + 1], ps_n[0:1, :], AX.X, ALU.max), inc="s_vector")
        mxtok = P.op("vector", lambda p: p.tensor_reduce(
            mx2[:], mxc[0:1, :], AX.X, ALU.max), inc="s_vector")

        P.wait("tensor", v_load[par])
        P.wait("vector", ones_done)
        P.wait("vector", vt_free[par])
        for kt in range(16):
            slot = kt % 4
            t = P.op("tensor", lambda p, par=par, kt=kt, slot=slot: p.transpose(
                ps_tr[:, slot * 128:slot * 128 + 64],
                vfeat[par][:, kt * 128:(kt + 1) * 128], C.id_b[0:64, 0:64]),
                inc="s_tensor")
            P.wait("vector", t)
            P.op("vector", lambda p, par=par, kt=kt, slot=slot: p.tensor_copy(
                vtok[par][kt][:, 0:64], ps_tr[:, slot * 128:slot * 128 + 64]),
                inc="s_vector")
            if kt >= 3:
                P.wait("tensor", P.cur("s_vector"))
        v_free[par] = P.cur("s_tensor")
        vt_ready = P.cur("s_vector")

        for ch in range(NCH):
            sl = slice(ch * CH, (ch + 1) * CH)
            qb = ch % 2
            P.wait("vector", mxtok)
            P.wait("vector", qa_tok)
            cp = P.op("vector", lambda p, h=h, sl=sl, qb=qb: p.tensor_copy(
                qaug[qb][0:64, :], qa[h // 2][(h % 2) * 64:(h % 2) * 64 + 64, sl]),
                inc="s_vector")
            if layer == 1:
                P.wait("tensor", cp)
                t = P.op("tensor", lambda p, qb=qb: p.matmul(
                    ps_b[0:64, :], rt[:], qaug[qb][0:64, :], start=True, stop=True),
                    inc="s_tensor")
                P.wait("scalar", t)
                qrot = P.op("scalar", lambda p, qb=qb: p.activation(
                    qaug[qb][0:64, :], ps_b[0:64, :], AF.Copy), inc="s_scalar")
            else:
                qrot = cp
            P.wait("scalar", qrot)
            sct = P.op("scalar", lambda p, qb=qb: p.activation(
                sq2[:], qaug[qb][0:64, :], AF.Square), inc="s_scalar")
            P.wait("tensor", sct)
            qn = P.op("tensor", lambda p: p.matmul(
                ps_n[0:1, :], C.ones_b[0:64, 0:1], sq2[:], start=True, stop=True),
                inc="s_tensor")
            P.wait("vector", qn)
            P.op("vector", lambda p: p.tensor_scalar(
                nrow[:], ps_n[0:1, :], mx2[0:1, 0:1], None, ALU.mult), inc="s_vector")
            P.wait("scalar", P.cur("s_vector"))
            st = P.op("scalar", lambda p: p.activation(
                nrow2[:], nrow[0:1, :], AF.Sqrt), inc="s_scalar")
            P.wait("vector", st)
            rowtok = P.op("vector", lambda p, qb=qb: p.tensor_scalar(
                qaug[qb][64:65, :], nrow2[0:1, :], -1.0, None, ALU.mult), inc="s_vector")
            P.wait("tensor", rowtok)
            P.wait("tensor", vt_ready)

            smm = [None] * 16
            exp_toks = [None, None]

            def emit_s(kt):
                sb = kt % 2
                P.wait("tensor", exp_toks[sb])
                smm[kt] = P.op("tensor", lambda p, par=par, kt=kt, sb=sb, qb=qb: p.matmul(
                    ps_s[sb][:], kaug[par][0:65, kt * 128:(kt + 1) * 128],
                    qaug[qb][0:65, :], start=True, stop=True), inc="s_tensor")

            emit_s(0)
            for kt in range(16):
                if kt + 1 < 16:
                    emit_s(kt + 1)
                sb = kt % 2
                P.wait("scalar", smm[kt])
                exp_toks[sb] = P.op("scalar", lambda p, sb=sb: p.activation(
                    pt[sb][:], ps_s[sb][:], AF.Exp, scale=0.125), inc="s_scalar")
                P.wait("tensor", exp_toks[sb])
                P.op("tensor", lambda p, par=par, kt=kt, sb=sb: p.matmul(
                    ps_c[0:65, :], vtok[par][kt][:, 0:65], pt[sb][:],
                    start=(kt == 0), stop=(kt == 15)),
                    inc="s_tensor" if kt == 15 else None)
            ctx_done = P.cur("s_tensor")
            P.wait("scalar", ctx_done)
            P.op("scalar", lambda p: p.activation(nrow[:], ps_c[64:65, :], AF.Ln))
            rtok = P.op("scalar", lambda p: p.activation(
                nrow2[:], nrow[0:1, :], AF.Exp, scale=-1.0), inc="s_scalar")
            P.wait("tensor", rtok)
            bc = P.op("tensor", lambda p: p.matmul(
                ps_b[:], C.ones_f[0:1, :], nrow2[0:1, :], start=True, stop=True),
                inc="s_tensor")
            P.wait("vector", bc)
            P.op("vector", lambda p: p.tensor_copy(rb[:], ps_b[:]), inc="s_vector")
            P.op("vector", lambda p, h=h, sl=sl: p.tensor_tensor(
                ctx[h // 2][(h % 2) * 64:(h % 2) * 64 + 64, sl], ps_c[0:64, :],
                rb[0:64, :], ALU.mult), inc="s_vector")
            P.wait("tensor", P.cur("s_vector"))

        k_free[par] = P.cur("s_tensor")
        vt_free[par] = P.cur("s_tensor")
    les.close()


def emit_proj_residual(C, es, w_name, b_name):
    nc, P = C.nc, C.P
    ctx = C.ctx
    les = ExitStack()
    wfull = C.sbuf(les, [128, CT * DM], BF16, "pjw")
    bb = C.sbuf(les, [128, CT], F32, "pjb")
    pss = [C.psum(les, [128, CH], F32, "pjps") for _ in range(2)]
    w = C.ins[w_name].rearrange("(a p) o -> p a o", p=128)
    wdst = wfull[:].rearrange("p (a o) -> p a o", o=DM)
    wtoks = [P.dma("gpsimd", wdst[:, a:a + 1, :], w[:, a:a + 1, :], f"d_pw{a % 4}")
             for a in range(CT)]
    btok = P.dma("sync", bb[:], C.ins[b_name], "d_pjb")
    P.waits("tensor", wtoks)
    P.wait("vector", btok)

    ev_tok = [None, None]
    for o in range(CT):
        for ch in range(NCH):
            pb = ch % 2
            P.wait("tensor", ev_tok[pb])
            for c in range(CT):
                P.op("tensor", lambda p, c=c, ch=ch, pb=pb, o=o: p.matmul(
                    pss[pb][:], wdst[:, c, o * 128:(o + 1) * 128],
                    ctx[c][:, ch * CH:(ch + 1) * CH],
                    start=(c == 0), stop=(c == CT - 1)),
                    inc="s_tensor" if c == CT - 1 else None)
            P.wait("vector", P.cur("s_tensor"))
            ev_tok[pb] = P.op("vector", lambda p, o=o, ch=ch, pb=pb: p.scalar_tensor_tensor(
                C.resid[o][:, ch * CH:(ch + 1) * CH], pss[pb][:], bb[:, o:o + 1],
                C.resid[o][:, ch * CH:(ch + 1) * CH], ALU.add, ALU.add), inc="s_vector")
    P.wait("tensor", P.cur("s_vector"))
    les.close()


def emit_routing_l0(C, es, route_es):
    nc, P = C.nc, C.P
    les = ExitStack()
    C.gatesT = C.sbuf(route_es, [8, T], BF16, "gatesT")
    C.esel = C.sbuf(route_es, [32, 32 * 128], BF16, "esel")
    C.id_f = C.sbuf(route_es, [128, 128], F32, "idf")
    gatesT, id_f, xn = C.gatesT, C.id_f, C.xn
    P.dma("gpsimd", C.esel[:], C.ins["c_esel"], "d_esel_g")
    et = P.dma("sync", C.id_f[:], C.ins["c_ident"], "d_esel")
    P.wait("tensor", et)
    gw = C.sbuf(les, [128, CT * 8], BF16, "gw")
    lgT = C.sbuf(les, [8, T], F32, "lgT")
    lg_tok = C.sbuf(les, [128, 8], F32, "lgtok")
    w1 = C.sbuf(les, [128, 1], F32, "ro1")
    w2 = C.sbuf(les, [128, 1], F32, "ro2")
    msk = C.sbuf(les, [128, 8], F32, "romask")
    e1 = C.sbuf(les, [128, 8], F32, "roe1")
    ps_l = C.psum(les, [128, CH], F32, "rops")
    ps_t = C.psum(les, [128, 512], F32, "rops2")

    wv = C.ins["l0_gwt"].rearrange("(a p) e -> p a e", p=128)
    gwv = gw[:].rearrange("p (a e) -> p a e", e=8)
    wtok = P.dma("gpsimd", gwv[:], wv[:], "d_rw")
    P.wait("tensor", wtok)
    prev_ev = None
    for ch in range(NCH):
        sl = slice(ch * CH, (ch + 1) * CH)
        P.wait("tensor", prev_ev)
        for c in range(CT):
            P.op("tensor", lambda p, c=c, sl=sl: p.matmul(
                ps_l[0:8, :], gwv[:, c, :], xn[c][:, sl],
                start=(c == 0), stop=(c == CT - 1)),
                inc="s_tensor" if c == CT - 1 else None)
        P.wait("vector", P.cur("s_tensor"))
        prev_ev = P.op("vector", lambda p, sl=sl: p.tensor_copy(lgT[:, sl], ps_l[0:8, :]),
                       inc="s_vector")
    prev_wr = None
    prev_tr = None
    for tt in range(T // 128):
        tsl = slice(tt * 128, (tt + 1) * 128)
        P.wait("tensor", prev_wr)
        tr = P.op("tensor", lambda p, tsl=tsl: p.transpose(
            ps_t[:, 0:8], lgT[0:8, tsl], id_f[0:8, 0:8]), inc="s_tensor")
        P.wait("vector", tr)
        P.op("vector", lambda p: p.tensor_copy(lg_tok[:], ps_t[:, 0:8]), inc="s_vector")
        P.op("vector", lambda p: p.tensor_reduce(w1[:], lg_tok[:], AX.X, ALU.max))
        P.op("vector", lambda p: p.tensor_scalar(msk[:], lg_tok[:], w1[:, 0:1], None,
                                                 ALU.is_equal))
        P.op("vector", lambda p: p.scalar_tensor_tensor(
            msk[:], msk[:], -1e30, lg_tok[:], ALU.mult, ALU.add))
        P.op("vector", lambda p: p.tensor_reduce(w2[:], msk[:], AX.X, ALU.max))
        P.op("vector", lambda p: p.tensor_scalar(w1[:], w1[:], -1.0, None, ALU.mult),
             inc="s_vector")
        vt = P.cur("s_vector")
        P.wait("scalar", vt)
        ex = P.op("scalar", lambda p: p.activation(e1[:], lg_tok[:], AF.Exp, bias=w1[:, 0:1]),
                  inc="s_scalar")
        P.wait("vector", ex)
        P.op("vector", lambda p: p.tensor_scalar(msk[:], lg_tok[:], w2[:, 0:1], None,
                                                 ALU.is_ge))
        P.op("vector", lambda p: p.tensor_tensor(e1[:], e1[:], msk[:], ALU.mult))
        P.op("vector", lambda p: p.tensor_reduce(w2[:], e1[:], AX.X, ALU.add))
        P.op("vector", lambda p: p.reciprocal(w1[:], w2[:]))
        P.op("vector", lambda p: p.tensor_scalar(e1[:], e1[:], w1[:, 0:1], None, ALU.mult),
             inc="s_vector")
        gt = P.cur("s_vector")
        P.wait("tensor", gt)
        prev_tr = P.op("tensor", lambda p: p.transpose(
            ps_t[0:8, 8:136], e1[:], id_f[:]), inc="s_tensor")
        P.wait("vector", prev_tr)
        prev_wr = P.op("vector", lambda p, tsl=tsl: p.tensor_copy(
            gatesT[0:8, tsl], ps_t[0:8, 8:136]), inc="s_vector")
        P.wait("tensor", prev_wr)
    les.close()


def emit_moe_l0(C, es):
    nc, P = C.nc, C.P
    esel, gatesT, xn = C.esel, C.gatesT, C.xn
    les = ExitStack()
    NE = 9
    NHT = 32
    HALF = T // 2
    w1b = [C.sbuf(les, [128, CT * 256], BF16, "w1b") for _ in range(2)]
    w2b = [C.sbuf(les, [128, 2 * 512], BF16, "w2b") for _ in range(2)]
    aE = [C.sbuf(les, [128, HALF], BF16, "aE") for _ in range(NHT)]
    gBs = [C.sbuf(les, [128, CH], BF16, "gBs") for _ in range(2)]
    sst = [C.sbuf(les, [128, CH], BF16, "sst") for _ in range(2)]
    pool = [C.psum(les, [128, CH], F32, "fps") for _ in range(8)]

    w1v = C.ins["l0_w1t"]
    w2v = C.ins["l0_w2t"]

    w1_tok = [None, None]
    w1_use = [None, None]
    w2_tok = [None, None]
    w2_use = [None, None]
    gB_done = [None, None]

    def load_w1(e, ht2, buf):
        P.wait("gpsimd", w1_use[buf])
        src = C.ins["l0_w1t"][e].rearrange("(a p) o -> p a o", p=128)[
            :, :, ht2 * 256:(ht2 + 1) * 256]
        dst = w1b[buf][:].rearrange("p (a o) -> p a o", o=256)
        w1_tok[buf] = P.dma("gpsimd", dst[:], src, f"d_w1{buf}")

    def load_w2(e, oh, ht2, buf):
        P.wait("gpsimd", w2_use[buf])
        src = C.ins["l0_w2t"][e][ht2 * 256:(ht2 + 1) * 256,
                                 oh * 512:(oh + 1) * 512].rearrange("(b p) o -> p b o", p=128)
        dst = w2b[buf][:].rearrange("p (b o) -> p b o", o=512)
        w2_tok[buf] = P.dma("gpsimd", dst[:], src, f"d_w2{buf}")

    a_ready = None
    ev_prev = {}
    for e in range(NE):
        for hf in range(2):
            chs = (2 * hf, 2 * hf + 1)
            if e > 0:
                for i, ch in enumerate(chs):
                    P.wait("tensor", gB_done[i])
                    t = P.op("tensor", lambda p, e=e, ch=ch, i=i: p.matmul(
                        pool[2 + i][:], esel[0:8, (e - 1) * 128:e * 128],
                        gatesT[0:8, ch * CH:(ch + 1) * CH], start=True, stop=True),
                        inc="s_tensor")
                    P.wait("vector", t)
                    gB_done[i] = P.op("vector", lambda p, i=i: p.tensor_copy(
                        gBs[i][:], pool[2 + i][:]), inc="s_vector")
                P.wait("tensor", gB_done[0])
                P.wait("tensor", gB_done[1])

            load_w1(e, 0, 0)
            for ht2 in range(16):
                buf = ht2 % 2
                if ht2 + 1 < 16:
                    load_w1(e, ht2 + 1, (ht2 + 1) % 2)
                P.wait("tensor", w1_tok[buf])
                wv = w1b[buf][:].rearrange("p (a o) -> p a o", o=256)
                for k in range(2):
                    ht = ht2 * 2 + k
                    for sub in range(2):
                        ch = chs[sub]
                        pb = (ht * 2 + sub) % 2
                        P.wait("tensor", ev_prev.get(("A", pb)))
                        for c in range(CT):
                            P.op("tensor", lambda p, c=c, k=k, ch=ch, pb=pb, wv=wv: p.matmul(
                                pool[pb][:], wv[:, c, k * 128:(k + 1) * 128],
                                xn[c][:, ch * CH:(ch + 1) * CH],
                                start=(c == 0), stop=(c == CT - 1)),
                                inc="s_tensor" if c == CT - 1 else None)
                        mm = P.cur("s_tensor")
                        P.wait("scalar", mm)
                        if e == 0:
                            ev_prev[("A", pb)] = P.op(
                                "scalar", lambda p, ht=ht, sub=sub, pb=pb: p.activation(
                                    aE[ht][:, sub * CH:(sub + 1) * CH], pool[pb][:], AF.Silu),
                                inc="s_scalar")
                        else:
                            st = P.op("scalar", lambda p, sub=sub, pb=pb: p.activation(
                                sst[pb][:], pool[pb][:], AF.Silu), inc="s_scalar")
                            P.wait("vector", st)
                            ev_prev[("A", pb)] = P.op(
                                "vector", lambda p, ht=ht, sub=sub, pb=pb: p.tensor_tensor(
                                    aE[ht][:, sub * CH:(sub + 1) * CH], sst[pb][:],
                                    gBs[sub][:], ALU.mult), inc="s_vector")
                w1_use[buf] = P.cur("s_tensor")
            a_done_tok = {"scalar": P.cur("s_scalar"), "vector": P.cur("s_vector")}

            for oh in range(2):
                load_w2(e, oh, 0, 0)
                for ht2 in range(16):
                    buf = ht2 % 2
                    if ht2 + 1 < 16:
                        load_w2(e, oh, ht2 + 1, (ht2 + 1) % 2)
                    P.wait("tensor", w2_tok[buf])
                    if ht2 == 0:
                        P.wait("tensor", a_done_tok["scalar"])
                        P.wait("tensor", a_done_tok["vector"])
                    w2vv = w2b[buf][:].rearrange("p (b o) -> p b o", o=512)
                    for k in range(2):
                        ht = ht2 * 2 + k
                        for sub in range(2):
                            for oi in range(4):
                                pb = sub * 4 + oi
                                if ht == 0:
                                    P.wait("tensor", ev_prev.get(("B", oh, pb)))
                                P.op("tensor", lambda p, k=k, oi=oi, sub=sub, pb=pb, ht=ht,
                                     w2vv=w2vv: p.matmul(
                                         pool[pb][:], w2vv[:, k, oi * 128:(oi + 1) * 128],
                                         aE[ht][:, sub * CH:(sub + 1) * CH],
                                         start=(ht == 0), stop=(ht == NHT - 1)),
                                     inc="s_tensor" if (k == 1 and sub == 1 and oi == 3) else None)
                    w2_use[buf] = P.cur("s_tensor")
                bdone = P.cur("s_tensor")
                P.wait("vector", bdone)
                for sub in range(2):
                    ch = chs[sub]
                    for oi in range(4):
                        pb = sub * 4 + oi
                        ot = oh * 4 + oi
                        ev_prev[("B", oh, pb)] = P.op(
                            "vector", lambda p, ot=ot, ch=ch, pb=pb: p.tensor_tensor(
                                C.resid[ot][:, ch * CH:(ch + 1) * CH], pool[pb][:],
                                C.resid[ot][:, ch * CH:(ch + 1) * CH], ALU.add),
                            inc="s_vector")
                P.wait("tensor", P.cur("s_vector"))
    les.close()


def emit_sw_spill(C, es):
    nc, P = C.nc, C.P
    les = ExitStack()
    swf = [C.sbuf(les, [128, 3072], F32, "swf") for _ in range(2)]
    scf = C.sbuf(les, [128, 3072], F32, "scf")
    ob = [C.sbuf(les, [128, 3072], BF16, "swob") for _ in range(2)]
    sc_users = None
    st_tok = [None, None]
    sw_tok = [None, None]
    for c in range(CT):
        P.wait("sync", sc_users)
        sc_tok = P.dma("sync", scf[:], C.ins["l1_sct"][c * 128:(c + 1) * 128, :], "d_sc")
        for j in range(8):
            buf = j % 2
            P.wait("sync", sw_tok[buf])
            sw_load = P.dma("sync", swf[buf][:], C.ins["l1_swt"][j, c * 128:(c + 1) * 128, :],
                            f"d_sw{buf}")
            P.wait("vector", sw_load)
            P.wait("vector", sc_tok)
            P.wait("vector", st_tok[buf])
            tt = P.op("vector", lambda p, buf=buf: p.tensor_tensor(
                ob[buf][:], swf[buf][:], scf[:], ALU.mult), inc="s_vector")
            sw_tok[buf] = tt
            P.wait("sync", tt)
            st_tok[buf] = P.dma("sync", C.swb_s[j, c * 128:(c + 1) * 128, :], ob[buf][:],
                                f"d_sb{buf}")
        sc_users = P.cur("s_vector")
    P.waits("sync", [st_tok[0], st_tok[1]])
    les.close()


def emit_spline_qkv(C, es):
    nc, P = C.nc, C.P
    mrow, rrow = C.mrow, C.rrow
    les = ExitStack()
    NJ0 = 11
    bsF = [C.sbuf(les, [128, 8 * CH], BF16, "bsF") for _ in range(CT - 1)]
    bw = C.sbuf(les, [128, NJ0 * CH], BF16, "bw")

    def bs_src(c, j):
        if c < CT - 1:
            return bsF[c][:, j * CH:(j + 1) * CH]
        return bw[:, j * CH:(j + 1) * CH]
    sx = [C.sbuf(les, [128, CH], BF16, "sx") for _ in range(CT)]
    t1 = C.sbuf(les, [128, CH], BF16, "spt1")
    t2 = C.sbuf(les, [128, CH], BF16, "spt2")
    xnf = C.sbuf(les, [128, CH], F32, "spxnf")
    tmp = C.sbuf(les, [128, CH], F32, "sptmp")
    gcolall = C.sbuf(les, [128, CT * 12], F32, "gcol")
    gcol = [gcolall[:, c * 12:(c + 1) * 12] for c in range(CT)]
    rtab = [C.sbuf(les, [128, 64], F32, "rtab") for _ in range(CT)]
    sblk = [C.sbuf(les, [128, 64 * 128], BF16, "sblk") for _ in range(2)]
    bwb = [C.sbuf(les, [128, CT * 128], BF16, "bwb") for _ in range(2)]
    stage = [C.sbuf(les, [128, CH], BF16, "spst") for _ in range(2)]
    ps_m = C.psum(les, [128, CH], F32, "sppsM")
    ps_r = C.psum(les, [128, CH], F32, "sppsR")
    pss = [C.psum(les, [128, CH], F32, "spps") for _ in range(2)]

    OFF_R1 = {1: 0, 2: 20, 3: 38}
    OFF_N2 = {1: 10, 2: 29, 3: 46}

    g_tok = P.dma("sync", gcolall[:].rearrange("p (c g) -> p c g", g=12),
                  C.ins["l1_grid"].rearrange("(c p) g -> p c g", p=128), "d_grid")
    P.wait("vector", g_tok)
    for c in range(CT):
        for p_ in (1, 2, 3):
            cnt = NJ0 - p_
            P.op("vector", lambda p, c=c, p_=p_, cnt=cnt: p.tensor_tensor(
                rtab[c][:, OFF_R1[p_]:OFF_R1[p_] + cnt], gcol[c][:, p_:p_ + cnt],
                gcol[c][:, 0:cnt], ALU.subtract))
            P.op("vector", lambda p, c=c, p_=p_, cnt=cnt: p.reciprocal(
                rtab[c][:, OFF_R1[p_]:OFF_R1[p_] + cnt],
                rtab[c][:, OFF_R1[p_]:OFF_R1[p_] + cnt]))
            P.op("vector", lambda p, c=c, p_=p_, cnt=cnt: p.tensor_tensor(
                rtab[c][:, OFF_N2[p_]:OFF_N2[p_] + cnt], gcol[c][:, 1:1 + cnt],
                gcol[c][:, p_ + 1:p_ + 1 + cnt], ALU.subtract))
            P.op("vector", lambda p, c=c, p_=p_, cnt=cnt: p.reciprocal(
                rtab[c][:, OFF_N2[p_]:OFF_N2[p_] + cnt],
                rtab[c][:, OFF_N2[p_]:OFF_N2[p_] + cnt]))

    sp_tok = [None, None]
    bw_tok = [None, None]
    blk_use = [None, None]
    st_tok = [None, None]
    mm_all = None
    for ch in range(NCH):
        sl = slice(ch * CH, (ch + 1) * CH)
        P.wait("tensor", P.cur("s_vector"))
        P.op("tensor", lambda p, sl=sl: p.matmul(
            ps_m[:], C.ones_f[0:1, :], mrow[0:1, sl], start=True, stop=True))
        bt = P.op("tensor", lambda p, sl=sl: p.matmul(
            ps_r[:], C.ones_f[0:1, :], rrow[0:1, sl], start=True, stop=True),
            inc="s_tensor")
        P.wait("vector", bt)
        P.wait("vector", mm_all)
        P.wait("scalar", mm_all)
        sil_tok = None
        for c in range(CT):
            P.op("vector", lambda p, c=c, sl=sl: p.tensor_tensor(
                tmp[:], C.resid[c][:, sl], ps_m[:], ALU.subtract))
            P.wait("vector", sil_tok)
            xt = P.op("vector", lambda p: p.tensor_tensor(xnf[:], tmp[:], ps_r[:], ALU.mult))
            P.wait("scalar", xt)
            sil_tok = P.op("scalar", lambda p, c=c: p.activation(sx[c][:], xnf[:], AF.Silu))
            for j in range(NJ0):
                P.op("vector", lambda p, c=c, j=j: p.tensor_scalar(
                    t1[:], xnf[:], gcol[c][:, j:j + 1], None, ALU.is_ge))
                P.op("vector", lambda p, c=c, j=j: p.tensor_scalar(
                    t2[:], xnf[:], gcol[c][:, j + 1:j + 2], None, ALU.is_lt))
                P.op("vector", lambda p, j=j: p.tensor_tensor(
                    bw[:, j * CH:(j + 1) * CH], t1[:], t2[:], ALU.mult))
            for p_ in (1, 2, 3):
                for j in range(NJ0 - p_):
                    P.op("vector", lambda p, c=c, p_=p_, j=j: p.tensor_scalar(
                        t1[:], xnf[:], gcol[c][:, j:j + 1],
                        rtab[c][:, OFF_R1[p_] + j:OFF_R1[p_] + j + 1],
                        ALU.subtract, ALU.mult))
                    P.op("vector", lambda p, c=c, p_=p_, j=j: p.tensor_scalar(
                        t2[:], xnf[:], gcol[c][:, j + p_ + 1:j + p_ + 2],
                        rtab[c][:, OFF_N2[p_] + j:OFF_N2[p_] + j + 1],
                        ALU.subtract, ALU.mult))
                    P.op("vector", lambda p, j=j: p.tensor_tensor(
                        t1[:], t1[:], bw[:, j * CH:(j + 1) * CH], ALU.mult))
                    P.op("vector", lambda p, j=j: p.tensor_tensor(
                        t2[:], t2[:], bw[:, (j + 1) * CH:(j + 2) * CH], ALU.mult))
                    P.op("vector", lambda p, j=j: p.tensor_tensor(
                        bw[:, j * CH:(j + 1) * CH], t1[:], t2[:], ALU.add))
            if c < CT - 1:
                for j in range(8):
                    P.op("vector", lambda p, c=c, j=j: p.tensor_copy(
                        bsF[c][:, j * CH:(j + 1) * CH], bw[:, j * CH:(j + 1) * CH]))
        bs_done = P.cur("s_vector")
        sx_done = P.cur("s_scalar")

        ev_tok = [None, None]
        for o in range(24):
            buf = o % 2
            P.wait("sync", blk_use[buf])
            svw = sblk[buf][:].rearrange("p (j c o) -> p j c o", c=CT, o=128)
            for j in range(8):
                sp_tok[buf] = P.dma(
                    "sync", svw[:, j, :, :],
                    C.swb_s[j, :, o * 128:(o + 1) * 128].rearrange(
                        "(c p) o -> p c o", p=128), f"d_sp{buf}")
            P.wait("gpsimd", blk_use[buf])
            bvw = bwb[buf][:].rearrange("p (c o) -> p c o", o=128)
            bw_tok[buf] = P.dma(
                "gpsimd", bvw[:],
                C.ins["l1_bwt"][:, o * 128:(o + 1) * 128].rearrange("(c p) o -> p c o", p=128),
                f"d_bw{buf}")
            pb = o % 2
            P.wait("tensor", ev_tok[pb])
            P.wait("tensor", bw_tok[buf])
            P.wait("tensor", sx_done)
            for c in range(CT):
                P.op("tensor", lambda p, c=c, buf=buf, pb=pb: p.matmul(
                    pss[pb][:], bwb[buf][:].rearrange("p (c o) -> p c o", o=128)[:, c, :],
                    sx[c][:], start=(c == 0), stop=False))
            P.wait("tensor", sp_tok[buf])
            P.wait("tensor", bs_done)
            for j in range(8):
                for c in range(CT):
                    P.op("tensor", lambda p, j=j, c=c, buf=buf, pb=pb: p.matmul(
                        pss[pb][:], sblk[buf][:].rearrange(
                            "p (j c o) -> p j c o", c=CT, o=128)[:, j, c, :],
                        bs_src(c, j),
                        start=False, stop=(j == 7 and c == CT - 1)),
                        inc="s_tensor" if (j == 7 and c == CT - 1) else None)
            mmt = P.cur("s_tensor")
            blk_use[buf] = mmt
            P.wait("scalar", mmt)
            P.wait("scalar", st_tok[buf])
            ev = P.op("scalar", lambda p, buf=buf, pb=pb: p.activation(
                stage[buf][:], pss[pb][:], AF.Copy), inc="s_scalar")
            ev_tok[pb] = ev
            P.wait("sync", ev)
            st_tok[buf] = P.dma("sync", C.qkv_s[o * 128:(o + 1) * 128, sl], stage[buf][:],
                                f"d_sq{buf}")
        mm_all = P.cur("s_tensor")
    P.waits("sync", [st_tok[0], st_tok[1]])
    les.close()


def emit_routing_l1(C, es, route_es):
    nc, P = C.nc, C.P
    les = ExitStack()
    C.routeT = C.sbuf(route_es, [32, T], BF16, "routeT")
    C.esel = C.sbuf(route_es, [32, 32 * 128], BF16, "esel")
    C.id_f = C.sbuf(route_es, [128, 128], F32, "idf")
    routeT, id_f, xn = C.routeT, C.id_f, C.xn
    P.dma("gpsimd", C.esel[:], C.ins["c_esel"], "d_esel_g")
    et = P.dma("sync", C.id_f[:], C.ins["c_ident"], "d_esel")
    P.wait("tensor", et)

    rw = C.sbuf(les, [128, CT * 32], BF16, "rw")
    lgT = C.sbuf(les, [32, T], F32, "lgT")
    br = C.sbuf(les, [1, 32], F32, "rbr")
    bB = C.sbuf(les, [128, 32], F32, "rbB")
    ltok = C.sbuf(les, [128, 32], F32, "rltok")
    sc = C.sbuf(les, [128, 32], F32, "rsc")
    sfc = C.sbuf(les, [128, 32], F32, "rsfc")
    tmp = C.sbuf(les, [128, 32], F32, "rtmp")
    tc = C.sbuf(les, [128, 32], F32, "rtc")
    msk = C.sbuf(les, [128, 32], F32, "rmsk")
    m8 = C.sbuf(les, [128, 8], F32, "rm8")
    gsc = C.sbuf(les, [128, 4], F32, "rgsc")
    g2 = C.sbuf(les, [128, 4], F32, "rg2")
    c1 = C.sbuf(les, [128, 1], F32, "rc1")
    c2 = C.sbuf(les, [128, 1], F32, "rc2")
    ps_l = C.psum(les, [128, CH], F32, "rops")
    ps_t = C.psum(les, [128, 512], F32, "rops2")

    wv = C.ins["l1_rwt"].rearrange("(a p) e -> p a e", p=128)
    rwv = rw[:].rearrange("p (a e) -> p a e", e=32)
    wtok = P.dma("gpsimd", rwv[:], wv[:], "d_rw")
    btok = P.dma("sync", br[:], C.ins["l1_rbias"], "d_rb")
    P.wait("tensor", wtok)
    P.wait("tensor", btok)
    bc = P.op("tensor", lambda p: p.matmul(
        ps_t[:, 0:32], C.ones_f[0:1, :], br[0:1, :], start=True, stop=True), inc="s_tensor")
    P.wait("vector", bc)
    bB_tok = P.op("vector", lambda p: p.tensor_copy(bB[:], ps_t[:, 0:32]))

    prev_ev = None
    for ch in range(NCH):
        sl = slice(ch * CH, (ch + 1) * CH)
        P.wait("tensor", prev_ev)
        for c in range(CT):
            P.op("tensor", lambda p, c=c, sl=sl: p.matmul(
                ps_l[0:32, :], rwv[:, c, :], xn[c][:, sl],
                start=(c == 0), stop=(c == CT - 1)),
                inc="s_tensor" if c == CT - 1 else None)
        P.wait("vector", P.cur("s_tensor"))
        prev_ev = P.op("vector", lambda p, sl=sl: p.tensor_copy(lgT[:, sl], ps_l[0:32, :]))

    prev_wr = None
    for tt in range(T // 128):
        tsl = slice(tt * 128, (tt + 1) * 128)
        P.wait("tensor", prev_wr)
        tr = P.op("tensor", lambda p, tsl=tsl: p.transpose(
            ps_t[0:128, 64:96], lgT[0:32, tsl], id_f[0:32, 0:32]), inc="s_tensor")
        P.wait("vector", tr)
        ct = P.op("vector", lambda p: p.tensor_copy(ltok[:], ps_t[0:128, 64:96]))
        P.wait("scalar", ct)
        sg = P.op("scalar", lambda p: p.activation(sc[:], ltok[:], AF.Sigmoid))
        P.wait("vector", sg)
        P.wait("vector", bB_tok)
        P.op("vector", lambda p: p.tensor_tensor(sfc[:], sc[:], bB[:], ALU.add))
        for g in range(4):
            s8 = slice(8 * g, 8 * g + 8)
            P.op("vector", lambda p, s8=s8: p.tensor_reduce(c1[:], sfc[:, s8], AX.X, ALU.max))
            P.op("vector", lambda p, s8=s8: p.tensor_scalar(
                m8[:], sfc[:, s8], c1[:, 0:1], None, ALU.is_equal))
            P.op("vector", lambda p, s8=s8: p.scalar_tensor_tensor(
                m8[:], m8[:], -1e30, sfc[:, s8], ALU.mult, ALU.add))
            P.op("vector", lambda p: p.tensor_reduce(c2[:], m8[:], AX.X, ALU.max))
            P.op("vector", lambda p, g=g: p.tensor_tensor(
                gsc[:, g:g + 1], c1[:], c2[:], ALU.add))
        P.op("vector", lambda p: p.tensor_reduce(c1[:], gsc[:], AX.X, ALU.max))
        P.op("vector", lambda p: p.tensor_scalar(g2[:], gsc[:], c1[:, 0:1], None,
                                                 ALU.is_equal))
        P.op("vector", lambda p: p.scalar_tensor_tensor(g2[:], g2[:], -1e30, gsc[:],
                                                        ALU.mult, ALU.add))
        P.op("vector", lambda p: p.tensor_reduce(c2[:], g2[:], AX.X, ALU.max))
        for g in range(4):
            s8 = slice(8 * g, 8 * g + 8)
            P.op("vector", lambda p, g=g: p.tensor_scalar(
                c1[:], gsc[:, g:g + 1], c2[:, 0:1], None, ALU.is_ge))
            P.op("vector", lambda p, s8=s8: p.tensor_scalar(
                tmp[:, s8], sfc[:, s8], c1[:, 0:1], None, ALU.mult))
        P.op("vector", lambda p: p.tensor_copy(tc[:], tmp[:]))
        for _ in range(3):
            P.op("vector", lambda p: p.tensor_reduce(c1[:], tc[:], AX.X, ALU.max))
            P.op("vector", lambda p: p.tensor_scalar(msk[:], tc[:], c1[:, 0:1], None,
                                                     ALU.is_equal))
            P.op("vector", lambda p: p.scalar_tensor_tensor(
                tc[:], msk[:], -1e30, tc[:], ALU.mult, ALU.add))
        P.op("vector", lambda p: p.tensor_reduce(c1[:], tc[:], AX.X, ALU.max))
        P.op("vector", lambda p: p.tensor_scalar(msk[:], tmp[:], c1[:, 0:1], None,
                                                 ALU.is_ge))
        P.op("vector", lambda p: p.tensor_tensor(tmp[:], sc[:], msk[:], ALU.mult))
        P.op("vector", lambda p: p.tensor_reduce(c2[:], tmp[:], AX.X, ALU.add))
        P.op("vector", lambda p: p.tensor_scalar(c2[:], c2[:], 1e-20, None,
                                                 ALU.add))
        P.op("vector", lambda p: p.reciprocal(c1[:], c2[:]))
        P.op("vector", lambda p: p.tensor_scalar(c1[:], c1[:], 2.5, None, ALU.mult))
        gt = P.op("vector", lambda p: p.tensor_scalar(tmp[:], tmp[:], c1[:, 0:1], None,
                                                      ALU.mult))
        P.wait("tensor", gt)
        tr2 = P.op("tensor", lambda p: p.transpose(
            ps_t[0:32, 96:224], tmp[:], id_f[:]), inc="s_tensor")
        P.wait("vector", tr2)
        prev_wr = P.op("vector", lambda p, tsl=tsl: p.tensor_copy(
            routeT[0:32, tsl], ps_t[0:32, 96:224]))
        P.wait("tensor", prev_wr)
    les.close()


def emit_moe_l1(C, es):
    nc, P = C.nc, C.P
    xn, esel, routeT = C.xn, C.esel, C.routeT
    pool_es = ExitStack()
    pool = [C.psum(pool_es, [128, CH], F32, "l1ps") for _ in range(8)]

    sh_es = ExitStack()
    sgw = C.sbuf(sh_es, [128, CT * DM], BF16, "sgw")
    suw = C.sbuf(sh_es, [128, CT * DM], BF16, "suw")
    sdw = C.sbuf(sh_es, [128, CT * DM], BF16, "sdw")
    aSH = [C.sbuf(sh_es, [128, T], BF16, "aSH") for _ in range(CT)]
    sst = [C.sbuf(sh_es, [128, CH], BF16, "shst") for _ in range(2)]
    toks = []
    for nm, dst in (("l1_sgwt", sgw), ("l1_suwt", suw), ("l1_sdwt", sdw)):
        toks.append(P.dma("gpsimd", dst[:].rearrange("p (a o) -> p a o", o=DM),
                          C.ins[nm].rearrange("(a p) o -> p a o", p=128), "d_shw"))
    P.waits("tensor", toks)
    sgv = sgw[:].rearrange("p (a o) -> p a o", o=DM)
    suv = suw[:].rearrange("p (a o) -> p a o", o=DM)
    sdv = sdw[:].rearrange("p (a o) -> p a o", o=DM)

    ev_prev = [None, None]
    for ht in range(CT):
        for sub in range(NCH):
            ssl = slice(sub * CH, (sub + 1) * CH)
            pb = (ht * NCH + sub) % 2
            P.wait("tensor", ev_prev[pb])
            for c in range(CT):
                P.op("tensor", lambda p, c=c, ht=ht, ssl=ssl, pb=pb: p.matmul(
                    pool[pb][:], sgv[:, c, ht * 128:(ht + 1) * 128], xn[c][:, ssl],
                    start=(c == 0), stop=(c == CT - 1)))
            for c in range(CT):
                P.op("tensor", lambda p, c=c, ht=ht, ssl=ssl, pb=pb: p.matmul(
                    pool[4 + pb][:], suv[:, c, ht * 128:(ht + 1) * 128], xn[c][:, ssl],
                    start=(c == 0), stop=(c == CT - 1)),
                    inc="s_tensor" if c == CT - 1 else None)
            mm = P.cur("s_tensor")
            P.wait("scalar", mm)
            st = P.op("scalar", lambda p, pb=pb: p.activation(sst[pb][:], pool[pb][:],
                                                              AF.Silu))
            P.wait("vector", st)
            ev_prev[pb] = P.op("vector", lambda p, ht=ht, ssl=ssl, pb=pb: p.tensor_tensor(
                aSH[ht][:, ssl], sst[pb][:], pool[4 + pb][:], ALU.mult))
    a_done = P.cur("s_vector")
    for sub in range(NCH):
        ssl = slice(sub * CH, (sub + 1) * CH)
        for ht in range(CT):
            if ht == 0:
                P.wait("tensor", a_done)
                P.wait("tensor", P.cur("s_scalar"))
                P.wait("tensor", P.cur("s_vector"))
            for oi in range(CT):
                P.op("tensor", lambda p, ht=ht, oi=oi, ssl=ssl: p.matmul(
                    pool[oi][:], sdv[:, ht, oi * 128:(oi + 1) * 128], aSH[ht][:, ssl],
                    start=(ht == 0), stop=(ht == CT - 1)),
                    inc="s_tensor" if (ht == CT - 1 and oi == CT - 1) else None)
        bdone = P.cur("s_tensor")
        P.wait("vector", bdone)
        for oi in range(CT):
            P.op("vector", lambda p, oi=oi, ssl=ssl: p.tensor_tensor(
                C.resid[oi][:, ssl], pool[oi][:], C.resid[oi][:, ssl], ALU.add))
    P.wait("tensor", P.cur("s_vector"))
    sh_es.close()

    ex_es = ExitStack()
    NE = 32
    wg = [C.sbuf(ex_es, [128, CT * 512], BF16, "wg") for _ in range(2)]
    wu = [C.sbuf(ex_es, [128, CT * 512], BF16, "wu") for _ in range(2)]
    wd = [C.sbuf(ex_es, [128, 4 * DM], BF16, "wd") for _ in range(2)]
    aE = [C.sbuf(ex_es, [128, T], BF16, "aE1") for _ in range(4)]
    wBs = [C.sbuf(ex_es, [128, CH], BF16, "wBs") for _ in range(NCH)]
    gst = [C.sbuf(ex_es, [128, CH], BF16, "gst") for _ in range(2)]
    ust = [C.sbuf(ex_es, [128, CH], BF16, "ust") for _ in range(2)]

    w_tok = [None, None]
    w_use = [None, None]

    def load_we(e, buf):
        P.wait("gpsimd", w_use[buf])
        P.dma("gpsimd", wg[buf][:].rearrange("p (a o) -> p a o", o=512),
              C.ins["l1_gwt"][e].rearrange("(a p) o -> p a o", p=128), f"d_we{buf}")
        P.dma("gpsimd", wu[buf][:].rearrange("p (a o) -> p a o", o=512),
              C.ins["l1_uwt"][e].rearrange("(a p) o -> p a o", p=128), f"d_we{buf}")
        w_tok[buf] = P.dma("gpsimd", wd[buf][:].rearrange("p (b o) -> p b o", o=DM),
                           C.ins["l1_dwt"][e].rearrange("(b p) o -> p b o", p=128),
                           f"d_we{buf}")

    P.wait("gpsimd", P.cur("s_tensor"))
    P.wait("gpsimd", P.cur("s_vector"))
    P.wait("gpsimd", P.cur("s_scalar"))
    load_we(0, 0)
    wB_done = [None] * NCH
    ev_prev = {}
    for e in range(NE):
        buf = e % 2
        if e + 1 < NE:
            load_we(e + 1, (e + 1) % 2)
        last_copy = wB_done[NCH - 1]
        for ch in range(NCH):
            P.wait("tensor", last_copy)
            t = P.op("tensor", lambda p, e=e, ch=ch: p.matmul(
                pool[2][:], esel[0:32, e * 128:(e + 1) * 128],
                routeT[0:32, ch * CH:(ch + 1) * CH], start=True, stop=True),
                inc="s_tensor")
            P.wait("vector", t)
            last_copy = wB_done[ch] = P.op("vector", lambda p, ch=ch: p.tensor_copy(
                wBs[ch][:], pool[2][:]))
        wv_tok = P.cur("s_vector")
        P.wait("tensor", w_tok[buf])
        wgv = wg[buf][:].rearrange("p (a o) -> p a o", o=512)
        wuv = wu[buf][:].rearrange("p (a o) -> p a o", o=512)
        wdv = wd[buf][:].rearrange("p (b o) -> p b o", o=DM)
        for ht in range(4):
            for sub in range(NCH):
                ssl = slice(sub * CH, (sub + 1) * CH)
                pb = (ht * NCH + sub) % 2
                P.wait("tensor", ev_prev.get(("A", pb)))
                for c in range(CT):
                    P.op("tensor", lambda p, c=c, ht=ht, ssl=ssl, pb=pb, wgv=wgv: p.matmul(
                        pool[pb][:], wgv[:, c, ht * 128:(ht + 1) * 128], xn[c][:, ssl],
                        start=(c == 0), stop=(c == CT - 1)))
                for c in range(CT):
                    P.op("tensor", lambda p, c=c, ht=ht, ssl=ssl, pb=pb, wuv=wuv: p.matmul(
                        pool[4 + pb][:], wuv[:, c, ht * 128:(ht + 1) * 128], xn[c][:, ssl],
                        start=(c == 0), stop=(c == CT - 1)),
                        inc="s_tensor" if c == CT - 1 else None)
                mm = P.cur("s_tensor")
                P.wait("scalar", mm)
                st = P.op("scalar", lambda p, pb=pb: p.activation(
                    gst[pb][:], pool[pb][:], AF.Silu))
                P.wait("vector", st)
                P.wait("vector", wv_tok)
                P.op("vector", lambda p, pb=pb: p.tensor_tensor(
                    ust[pb][:], gst[pb][:], pool[4 + pb][:], ALU.mult))
                ev_prev[("A", pb)] = P.op(
                    "vector", lambda p, ht=ht, ssl=ssl, pb=pb, sub=sub: p.tensor_tensor(
                        aE[ht][:, ssl], ust[pb][:], wBs[sub][:], ALU.mult))
        a_done_v = P.cur("s_vector")
        for sub in range(NCH):
            ssl = slice(sub * CH, (sub + 1) * CH)
            for ht in range(4):
                if ht == 0:
                    P.wait("tensor", a_done_v)
                    P.wait("tensor", P.cur("s_scalar"))
                    P.wait("tensor", P.cur("s_vector"))
                for oi in range(CT):
                    P.op("tensor", lambda p, ht=ht, oi=oi, ssl=ssl, wdv=wdv: p.matmul(
                        pool[oi][:], wdv[:, ht, oi * 128:(oi + 1) * 128], aE[ht][:, ssl],
                        start=(ht == 0), stop=(ht == 3)),
                        inc="s_tensor" if (ht == 3 and oi == CT - 1) else None)
            bdone = P.cur("s_tensor")
            P.wait("vector", bdone)
            for oi in range(CT):
                P.op("vector", lambda p, oi=oi, ssl=ssl: p.tensor_tensor(
                    C.resid[oi][:, ssl], pool[oi][:], C.resid[oi][:, ssl], ALU.add))
            ev_prev[("B", sub)] = P.cur("s_vector")
        w_use[buf] = P.cur("s_tensor")
        P.wait("tensor", P.cur("s_vector"))
    ex_es.close()
    pool_es.close()



_ESEL = None


def _esel_host():
    global _ESEL
    if _ESEL is None:
        e = np.zeros((32, 32 * 128), dtype=np.float32)
        for k in range(32):
            e[k, k * 128:(k + 1) * 128] = 1.0
        _ESEL = e
    return _ESEL


GROUP_INPUTS = {
    "l0a": [("l0_wattn_t", [DM, 3072]), ("l0_wproj_t", [DM, DM]), ("l0_bproj_t", [128, CT])],
    "l0f": [("c_esel", [32, 32 * 128]), ("l0_gwt", [DM, 8]),
            ("l0_w1t", [9, DM, 4096]), ("l0_w2t", [9, 4096, DM])],
    "l1a": [("l1_bwt", [DM, 3072]), ("l1_swt", [8, DM, 3072]), ("l1_sct", [DM, 3072]),
            ("l1_grid", [DM, 12]), ("l1_rot_t", [64, 64]),
            ("l1_owt", [DM, DM]), ("l1_obt", [128, CT])],
    "l1f": [("c_esel", [32, 32 * 128]), ("l1_rwt", [DM, 32]), ("l1_rbias", [1, 32]),
            ("l1_gwt", [32, DM, 512]), ("l1_uwt", [32, DM, 512]), ("l1_dwt", [32, 512, DM]),
            ("l1_sgwt", [DM, DM]), ("l1_suwt", [DM, DM]), ("l1_sdwt", [DM, DM])],
}


def prep_host(inputs, groups):
    h = {"c_ident": np.eye(128, dtype=np.float32)}
    cat = np.concatenate
    if "l0a" in groups:
        h["l0_wattn_t"] = np.ascontiguousarray(inputs["L0_attn_W_attn"].T)
        h["l0_wproj_t"] = np.ascontiguousarray(inputs["L0_attn_W_proj"].T)
        h["l0_bproj_t"] = np.ascontiguousarray(inputs["L0_attn_b_proj"].reshape(CT, 128).T)
    if "l0f" in groups:
        h["c_esel"] = _esel_host()
        h["l0_gwt"] = np.ascontiguousarray(inputs["L0_ffn_gate_w"].T)
        w1 = cat([inputs["L0_ffn_shared_w1"][None], inputs["L0_ffn_experts_w1"]], 0)
        w2 = cat([inputs["L0_ffn_shared_w2"][None], inputs["L0_ffn_experts_w2"]], 0)
        h["l0_w1t"] = np.ascontiguousarray(w1.transpose(0, 2, 1))
        h["l0_w2t"] = np.ascontiguousarray(w2.transpose(0, 2, 1))
    if "l1a" in groups:
        h["l1_bwt"] = np.ascontiguousarray(inputs["L1_attn_base_weight"].T)
        h["l1_swt"] = np.ascontiguousarray(inputs["L1_attn_spline_weight"].transpose(2, 1, 0))
        h["l1_sct"] = np.ascontiguousarray(inputs["L1_attn_spline_scaler"].T)
        h["l1_grid"] = np.ascontiguousarray(inputs["L1_attn_grid"])
        h["l1_rot_t"] = np.ascontiguousarray(inputs["L1_attn_rotation_matrix"].T)
        h["l1_owt"] = np.ascontiguousarray(inputs["L1_attn_out_w"].T)
        h["l1_obt"] = np.ascontiguousarray(inputs["L1_attn_out_b"].reshape(CT, 128).T)
    if "l1f" in groups:
        h["c_esel"] = _esel_host()
        h["l1_rwt"] = np.ascontiguousarray(inputs["L1_ffn_router_weight"].T)
        h["l1_rbias"] = np.ascontiguousarray(inputs["L1_ffn_router_bias"].reshape(1, 32))
        h["l1_gwt"] = np.ascontiguousarray(inputs["L1_ffn_gate_w"].transpose(0, 2, 1))
        h["l1_uwt"] = np.ascontiguousarray(inputs["L1_ffn_up_w"].transpose(0, 2, 1))
        h["l1_dwt"] = np.ascontiguousarray(inputs["L1_ffn_down_w"].transpose(0, 2, 1))
        h["l1_sgwt"] = np.ascontiguousarray(inputs["L1_ffn_shared_gate_w"].T)
        h["l1_suwt"] = np.ascontiguousarray(inputs["L1_ffn_shared_up_w"].T)
        h["l1_sdwt"] = np.ascontiguousarray(inputs["L1_ffn_shared_down_w"].T)
    return h


def build_group(groups):
    nc = bass.Bass("TRN2", target_bir_lowering=False, debug=False)
    C = Ctx(nc)
    C.ins = {}
    seen = set()
    decls = [("resid_in", [DM, T]), ("c_ident", [128, 128])]
    for g in groups:
        decls += GROUP_INPUTS[g]
    for nm, shp in decls:
        if nm in seen:
            continue
        seen.add(nm)
        C.ins[nm] = nc.dram_tensor(nm, shp, F32, kind="ExternalInput").ap()
    out = nc.dram_tensor("resid_out", [DM, T], F32, kind="ExternalOutput").ap()
    if "l0a" in groups or "l1a" in groups:
        C.qkv_s = nc.dram_tensor("qkv_s", [3072, T], BF16, kind="Internal").ap()
    if "l1a" in groups:
        C.swb_s = nc.dram_tensor("swb_s", [8, DM, 3072], BF16, kind="Internal").ap()

    es = C.es
    emit_common_setup(C, es)
    C.P.barrier()
    for g in groups:
        if g == "l0a":
            xn_es = ExitStack()
            st_es = ExitStack()
            emit_ln(C, es, make_xn=True, xn_es=xn_es, stats_es=st_es)
            C.P.barrier()
            st_es.close()
            emit_dense_to_scratch(C, es, "l0_wattn_t", 3072, C.xn, C.qkv_s)
            xn_es.close()
            C.P.barrier()
            attn_es = ExitStack()
            emit_attention(C, es, layer=0, ctx_es=attn_es)
            C.P.barrier()
            emit_proj_residual(C, es, "l0_wproj_t", "l0_bproj_t")
            attn_es.close()
            C.P.barrier()
        elif g == "l0f":
            xn_es = ExitStack()
            st_es = ExitStack()
            emit_ln(C, es, make_xn=True, xn_es=xn_es, stats_es=st_es)
            C.P.barrier()
            st_es.close()
            route_es = ExitStack()
            emit_routing_l0(C, es, route_es)
            C.P.barrier()
            emit_moe_l0(C, es)
            route_es.close()
            xn_es.close()
            C.P.barrier()
        elif g == "l1a":
            st_es = ExitStack()
            emit_ln(C, es, make_xn=False, stats_es=st_es)
            C.P.barrier()
            emit_sw_spill(C, es)
            C.P.barrier()
            emit_spline_qkv(C, es)
            C.P.barrier()
            st_es.close()
            attn_es = ExitStack()
            emit_attention(C, es, layer=1, ctx_es=attn_es)
            C.P.barrier()
            emit_proj_residual(C, es, "l1_owt", "l1_obt")
            attn_es.close()
            C.P.barrier()
        elif g == "l1f":
            xn_es = ExitStack()
            st_es = ExitStack()
            emit_ln(C, es, make_xn=True, xn_es=xn_es, stats_es=st_es)
            C.P.barrier()
            st_es.close()
            route_es = ExitStack()
            emit_routing_l1(C, es, route_es)
            C.P.barrier()
            emit_moe_l1(C, es)
            route_es.close()
            xn_es.close()
            C.P.barrier()
    toks = [C.P.dma("sync", out[c * 128:(c + 1) * 128, :], C.resid[c][:], "d_out")
            for c in range(CT)]
    C.P.wait("sync", C.P.cur("d_out"))
    C.P.build()
    es.close()
    return nc


_BUILD_CACHE = {}


def _get_prog(groups):
    key = tuple(groups)
    if key not in _BUILD_CACHE:
        _BUILD_CACHE[key] = build_group(list(groups))
    return _BUILD_CACHE[key]


LAUNCH_PLAN = [["l0a", "l0f", "l1a", "l1f"]]


def kernel(**inputs):
    x = inputs["L0_attn_x"]
    resid = [np.ascontiguousarray(x[b].T) for b in range(NCORES)]
    for groups in LAUNCH_PLAN:
        h = prep_host(inputs, groups)
        nc = _get_prog(tuple(groups))
        in_maps = []
        for b in range(NCORES):
            m = dict(h)
            m["resid_in"] = resid[b]
            in_maps.append(m)
        res = run_bass_kernel_spmd(nc, in_maps, core_ids=list(range(NCORES)))
        resid = [r["resid_out"] for r in res.results]
    return np.stack([r.T for r in resid]).astype(np.float32)


# version 07
# speedup vs baseline: 1.0601x; 1.0543x over previous
import numpy as np
from contextlib import ExitStack

import concourse.bass as bass
import concourse.mybir as mybir
from concourse.bass_utils import run_bass_kernel_spmd

F32 = mybir.dt.float32
BF16 = mybir.dt.bfloat16
AF = mybir.ActivationFunctionType
ALU = mybir.AluOpType
AX = mybir.AxisListType

NCORES = 8
DM = 1024
CT = DM // 128
T = 2048
CH = 512
NCH = T // CH
NH = 16
HD = 64
EPS = 1e-5

ENGS = ("tensor", "scalar", "vector", "sync", "gpsimd")


class Prog:
    def __init__(self, nc):
        self.nc = nc
        self.ops = {e: [] for e in ENGS}
        self.sems = {}
        self.lastwait = {}

    def sem(self, name):
        if name not in self.sems:
            h = self.nc.alloc_semaphore(name=name)
            self.sems[name] = [h, 0]
        return name

    def wait(self, eng, tok):
        if tok is None:
            return
        sname, val = tok
        if val <= 0:
            return
        if self.lastwait.get((eng, sname), -1) >= val:
            return
        self.lastwait[(eng, sname)] = val
        self.ops[eng].append(("w", self.sems[sname][0], val, sname))

    def waits(self, eng, toks):
        best = {}
        for t in toks:
            if t is None:
                continue
            if t[0] not in best or best[t[0]] < t[1]:
                best[t[0]] = t[1]
        for s, v in best.items():
            self.wait(eng, (s, v))

    def op(self, eng, fn, inc=None, amt=1):
        if eng in ("vector", "scalar"):
            sname = "s_" + eng
            self.wait(eng, (sname, self.sems[sname][1]))
            inc = sname
        tok = None
        h = None
        if inc is not None:
            self.sem(inc)
            s = self.sems[inc]
            s[1] += amt
            tok = (inc, s[1])
            h = s[0]
        self.ops[eng].append(("c", fn, h, amt))
        return tok

    def dma(self, eng, out, in_, sem):
        self.sem(sem)
        s = self.sems[sem]
        s[1] += 16
        tok = (sem, s[1])
        self.ops[eng].append(("d", out, in_, s[0]))
        return tok

    def cur(self, semname):
        if semname not in self.sems:
            return None
        return (semname, self.sems[semname][1])

    def barrier(self):
        for e in ENGS:
            self.op(e, lambda p: p.nop(), inc="bar_" + e)
        toks = [self.cur("bar_" + e) for e in ENGS]
        toks += [self.cur(s) for s in list(self.sems) if not s.startswith("bar_")]
        for e in ENGS:
            for t in toks:
                self.wait(e, t)

    def check(self):
        vals = {name: 0 for name in self.sems}
        idx = {e: 0 for e in ENGS}
        h2n = {id(v[0]): k for k, v in self.sems.items()}
        progress = True
        while progress:
            progress = False
            for e in ENGS:
                while idx[e] < len(self.ops[e]):
                    it = self.ops[e][idx[e]]
                    if it[0] == "w":
                        if vals[it[3]] < it[2]:
                            break
                    elif it[0] == "d":
                        vals[h2n[id(it[3])]] += 16
                    else:
                        if it[2] is not None:
                            vals[h2n[id(it[2])]] += it[3]
                    idx[e] += 1
                    progress = True
        stuck = {e: (idx[e], len(self.ops[e])) for e in ENGS if idx[e] < len(self.ops[e])}
        if stuck:
            det = []
            for e, (i, n) in stuck.items():
                it = self.ops[e][i]
                det.append(f"{e}@{i}/{n} waiting {it[3]}>={it[2]} (now {vals[it[3]]})")
            raise RuntimeError("deadlock: " + "; ".join(det))

    def build(self):
        self.check()
        with self.nc.Block() as blk:
            for eng in ENGS:
                if not self.ops[eng]:
                    continue
                deco = getattr(blk, eng)

                def body(proxy, _e=eng):
                    for it in self.ops[_e]:
                        if it[0] == "w":
                            proxy.wait_ge(it[1], it[2])
                        elif it[0] == "d":
                            proxy.dma_start(it[1], it[2]).then_inc(it[3], 16)
                        else:
                            ins = it[1](proxy)
                            if it[2] is not None:
                                ins.then_inc(it[2], it[3])

                deco(body)


class Ctx:
    def __init__(self, nc):
        self.nc = nc
        self.P = Prog(nc)
        for e in ENGS:
            self.P.sem("s_" + e)
        self.es = ExitStack()
        self.uid = 0

    def name(self, base):
        self.uid += 1
        return f"{base}_{self.uid}"

    def sbuf(self, es, shape, dtype, base="t"):
        return es.enter_context(self.nc.sbuf_tensor(self.name(base), shape, dtype))

    def psum(self, es, shape, dtype, base="p"):
        return es.enter_context(self.nc.psum_tensor(self.name(base), shape, dtype))


def emit_common_setup(C, es):
    nc, P = C.nc, C.P
    C.resid = [C.sbuf(es, [128, T], F32, "resid") for _ in range(CT)]
    C.ones_f = C.sbuf(es, [128, 128], F32, "onesf")
    C.ones_b = C.sbuf(es, [128, 128], BF16, "onesb")
    C.id_b = C.sbuf(es, [128, 128], BF16, "idb")

    t2 = P.dma("gpsimd", C.id_b[:], C.ins["c_ident"], "d_misc2")
    P.op("vector", lambda p: p.memset(C.ones_f[:], 1.0))
    P.op("vector", lambda p: p.memset(C.ones_b[:], 1.0), inc="s_vector")
    P.wait("vector", t2)

    for c in range(CT):
        P.dma("sync", C.resid[c][:], C.ins["resid_in"][c * 128:(c + 1) * 128, :], "d_resid")


def emit_ln(C, es, make_xn=True, xn_es=None, stats_es=None):
    nc, P = C.nc, C.P
    if make_xn:
        C.xn = [C.sbuf(xn_es, [128, T], BF16, "xn") for _ in range(CT)]
    if stats_es is None:
        stats_es = es
    C.mrow = C.sbuf(stats_es, [1, T], F32, "mrow")
    C.rrow = C.sbuf(stats_es, [1, T], F32, "rrow")
    xn = C.xn if make_xn else None
    mrow, rrow = C.mrow, C.rrow
    les = ExitStack()
    sq = [C.sbuf(les, [128, CH], F32, "lnsq") for _ in range(2)]
    ps_s = C.psum(les, [128, CH], F32, "lnpsS")
    ps_q = C.psum(les, [128, CH], F32, "lnpsQ")
    ps_m = C.psum(les, [128, CH], F32, "lnpsM")
    ps_r = C.psum(les, [128, CH], F32, "lnpsR")
    trow = C.sbuf(les, [1, CH], F32, "lnt1")
    trow2 = C.sbuf(les, [1, CH], F32, "lnt2")

    sq_mm = [None, None]
    prev_vec = None
    prev_sqrt = None
    prev_recip = None
    for ch in range(NCH):
        sl = slice(ch * CH, (ch + 1) * CH)
        for c in range(CT):
            P.wait("scalar", sq_mm[c % 2])
            sq_tok = P.op("scalar", lambda p, c=c, sl=sl: p.activation(
                sq[c % 2][:], C.resid[c][:, sl], AF.Square), inc="s_scalar")
            if c == 0:
                P.wait("tensor", prev_vec)
            P.op("tensor", lambda p, c=c, sl=sl: p.matmul(
                ps_s[0:1, :], C.ones_f[:, 0:1], C.resid[c][:, sl],
                start=(c == 0), stop=(c == CT - 1)))
            P.wait("tensor", sq_tok)
            sq_mm[c % 2] = P.op("tensor", lambda p, c=c: p.matmul(
                ps_q[0:1, :], C.ones_f[:, 0:1], sq[c % 2][:],
                start=(c == 0), stop=(c == CT - 1)), inc="s_tensor")
        mm_done = P.cur("s_tensor")
        P.wait("vector", mm_done)
        P.wait("vector", prev_sqrt)
        P.op("vector", lambda p, sl=sl: p.tensor_scalar(
            mrow[:, sl], ps_s[0:1, :], 1.0 / DM, None, ALU.mult))
        P.op("vector", lambda p: p.tensor_scalar(
            trow[:], ps_q[0:1, :], 1.0 / DM, None, ALU.mult), inc="s_vector")
        mtok = P.cur("s_vector")
        P.wait("scalar", mtok)
        P.wait("scalar", prev_recip)
        sqm = P.op("scalar", lambda p, sl=sl: p.activation(
            trow2[:], mrow[0:1, sl], AF.Square), inc="s_scalar")
        P.wait("vector", sqm)
        P.op("vector", lambda p: p.tensor_tensor(
            trow[:], trow[:], trow2[:], ALU.subtract), inc="s_vector")
        P.wait("scalar", P.cur("s_vector"))
        prev_sqrt = P.op("scalar", lambda p: p.activation(
            trow2[:], trow[0:1, :], AF.Sqrt, bias=np.float32(EPS)), inc="s_scalar")
        P.wait("vector", prev_sqrt)
        prev_recip = P.op("vector", lambda p, sl=sl: p.reciprocal(
            rrow[:, sl], trow2[:]), inc="s_vector")

        if make_xn:
            P.wait("tensor", prev_recip)
            P.op("tensor", lambda p, sl=sl: p.matmul(
                ps_m[:], C.ones_f[0:1, :], mrow[0:1, sl], start=True, stop=True))
            bt = P.op("tensor", lambda p, sl=sl: p.matmul(
                ps_r[:], C.ones_f[0:1, :], rrow[0:1, sl], start=True, stop=True),
                inc="s_tensor")
            P.wait("vector", bt)
            for c in range(CT):
                P.op("vector", lambda p, c=c, sl=sl: p.tensor_tensor(
                    sq[c % 2][:], C.resid[c][:, sl], ps_m[:], ALU.subtract))
                P.op("vector", lambda p, c=c, sl=sl: p.tensor_tensor(
                    xn[c][:, sl], sq[c % 2][:], ps_r[:], ALU.mult), inc="s_vector")
            sq_mm[0] = sq_mm[1] = P.cur("s_vector")
        prev_vec = P.cur("s_vector")
    les.close()


def emit_dense_to_scratch(C, es, w_name, nout, src_tiles, dst, dst_row0=0):
    nc, P = C.nc, C.P
    les = ExitStack()
    no = nout // 128
    wfull = C.sbuf(les, [128, CT * nout], BF16, "dpw")
    stage = [C.sbuf(les, [128, T], BF16, "dpst") for _ in range(2)]
    pss = [C.psum(les, [128, CH], F32, "dpps") for _ in range(2)]
    w = C.ins[w_name].rearrange("(a p) o -> p a o", p=128)
    wdst = wfull[:].rearrange("p (a o) -> p a o", o=nout)
    wtoks = [P.dma("gpsimd", wdst[:, a:a + 1, :], w[:, a:a + 1, :], f"d_dw{a % 4}")
             for a in range(CT)]
    P.waits("tensor", wtoks)

    ev_tok = [None, None]
    last_store = [None, None]
    for o in range(no):
        b = o % 2
        for ch in range(NCH):
            pb = ch % 2
            P.wait("tensor", ev_tok[pb])
            for c in range(CT):
                P.op("tensor", lambda p, c=c, ch=ch, pb=pb, o=o: p.matmul(
                    pss[pb][:], wdst[:, c, o * 128:(o + 1) * 128],
                    src_tiles[c][:, ch * CH:(ch + 1) * CH],
                    start=(c == 0), stop=(c == CT - 1)),
                    inc="s_tensor" if c == CT - 1 else None)
            P.wait("scalar", P.cur("s_tensor"))
            P.wait("scalar", last_store[b])
            ev_tok[pb] = P.op("scalar", lambda p, b=b, ch=ch, pb=pb: p.activation(
                stage[b][:, ch * CH:(ch + 1) * CH], pss[pb][:], AF.Copy), inc="s_scalar")
        P.wait("sync", ev_tok[0])
        P.wait("sync", ev_tok[1])
        r0 = dst_row0 + o * 128
        last_store[b] = P.dma("sync", dst[r0:r0 + 128, :], stage[b][:], f"d_ds{b}")
    P.waits("sync", [t for t in last_store if t is not None])
    les.close()


def emit_attention(C, es, layer, ctx_es):
    nc, P = C.nc, C.P
    les = ExitStack()
    C.ctx = [C.sbuf(ctx_es, [128, T], BF16, "ctx") for _ in range(CT)]
    ctx = C.ctx

    qa = [C.sbuf(les, [128, T], BF16, "qa") for _ in range(CT)]
    kaug = [C.sbuf(les, [128, T], BF16, "kaug") for _ in range(2)]
    vfeat = [C.sbuf(les, [64, T], BF16, "vfeat") for _ in range(2)]
    vtok = [[C.sbuf(les, [128, 80], BF16, "vtok") for _ in range(16)] for _ in range(2)]
    sqk = C.sbuf(les, [64, T], BF16, "sqk")
    mxc = C.sbuf(les, [1, NCH], F32, "mxc")
    mx2 = C.sbuf(les, [1, 1], F32, "mx2")
    nrow = C.sbuf(les, [1, CH], F32, "nrow")
    nrow2 = C.sbuf(les, [1, CH], F32, "nrow2")
    qaug = [C.sbuf(les, [128, CH], BF16, "qaug") for _ in range(2)]
    sq2 = C.sbuf(les, [64, CH], BF16, "sq2")
    pt = [C.sbuf(les, [128, CH], BF16, "pt") for _ in range(2)]
    rb = C.sbuf(les, [128, CH], F32, "rb")

    ps_s = [C.psum(les, [128, CH], F32, "psS") for _ in range(2)]
    ps_c = C.psum(les, [128, CH], F32, "psC")
    ps_b = C.psum(les, [128, CH], F32, "psB")
    ps_n = C.psum(les, [128, CH], F32, "psN")
    ps_tr = C.psum(les, [128, 512], BF16, "psTr")

    if layer == 1:
        rt = C.sbuf(les, [64, 64], BF16, "rotT")
        rtok = P.dma("gpsimd", rt[:], C.ins["l1_rot_t"], "d_rot")
        P.wait("tensor", rtok)
        kr_ps = [C.psum(les, [128, CH], F32, "psKr") for _ in range(2)]

    def rows_q(h):
        return h * 64 if layer == 0 else h * 192

    def rows_k(h):
        return 1024 + h * 64 if layer == 0 else h * 192 + 64

    def rows_v(h):
        return 2048 + h * 64 if layer == 0 else h * 192 + 128

    for c in range(CT):
        h0, h1 = 2 * c, 2 * c + 1
        P.dma("sync", qa[c][0:64, :], C.qkv_s[rows_q(h0):rows_q(h0) + 64, :], "d_qa")
        P.dma("sync", qa[c][64:128, :], C.qkv_s[rows_q(h1):rows_q(h1) + 64, :], "d_qa")
    qa_tok = P.cur("d_qa")

    for par in range(2):
        for kt in range(16):
            P.op("vector", lambda p, par=par, kt=kt: p.memset(vtok[par][kt][:, 64:65], 1.0))
    P.op("vector", lambda p: p.memset(kaug[0][64:65, :], 1.0))
    P.op("vector", lambda p: p.memset(kaug[1][64:65, :], 1.0), inc="s_vector")
    ones_done = P.cur("s_vector")

    k_load = [None, None]
    v_load = [None, None]
    k_free = [None, None]
    v_free = [None, None]
    vt_free = [None, None]

    k_load[0] = P.dma("sync", kaug[0][0:64, :], C.qkv_s[rows_k(0):rows_k(0) + 64, :], "d_k0")
    v_load[0] = P.dma("sync", vfeat[0][:], C.qkv_s[rows_v(0):rows_v(0) + 64, :], "d_v0")

    for h in range(NH):
        par = h % 2
        if h + 1 < NH:
            np_ = (h + 1) % 2
            P.wait("sync", k_free[np_])
            k_load[np_] = P.dma("sync", kaug[np_][0:64, :],
                                C.qkv_s[rows_k(h + 1):rows_k(h + 1) + 64, :], f"d_k{np_}")
            P.wait("sync", v_free[np_])
            v_load[np_] = P.dma("sync", vfeat[np_][:],
                                C.qkv_s[rows_v(h + 1):rows_v(h + 1) + 64, :], f"d_v{np_}")

        P.wait("tensor", k_load[par])
        P.wait("scalar", k_load[par])
        if layer == 1:
            kr_ev = [None, None]
            for ch in range(NCH):
                sl = slice(ch * CH, (ch + 1) * CH)
                P.wait("tensor", kr_ev[ch % 2])
                t = P.op("tensor", lambda p, par=par, sl=sl, ch=ch: p.matmul(
                    kr_ps[ch % 2][0:64, :], rt[:], kaug[par][0:64, sl],
                    start=True, stop=True), inc="s_tensor")
                P.wait("scalar", t)
                kr_ev[ch % 2] = P.op("scalar", lambda p, par=par, sl=sl, ch=ch: p.activation(
                    kaug[par][0:64, sl], kr_ps[ch % 2][0:64, :], AF.Copy), inc="s_scalar")
            P.wait("scalar", P.cur("s_scalar"))
        sct = P.op("scalar", lambda p, par=par: p.activation(
            sqk[:], kaug[par][0:64, :], AF.Square), inc="s_scalar")
        P.wait("tensor", sct)
        mx_prev = None
        for ch in range(NCH):
            P.wait("tensor", mx_prev)
            t = P.op("tensor", lambda p, ch=ch: p.matmul(
                ps_n[0:1, :], C.ones_b[0:64, 0:1], sqk[:, ch * CH:(ch + 1) * CH],
                start=True, stop=True), inc="s_tensor")
            P.wait("vector", t)
            mx_prev = P.op("vector", lambda p, ch=ch: p.tensor_reduce(
                mxc[0:1, ch:ch + 1], ps_n[0:1, :], AX.X, ALU.max), inc="s_vector")
        mxtok = P.op("vector", lambda p: p.tensor_reduce(
            mx2[:], mxc[0:1, :], AX.X, ALU.max), inc="s_vector")

        P.wait("tensor", v_load[par])
        P.wait("vector", ones_done)
        P.wait("vector", vt_free[par])
        for kt in range(16):
            slot = kt % 4
            t = P.op("tensor", lambda p, par=par, kt=kt, slot=slot: p.transpose(
                ps_tr[:, slot * 128:slot * 128 + 64],
                vfeat[par][:, kt * 128:(kt + 1) * 128], C.id_b[0:64, 0:64]),
                inc="s_tensor")
            P.wait("vector", t)
            P.op("vector", lambda p, par=par, kt=kt, slot=slot: p.tensor_copy(
                vtok[par][kt][:, 0:64], ps_tr[:, slot * 128:slot * 128 + 64]),
                inc="s_vector")
            if kt >= 3:
                P.wait("tensor", P.cur("s_vector"))
        v_free[par] = P.cur("s_tensor")
        vt_ready = P.cur("s_vector")

        for ch in range(NCH):
            sl = slice(ch * CH, (ch + 1) * CH)
            qb = ch % 2
            P.wait("vector", mxtok)
            P.wait("vector", qa_tok)
            cp = P.op("vector", lambda p, h=h, sl=sl, qb=qb: p.tensor_copy(
                qaug[qb][0:64, :], qa[h // 2][(h % 2) * 64:(h % 2) * 64 + 64, sl]),
                inc="s_vector")
            if layer == 1:
                P.wait("tensor", cp)
                t = P.op("tensor", lambda p, qb=qb: p.matmul(
                    ps_b[0:64, :], rt[:], qaug[qb][0:64, :], start=True, stop=True),
                    inc="s_tensor")
                P.wait("scalar", t)
                qrot = P.op("scalar", lambda p, qb=qb: p.activation(
                    qaug[qb][0:64, :], ps_b[0:64, :], AF.Copy), inc="s_scalar")
            else:
                qrot = cp
            P.wait("scalar", qrot)
            sct = P.op("scalar", lambda p, qb=qb: p.activation(
                sq2[:], qaug[qb][0:64, :], AF.Square), inc="s_scalar")
            P.wait("tensor", sct)
            qn = P.op("tensor", lambda p: p.matmul(
                ps_n[0:1, :], C.ones_b[0:64, 0:1], sq2[:], start=True, stop=True),
                inc="s_tensor")
            P.wait("vector", qn)
            P.op("vector", lambda p: p.tensor_scalar(
                nrow[:], ps_n[0:1, :], mx2[0:1, 0:1], None, ALU.mult), inc="s_vector")
            P.wait("scalar", P.cur("s_vector"))
            st = P.op("scalar", lambda p: p.activation(
                nrow2[:], nrow[0:1, :], AF.Sqrt), inc="s_scalar")
            P.wait("vector", st)
            rowtok = P.op("vector", lambda p, qb=qb: p.tensor_scalar(
                qaug[qb][64:65, :], nrow2[0:1, :], -1.0, None, ALU.mult), inc="s_vector")
            P.wait("tensor", rowtok)
            P.wait("tensor", vt_ready)

            smm = [None] * 16
            exp_toks = [None, None]

            def emit_s(kt):
                sb = kt % 2
                P.wait("tensor", exp_toks[sb])
                smm[kt] = P.op("tensor", lambda p, par=par, kt=kt, sb=sb, qb=qb: p.matmul(
                    ps_s[sb][:], kaug[par][0:65, kt * 128:(kt + 1) * 128],
                    qaug[qb][0:65, :], start=True, stop=True), inc="s_tensor")

            emit_s(0)
            for kt in range(16):
                if kt + 1 < 16:
                    emit_s(kt + 1)
                sb = kt % 2
                P.wait("scalar", smm[kt])
                exp_toks[sb] = P.op("scalar", lambda p, sb=sb: p.activation(
                    pt[sb][:], ps_s[sb][:], AF.Exp, scale=0.125), inc="s_scalar")
                P.wait("tensor", exp_toks[sb])
                P.op("tensor", lambda p, par=par, kt=kt, sb=sb: p.matmul(
                    ps_c[0:65, :], vtok[par][kt][:, 0:65], pt[sb][:],
                    start=(kt == 0), stop=(kt == 15)),
                    inc="s_tensor" if kt == 15 else None)
            ctx_done = P.cur("s_tensor")
            P.wait("scalar", ctx_done)
            P.op("scalar", lambda p: p.activation(nrow[:], ps_c[64:65, :], AF.Ln))
            rtok = P.op("scalar", lambda p: p.activation(
                nrow2[:], nrow[0:1, :], AF.Exp, scale=-1.0), inc="s_scalar")
            P.wait("tensor", rtok)
            bc = P.op("tensor", lambda p: p.matmul(
                ps_b[:], C.ones_f[0:1, :], nrow2[0:1, :], start=True, stop=True),
                inc="s_tensor")
            P.wait("vector", bc)
            P.op("vector", lambda p: p.tensor_copy(rb[:], ps_b[:]), inc="s_vector")
            P.op("vector", lambda p, h=h, sl=sl: p.tensor_tensor(
                ctx[h // 2][(h % 2) * 64:(h % 2) * 64 + 64, sl], ps_c[0:64, :],
                rb[0:64, :], ALU.mult), inc="s_vector")
            P.wait("tensor", P.cur("s_vector"))

        k_free[par] = P.cur("s_tensor")
        vt_free[par] = P.cur("s_tensor")
    les.close()


def emit_proj_residual(C, es, w_name, b_name):
    nc, P = C.nc, C.P
    ctx = C.ctx
    les = ExitStack()
    wfull = C.sbuf(les, [128, CT * DM], BF16, "pjw")
    bb = C.sbuf(les, [128, CT], F32, "pjb")
    pss = [C.psum(les, [128, CH], F32, "pjps") for _ in range(2)]
    w = C.ins[w_name].rearrange("(a p) o -> p a o", p=128)
    wdst = wfull[:].rearrange("p (a o) -> p a o", o=DM)
    wtoks = [P.dma("gpsimd", wdst[:, a:a + 1, :], w[:, a:a + 1, :], f"d_pw{a % 4}")
             for a in range(CT)]
    btok = P.dma("sync", bb[:], C.ins[b_name], "d_pjb")
    P.waits("tensor", wtoks)
    P.wait("vector", btok)

    ev_tok = [None, None]
    for o in range(CT):
        for ch in range(NCH):
            pb = ch % 2
            P.wait("tensor", ev_tok[pb])
            for c in range(CT):
                P.op("tensor", lambda p, c=c, ch=ch, pb=pb, o=o: p.matmul(
                    pss[pb][:], wdst[:, c, o * 128:(o + 1) * 128],
                    ctx[c][:, ch * CH:(ch + 1) * CH],
                    start=(c == 0), stop=(c == CT - 1)),
                    inc="s_tensor" if c == CT - 1 else None)
            P.wait("vector", P.cur("s_tensor"))
            ev_tok[pb] = P.op("vector", lambda p, o=o, ch=ch, pb=pb: p.scalar_tensor_tensor(
                C.resid[o][:, ch * CH:(ch + 1) * CH], pss[pb][:], bb[:, o:o + 1],
                C.resid[o][:, ch * CH:(ch + 1) * CH], ALU.add, ALU.add), inc="s_vector")
    P.wait("tensor", P.cur("s_vector"))
    les.close()


def emit_routing_l0(C, es, route_es):
    nc, P = C.nc, C.P
    les = ExitStack()
    C.gatesT = C.sbuf(route_es, [8, T], BF16, "gatesT")
    C.esel = C.sbuf(route_es, [32, 32 * 128], BF16, "esel")
    C.id_f = C.sbuf(route_es, [128, 128], F32, "idf")
    gatesT, id_f, xn = C.gatesT, C.id_f, C.xn
    P.dma("gpsimd", C.esel[:], C.ins["c_esel"], "d_esel_g")
    et = P.dma("sync", C.id_f[:], C.ins["c_ident"], "d_esel")
    P.wait("tensor", et)
    gw = C.sbuf(les, [128, CT * 8], BF16, "gw")
    lgT = C.sbuf(les, [8, T], F32, "lgT")
    lg_tok = C.sbuf(les, [128, 8], F32, "lgtok")
    w1 = C.sbuf(les, [128, 1], F32, "ro1")
    w2 = C.sbuf(les, [128, 1], F32, "ro2")
    msk = C.sbuf(les, [128, 8], F32, "romask")
    e1 = C.sbuf(les, [128, 8], F32, "roe1")
    ps_l = C.psum(les, [128, CH], F32, "rops")
    ps_t = C.psum(les, [128, 512], F32, "rops2")

    wv = C.ins["l0_gwt"].rearrange("(a p) e -> p a e", p=128)
    gwv = gw[:].rearrange("p (a e) -> p a e", e=8)
    wtok = P.dma("gpsimd", gwv[:], wv[:], "d_rw")
    P.wait("tensor", wtok)
    prev_ev = None
    for ch in range(NCH):
        sl = slice(ch * CH, (ch + 1) * CH)
        P.wait("tensor", prev_ev)
        for c in range(CT):
            P.op("tensor", lambda p, c=c, sl=sl: p.matmul(
                ps_l[0:8, :], gwv[:, c, :], xn[c][:, sl],
                start=(c == 0), stop=(c == CT - 1)),
                inc="s_tensor" if c == CT - 1 else None)
        P.wait("vector", P.cur("s_tensor"))
        prev_ev = P.op("vector", lambda p, sl=sl: p.tensor_copy(lgT[:, sl], ps_l[0:8, :]),
                       inc="s_vector")
    prev_wr = None
    prev_tr = None
    for tt in range(T // 128):
        tsl = slice(tt * 128, (tt + 1) * 128)
        P.wait("tensor", prev_wr)
        tr = P.op("tensor", lambda p, tsl=tsl: p.transpose(
            ps_t[:, 0:8], lgT[0:8, tsl], id_f[0:8, 0:8]), inc="s_tensor")
        P.wait("vector", tr)
        P.op("vector", lambda p: p.tensor_copy(lg_tok[:], ps_t[:, 0:8]), inc="s_vector")
        P.op("vector", lambda p: p.tensor_reduce(w1[:], lg_tok[:], AX.X, ALU.max))
        P.op("vector", lambda p: p.tensor_scalar(msk[:], lg_tok[:], w1[:, 0:1], None,
                                                 ALU.is_equal))
        P.op("vector", lambda p: p.scalar_tensor_tensor(
            msk[:], msk[:], -1e30, lg_tok[:], ALU.mult, ALU.add))
        P.op("vector", lambda p: p.tensor_reduce(w2[:], msk[:], AX.X, ALU.max))
        P.op("vector", lambda p: p.tensor_scalar(w1[:], w1[:], -1.0, None, ALU.mult),
             inc="s_vector")
        vt = P.cur("s_vector")
        P.wait("scalar", vt)
        ex = P.op("scalar", lambda p: p.activation(e1[:], lg_tok[:], AF.Exp, bias=w1[:, 0:1]),
                  inc="s_scalar")
        P.wait("vector", ex)
        P.op("vector", lambda p: p.tensor_scalar(msk[:], lg_tok[:], w2[:, 0:1], None,
                                                 ALU.is_ge))
        P.op("vector", lambda p: p.tensor_tensor(e1[:], e1[:], msk[:], ALU.mult))
        P.op("vector", lambda p: p.tensor_reduce(w2[:], e1[:], AX.X, ALU.add))
        P.op("vector", lambda p: p.reciprocal(w1[:], w2[:]))
        P.op("vector", lambda p: p.tensor_scalar(e1[:], e1[:], w1[:, 0:1], None, ALU.mult),
             inc="s_vector")
        gt = P.cur("s_vector")
        P.wait("tensor", gt)
        prev_tr = P.op("tensor", lambda p: p.transpose(
            ps_t[0:8, 8:136], e1[:], id_f[:]), inc="s_tensor")
        P.wait("vector", prev_tr)
        prev_wr = P.op("vector", lambda p, tsl=tsl: p.tensor_copy(
            gatesT[0:8, tsl], ps_t[0:8, 8:136]), inc="s_vector")
        P.wait("tensor", prev_wr)
    les.close()


def emit_moe_l0(C, es):
    nc, P = C.nc, C.P
    esel, gatesT, xn = C.esel, C.gatesT, C.xn
    les = ExitStack()
    NE = 9
    NHT = 32
    HALF = T // 2
    w1b = [C.sbuf(les, [128, CT * 256], BF16, "w1b") for _ in range(2)]
    w2b = [C.sbuf(les, [128, 2 * 512], BF16, "w2b") for _ in range(2)]
    aE = [C.sbuf(les, [128, HALF], BF16, "aE") for _ in range(NHT)]
    gBs = [C.sbuf(les, [128, CH], BF16, "gBs") for _ in range(2)]
    sst = [C.sbuf(les, [128, CH], BF16, "sst") for _ in range(2)]
    pool = [C.psum(les, [128, CH], F32, "fps") for _ in range(8)]

    w1v = C.ins["l0_w1t"]
    w2v = C.ins["l0_w2t"]

    w1_tok = [None, None]
    w1_use = [None, None]
    w2_tok = [None, None]
    w2_use = [None, None]
    gB_done = [None, None]

    def load_w1(e, ht2, buf):
        P.wait("gpsimd", w1_use[buf])
        src = C.ins["l0_w1t"][e].rearrange("(a p) o -> p a o", p=128)[
            :, :, ht2 * 256:(ht2 + 1) * 256]
        dst = w1b[buf][:].rearrange("p (a o) -> p a o", o=256)
        w1_tok[buf] = P.dma("gpsimd", dst[:], src, f"d_w1{buf}")

    def load_w2(e, oh, ht2, buf):
        P.wait("gpsimd", w2_use[buf])
        src = C.ins["l0_w2t"][e][ht2 * 256:(ht2 + 1) * 256,
                                 oh * 512:(oh + 1) * 512].rearrange("(b p) o -> p b o", p=128)
        dst = w2b[buf][:].rearrange("p (b o) -> p b o", o=512)
        w2_tok[buf] = P.dma("gpsimd", dst[:], src, f"d_w2{buf}")

    a_ready = None
    ev_prev = {}
    for e in range(NE):
        for hf in range(2):
            chs = (2 * hf, 2 * hf + 1)
            if e > 0:
                for i, ch in enumerate(chs):
                    P.wait("tensor", gB_done[i])
                    t = P.op("tensor", lambda p, e=e, ch=ch, i=i: p.matmul(
                        pool[2 + i][:], esel[0:8, (e - 1) * 128:e * 128],
                        gatesT[0:8, ch * CH:(ch + 1) * CH], start=True, stop=True),
                        inc="s_tensor")
                    P.wait("vector", t)
                    gB_done[i] = P.op("vector", lambda p, i=i: p.tensor_copy(
                        gBs[i][:], pool[2 + i][:]), inc="s_vector")
                P.wait("tensor", gB_done[0])
                P.wait("tensor", gB_done[1])

            load_w1(e, 0, 0)
            for ht2 in range(16):
                buf = ht2 % 2
                if ht2 + 1 < 16:
                    load_w1(e, ht2 + 1, (ht2 + 1) % 2)
                P.wait("tensor", w1_tok[buf])
                wv = w1b[buf][:].rearrange("p (a o) -> p a o", o=256)
                for k in range(2):
                    ht = ht2 * 2 + k
                    for sub in range(2):
                        ch = chs[sub]
                        pb = (ht * 2 + sub) % 2
                        P.wait("tensor", ev_prev.get(("A", pb)))
                        for c in range(CT):
                            P.op("tensor", lambda p, c=c, k=k, ch=ch, pb=pb, wv=wv: p.matmul(
                                pool[pb][:], wv[:, c, k * 128:(k + 1) * 128],
                                xn[c][:, ch * CH:(ch + 1) * CH],
                                start=(c == 0), stop=(c == CT - 1)),
                                inc="s_tensor" if c == CT - 1 else None)
                        mm = P.cur("s_tensor")
                        P.wait("scalar", mm)
                        if e == 0:
                            ev_prev[("A", pb)] = P.op(
                                "scalar", lambda p, ht=ht, sub=sub, pb=pb: p.activation(
                                    aE[ht][:, sub * CH:(sub + 1) * CH], pool[pb][:], AF.Silu),
                                inc="s_scalar")
                        else:
                            st = P.op("scalar", lambda p, sub=sub, pb=pb: p.activation(
                                sst[pb][:], pool[pb][:], AF.Silu), inc="s_scalar")
                            P.wait("vector", st)
                            ev_prev[("A", pb)] = P.op(
                                "vector", lambda p, ht=ht, sub=sub, pb=pb: p.tensor_tensor(
                                    aE[ht][:, sub * CH:(sub + 1) * CH], sst[pb][:],
                                    gBs[sub][:], ALU.mult), inc="s_vector")
                w1_use[buf] = P.cur("s_tensor")
            a_done_tok = {"scalar": P.cur("s_scalar"), "vector": P.cur("s_vector")}

            for oh in range(2):
                load_w2(e, oh, 0, 0)
                for ht2 in range(16):
                    buf = ht2 % 2
                    if ht2 + 1 < 16:
                        load_w2(e, oh, ht2 + 1, (ht2 + 1) % 2)
                    P.wait("tensor", w2_tok[buf])
                    if ht2 == 0:
                        P.wait("tensor", a_done_tok["scalar"])
                        P.wait("tensor", a_done_tok["vector"])
                    w2vv = w2b[buf][:].rearrange("p (b o) -> p b o", o=512)
                    for k in range(2):
                        ht = ht2 * 2 + k
                        for sub in range(2):
                            for oi in range(4):
                                pb = sub * 4 + oi
                                if ht == 0:
                                    P.wait("tensor", ev_prev.get(("B", oh, pb)))
                                P.op("tensor", lambda p, k=k, oi=oi, sub=sub, pb=pb, ht=ht,
                                     w2vv=w2vv: p.matmul(
                                         pool[pb][:], w2vv[:, k, oi * 128:(oi + 1) * 128],
                                         aE[ht][:, sub * CH:(sub + 1) * CH],
                                         start=(ht == 0), stop=(ht == NHT - 1)),
                                     inc="s_tensor" if (k == 1 and sub == 1 and oi == 3) else None)
                    w2_use[buf] = P.cur("s_tensor")
                bdone = P.cur("s_tensor")
                P.wait("vector", bdone)
                for sub in range(2):
                    ch = chs[sub]
                    for oi in range(4):
                        pb = sub * 4 + oi
                        ot = oh * 4 + oi
                        ev_prev[("B", oh, pb)] = P.op(
                            "vector", lambda p, ot=ot, ch=ch, pb=pb: p.tensor_tensor(
                                C.resid[ot][:, ch * CH:(ch + 1) * CH], pool[pb][:],
                                C.resid[ot][:, ch * CH:(ch + 1) * CH], ALU.add),
                            inc="s_vector")
                P.wait("tensor", P.cur("s_vector"))
    les.close()


def emit_sw_spill(C, es):
    nc, P = C.nc, C.P
    les = ExitStack()
    swf = [C.sbuf(les, [128, 3072], F32, "swf") for _ in range(2)]
    scf = [C.sbuf(les, [128, 3072], F32, "scf") for _ in range(2)]
    ob = [C.sbuf(les, [128, 3072], BF16, "swob") for _ in range(2)]
    sc_users = [None, None]
    sc_tok = [None, None]
    st_tok = [None, None]
    sw_tok = [None, None]
    sc_tok[0] = P.dma("gpsimd", scf[0][:], C.ins["l1_sct"][0:128, :], "d_sc")
    for c in range(CT):
        cb = c % 2
        if c + 1 < CT:
            nb = (c + 1) % 2
            P.wait("gpsimd", sc_users[nb])
            sc_tok[nb] = P.dma("gpsimd", scf[nb][:],
                               C.ins["l1_sct"][(c + 1) * 128:(c + 2) * 128, :], "d_sc")
        for j in range(8):
            buf = j % 2
            P.wait("gpsimd", sw_tok[buf])
            sw_load = P.dma("gpsimd", swf[buf][:],
                            C.ins["l1_swt"][j, c * 128:(c + 1) * 128, :], f"d_sw{buf}")
            P.wait("vector", sw_load)
            P.wait("vector", sc_tok[cb])
            P.wait("vector", st_tok[buf])
            tt = P.op("vector", lambda p, buf=buf, cb=cb: p.tensor_tensor(
                ob[buf][:], swf[buf][:], scf[cb][:], ALU.mult), inc="s_vector")
            sw_tok[buf] = tt
            P.wait("sync", tt)
            st_tok[buf] = P.dma("sync", C.swb_s[j, c * 128:(c + 1) * 128, :], ob[buf][:],
                                f"d_sb{buf}")
        sc_users[cb] = P.cur("s_vector")
    P.waits("sync", [st_tok[0], st_tok[1]])
    les.close()


def emit_spline_qkv(C, es):
    nc, P = C.nc, C.P
    mrow, rrow = C.mrow, C.rrow
    les = ExitStack()
    NJ0 = 11
    bsF = [C.sbuf(les, [128, 8 * CH], BF16, "bsF") for _ in range(CT - 1)]
    bw = C.sbuf(les, [128, NJ0 * CH], BF16, "bw")

    def bs_src(c, j):
        if c < CT - 1:
            return bsF[c][:, j * CH:(j + 1) * CH]
        return bw[:, j * CH:(j + 1) * CH]
    sx = [C.sbuf(les, [128, CH], BF16, "sx") for _ in range(CT)]
    t1 = C.sbuf(les, [128, CH], BF16, "spt1")
    t2 = C.sbuf(les, [128, CH], BF16, "spt2")
    mbuf = [C.sbuf(les, [128, CH], BF16, "spm") for _ in range(2)]
    xnf = C.sbuf(les, [128, CH], F32, "spxnf")
    tmp = C.sbuf(les, [128, CH], F32, "sptmp")
    gcolall = C.sbuf(les, [128, CT * 12], F32, "gcol")
    gcol = [gcolall[:, c * 12:(c + 1) * 12] for c in range(CT)]
    rtab = [C.sbuf(les, [128, 64], F32, "rtab") for _ in range(CT)]
    sblk = [C.sbuf(les, [128, 64 * 128], BF16, "sblk") for _ in range(2)]
    bwb = [C.sbuf(les, [128, CT * 128], BF16, "bwb") for _ in range(2)]
    stage = [C.sbuf(les, [128, CH], BF16, "spst") for _ in range(2)]
    ps_m = C.psum(les, [128, CH], F32, "sppsM")
    ps_r = C.psum(les, [128, CH], F32, "sppsR")
    pss = [C.psum(les, [128, CH], F32, "spps") for _ in range(2)]

    OFF_R1 = {1: 0, 2: 20, 3: 38}
    OFF_N2 = {1: 10, 2: 29, 3: 46}

    g_tok = P.dma("sync", gcolall[:].rearrange("p (c g) -> p c g", g=12),
                  C.ins["l1_grid"].rearrange("(c p) g -> p c g", p=128), "d_grid")
    P.wait("vector", g_tok)
    for c in range(CT):
        for p_ in (1, 2, 3):
            cnt = NJ0 - p_
            P.op("vector", lambda p, c=c, p_=p_, cnt=cnt: p.tensor_tensor(
                rtab[c][:, OFF_R1[p_]:OFF_R1[p_] + cnt], gcol[c][:, p_:p_ + cnt],
                gcol[c][:, 0:cnt], ALU.subtract))
            P.op("vector", lambda p, c=c, p_=p_, cnt=cnt: p.reciprocal(
                rtab[c][:, OFF_R1[p_]:OFF_R1[p_] + cnt],
                rtab[c][:, OFF_R1[p_]:OFF_R1[p_] + cnt]))
            P.op("vector", lambda p, c=c, p_=p_, cnt=cnt: p.tensor_tensor(
                rtab[c][:, OFF_N2[p_]:OFF_N2[p_] + cnt], gcol[c][:, 1:1 + cnt],
                gcol[c][:, p_ + 1:p_ + 1 + cnt], ALU.subtract))
            P.op("vector", lambda p, c=c, p_=p_, cnt=cnt: p.reciprocal(
                rtab[c][:, OFF_N2[p_]:OFF_N2[p_] + cnt],
                rtab[c][:, OFF_N2[p_]:OFF_N2[p_] + cnt]))

    sp_tok = [None, None]
    bw_tok = [None, None]
    blk_use = [None, None]
    st_tok = [None, None]
    mm_all = None
    for ch in range(NCH):
        sl = slice(ch * CH, (ch + 1) * CH)
        P.wait("tensor", P.cur("s_vector"))
        P.op("tensor", lambda p, sl=sl: p.matmul(
            ps_m[:], C.ones_f[0:1, :], mrow[0:1, sl], start=True, stop=True))
        bt = P.op("tensor", lambda p, sl=sl: p.matmul(
            ps_r[:], C.ones_f[0:1, :], rrow[0:1, sl], start=True, stop=True),
            inc="s_tensor")
        P.wait("vector", bt)
        P.wait("vector", mm_all)
        P.wait("scalar", mm_all)
        sil_tok = None
        cp_pend = None
        for c in range(CT):
            P.op("vector", lambda p, c=c, sl=sl: p.tensor_tensor(
                tmp[:], C.resid[c][:, sl], ps_m[:], ALU.subtract))
            P.wait("vector", sil_tok)
            xt = P.op("vector", lambda p: p.tensor_tensor(xnf[:], tmp[:], ps_r[:], ALU.mult))
            P.wait("scalar", xt)
            sil_tok = P.op("scalar", lambda p, c=c: p.activation(sx[c][:], xnf[:], AF.Silu))
            P.wait("vector", cp_pend)
            for j in range(NJ0):
                P.op("vector", lambda p, c=c, j=j: p.tensor_scalar(
                    t1[:], xnf[:], gcol[c][:, j:j + 1], None, ALU.is_ge))
                P.op("vector", lambda p, c=c, j=j: p.tensor_scalar(
                    t2[:], xnf[:], gcol[c][:, j + 1:j + 2], None, ALU.is_lt))
                P.op("vector", lambda p, j=j: p.tensor_tensor(
                    bw[:, j * CH:(j + 1) * CH], t1[:], t2[:], ALU.mult))
            for p_ in (1, 2, 3):
                for j in range(NJ0 - p_):
                    P.op("vector", lambda p, c=c, p_=p_, j=j: p.tensor_scalar(
                        t1[:], xnf[:], gcol[c][:, j:j + 1],
                        rtab[c][:, OFF_R1[p_] + j:OFF_R1[p_] + j + 1],
                        ALU.subtract, ALU.mult))
                    P.op("vector", lambda p, c=c, p_=p_, j=j: p.tensor_scalar(
                        t2[:], xnf[:], gcol[c][:, j + p_ + 1:j + p_ + 2],
                        rtab[c][:, OFF_N2[p_] + j:OFF_N2[p_] + j + 1],
                        ALU.subtract, ALU.mult))
                    P.op("vector", lambda p, j=j: p.tensor_tensor(
                        t1[:], t1[:], bw[:, j * CH:(j + 1) * CH], ALU.mult))
                    P.op("vector", lambda p, j=j: p.tensor_tensor(
                        t2[:], t2[:], bw[:, (j + 1) * CH:(j + 2) * CH], ALU.mult))
                    P.op("vector", lambda p, j=j: p.tensor_tensor(
                        bw[:, j * CH:(j + 1) * CH], t1[:], t2[:], ALU.add))
            if c < CT - 1:
                cp_gate = P.cur("s_vector")
                P.wait("scalar", cp_gate)
                for j in range(8):
                    P.op("scalar", lambda p, c=c, j=j: p.activation(
                        bsF[c][:, j * CH:(j + 1) * CH], bw[:, j * CH:(j + 1) * CH],
                        AF.Copy), inc="s_scalar")
                cp_pend = P.cur("s_scalar")
        bs_done = P.cur("s_vector")
        sx_done = P.cur("s_scalar")
        bs_done_sc = P.cur("s_scalar")

        ev_tok = [None, None]
        for o in range(24):
            buf = o % 2
            P.wait("sync", blk_use[buf])
            svw = sblk[buf][:].rearrange("p (j c o) -> p j c o", c=CT, o=128)
            for j in range(8):
                sp_tok[buf] = P.dma(
                    "sync", svw[:, j, :, :],
                    C.swb_s[j, :, o * 128:(o + 1) * 128].rearrange(
                        "(c p) o -> p c o", p=128), f"d_sp{buf}")
            P.wait("gpsimd", blk_use[buf])
            bvw = bwb[buf][:].rearrange("p (c o) -> p c o", o=128)
            bw_tok[buf] = P.dma(
                "gpsimd", bvw[:],
                C.ins["l1_bwt"][:, o * 128:(o + 1) * 128].rearrange("(c p) o -> p c o", p=128),
                f"d_bw{buf}")
            pb = o % 2
            P.wait("tensor", ev_tok[pb])
            P.wait("tensor", bw_tok[buf])
            P.wait("tensor", sx_done)
            for c in range(CT):
                P.op("tensor", lambda p, c=c, buf=buf, pb=pb: p.matmul(
                    pss[pb][:], bwb[buf][:].rearrange("p (c o) -> p c o", o=128)[:, c, :],
                    sx[c][:], start=(c == 0), stop=False))
            P.wait("tensor", sp_tok[buf])
            P.wait("tensor", bs_done)
            P.wait("tensor", bs_done_sc)
            for j in range(8):
                for c in range(CT):
                    P.op("tensor", lambda p, j=j, c=c, buf=buf, pb=pb: p.matmul(
                        pss[pb][:], sblk[buf][:].rearrange(
                            "p (j c o) -> p j c o", c=CT, o=128)[:, j, c, :],
                        bs_src(c, j),
                        start=False, stop=(j == 7 and c == CT - 1)),
                        inc="s_tensor" if (j == 7 and c == CT - 1) else None)
            mmt = P.cur("s_tensor")
            blk_use[buf] = mmt
            P.wait("scalar", mmt)
            P.wait("scalar", st_tok[buf])
            ev = P.op("scalar", lambda p, buf=buf, pb=pb: p.activation(
                stage[buf][:], pss[pb][:], AF.Copy), inc="s_scalar")
            ev_tok[pb] = ev
            P.wait("sync", ev)
            st_tok[buf] = P.dma("sync", C.qkv_s[o * 128:(o + 1) * 128, sl], stage[buf][:],
                                f"d_sq{buf}")
        mm_all = P.cur("s_tensor")
    P.waits("sync", [st_tok[0], st_tok[1]])
    les.close()


def emit_routing_l1(C, es, route_es):
    nc, P = C.nc, C.P
    les = ExitStack()
    C.routeT = C.sbuf(route_es, [32, T], BF16, "routeT")
    C.esel = C.sbuf(route_es, [32, 32 * 128], BF16, "esel")
    C.id_f = C.sbuf(route_es, [128, 128], F32, "idf")
    routeT, id_f, xn = C.routeT, C.id_f, C.xn
    P.dma("gpsimd", C.esel[:], C.ins["c_esel"], "d_esel_g")
    et = P.dma("sync", C.id_f[:], C.ins["c_ident"], "d_esel")
    P.wait("tensor", et)

    rw = C.sbuf(les, [128, CT * 32], BF16, "rw")
    lgT = C.sbuf(les, [32, T], F32, "lgT")
    br = C.sbuf(les, [1, 32], F32, "rbr")
    bB = C.sbuf(les, [128, 32], F32, "rbB")
    ltok = C.sbuf(les, [128, 32], F32, "rltok")
    sc = C.sbuf(les, [128, 32], F32, "rsc")
    sfc = C.sbuf(les, [128, 32], F32, "rsfc")
    tmp = C.sbuf(les, [128, 32], F32, "rtmp")
    tc = C.sbuf(les, [128, 32], F32, "rtc")
    msk = C.sbuf(les, [128, 32], F32, "rmsk")
    m8 = C.sbuf(les, [128, 8], F32, "rm8")
    gsc = C.sbuf(les, [128, 4], F32, "rgsc")
    g2 = C.sbuf(les, [128, 4], F32, "rg2")
    c1 = C.sbuf(les, [128, 1], F32, "rc1")
    c2 = C.sbuf(les, [128, 1], F32, "rc2")
    ps_l = C.psum(les, [128, CH], F32, "rops")
    ps_t = C.psum(les, [128, 512], F32, "rops2")

    wv = C.ins["l1_rwt"].rearrange("(a p) e -> p a e", p=128)
    rwv = rw[:].rearrange("p (a e) -> p a e", e=32)
    wtok = P.dma("gpsimd", rwv[:], wv[:], "d_rw")
    btok = P.dma("sync", br[:], C.ins["l1_rbias"], "d_rb")
    P.wait("tensor", wtok)
    P.wait("tensor", btok)
    bc = P.op("tensor", lambda p: p.matmul(
        ps_t[:, 0:32], C.ones_f[0:1, :], br[0:1, :], start=True, stop=True), inc="s_tensor")
    P.wait("vector", bc)
    bB_tok = P.op("vector", lambda p: p.tensor_copy(bB[:], ps_t[:, 0:32]))

    prev_ev = None
    for ch in range(NCH):
        sl = slice(ch * CH, (ch + 1) * CH)
        P.wait("tensor", prev_ev)
        for c in range(CT):
            P.op("tensor", lambda p, c=c, sl=sl: p.matmul(
                ps_l[0:32, :], rwv[:, c, :], xn[c][:, sl],
                start=(c == 0), stop=(c == CT - 1)),
                inc="s_tensor" if c == CT - 1 else None)
        P.wait("vector", P.cur("s_tensor"))
        prev_ev = P.op("vector", lambda p, sl=sl: p.tensor_copy(lgT[:, sl], ps_l[0:32, :]))

    prev_wr = None
    for tt in range(T // 128):
        tsl = slice(tt * 128, (tt + 1) * 128)
        P.wait("tensor", prev_wr)
        tr = P.op("tensor", lambda p, tsl=tsl: p.transpose(
            ps_t[0:128, 64:96], lgT[0:32, tsl], id_f[0:32, 0:32]), inc="s_tensor")
        P.wait("vector", tr)
        ct = P.op("vector", lambda p: p.tensor_copy(ltok[:], ps_t[0:128, 64:96]))
        P.wait("scalar", ct)
        sg = P.op("scalar", lambda p: p.activation(sc[:], ltok[:], AF.Sigmoid))
        P.wait("vector", sg)
        P.wait("vector", bB_tok)
        P.op("vector", lambda p: p.tensor_tensor(sfc[:], sc[:], bB[:], ALU.add))
        for g in range(4):
            s8 = slice(8 * g, 8 * g + 8)
            P.op("vector", lambda p, s8=s8: p.tensor_reduce(c1[:], sfc[:, s8], AX.X, ALU.max))
            P.op("vector", lambda p, s8=s8: p.tensor_scalar(
                m8[:], sfc[:, s8], c1[:, 0:1], None, ALU.is_equal))
            P.op("vector", lambda p, s8=s8: p.scalar_tensor_tensor(
                m8[:], m8[:], -1e30, sfc[:, s8], ALU.mult, ALU.add))
            P.op("vector", lambda p: p.tensor_reduce(c2[:], m8[:], AX.X, ALU.max))
            P.op("vector", lambda p, g=g: p.tensor_tensor(
                gsc[:, g:g + 1], c1[:], c2[:], ALU.add))
        P.op("vector", lambda p: p.tensor_reduce(c1[:], gsc[:], AX.X, ALU.max))
        P.op("vector", lambda p: p.tensor_scalar(g2[:], gsc[:], c1[:, 0:1], None,
                                                 ALU.is_equal))
        P.op("vector", lambda p: p.scalar_tensor_tensor(g2[:], g2[:], -1e30, gsc[:],
                                                        ALU.mult, ALU.add))
        P.op("vector", lambda p: p.tensor_reduce(c2[:], g2[:], AX.X, ALU.max))
        for g in range(4):
            s8 = slice(8 * g, 8 * g + 8)
            P.op("vector", lambda p, g=g: p.tensor_scalar(
                c1[:], gsc[:, g:g + 1], c2[:, 0:1], None, ALU.is_ge))
            P.op("vector", lambda p, s8=s8: p.tensor_scalar(
                tmp[:, s8], sfc[:, s8], c1[:, 0:1], None, ALU.mult))
        P.op("vector", lambda p: p.tensor_copy(tc[:], tmp[:]))
        for _ in range(3):
            P.op("vector", lambda p: p.tensor_reduce(c1[:], tc[:], AX.X, ALU.max))
            P.op("vector", lambda p: p.tensor_scalar(msk[:], tc[:], c1[:, 0:1], None,
                                                     ALU.is_equal))
            P.op("vector", lambda p: p.scalar_tensor_tensor(
                tc[:], msk[:], -1e30, tc[:], ALU.mult, ALU.add))
        P.op("vector", lambda p: p.tensor_reduce(c1[:], tc[:], AX.X, ALU.max))
        P.op("vector", lambda p: p.tensor_scalar(msk[:], tmp[:], c1[:, 0:1], None,
                                                 ALU.is_ge))
        P.op("vector", lambda p: p.tensor_tensor(tmp[:], sc[:], msk[:], ALU.mult))
        P.op("vector", lambda p: p.tensor_reduce(c2[:], tmp[:], AX.X, ALU.add))
        P.op("vector", lambda p: p.tensor_scalar(c2[:], c2[:], 1e-20, None,
                                                 ALU.add))
        P.op("vector", lambda p: p.reciprocal(c1[:], c2[:]))
        P.op("vector", lambda p: p.tensor_scalar(c1[:], c1[:], 2.5, None, ALU.mult))
        gt = P.op("vector", lambda p: p.tensor_scalar(tmp[:], tmp[:], c1[:, 0:1], None,
                                                      ALU.mult))
        P.wait("tensor", gt)
        tr2 = P.op("tensor", lambda p: p.transpose(
            ps_t[0:32, 96:224], tmp[:], id_f[:]), inc="s_tensor")
        P.wait("vector", tr2)
        prev_wr = P.op("vector", lambda p, tsl=tsl: p.tensor_copy(
            routeT[0:32, tsl], ps_t[0:32, 96:224]))
        P.wait("tensor", prev_wr)
    les.close()


def emit_moe_l1(C, es):
    nc, P = C.nc, C.P
    xn, esel, routeT = C.xn, C.esel, C.routeT
    pool_es = ExitStack()
    pool = [C.psum(pool_es, [128, CH], F32, "l1ps") for _ in range(8)]

    sh_es = ExitStack()
    sgw = C.sbuf(sh_es, [128, CT * DM], BF16, "sgw")
    suw = C.sbuf(sh_es, [128, CT * DM], BF16, "suw")
    sdw = C.sbuf(sh_es, [128, CT * DM], BF16, "sdw")
    aSH = [C.sbuf(sh_es, [128, T], BF16, "aSH") for _ in range(CT)]
    sst = [C.sbuf(sh_es, [128, CH], BF16, "shst") for _ in range(2)]
    toks = []
    for nm, dst in (("l1_sgwt", sgw), ("l1_suwt", suw), ("l1_sdwt", sdw)):
        toks.append(P.dma("gpsimd", dst[:].rearrange("p (a o) -> p a o", o=DM),
                          C.ins[nm].rearrange("(a p) o -> p a o", p=128), "d_shw"))
    P.waits("tensor", toks)
    sgv = sgw[:].rearrange("p (a o) -> p a o", o=DM)
    suv = suw[:].rearrange("p (a o) -> p a o", o=DM)
    sdv = sdw[:].rearrange("p (a o) -> p a o", o=DM)

    ev_prev = [None, None]
    for ht in range(CT):
        for sub in range(NCH):
            ssl = slice(sub * CH, (sub + 1) * CH)
            pb = (ht * NCH + sub) % 2
            P.wait("tensor", ev_prev[pb])
            for c in range(CT):
                P.op("tensor", lambda p, c=c, ht=ht, ssl=ssl, pb=pb: p.matmul(
                    pool[pb][:], sgv[:, c, ht * 128:(ht + 1) * 128], xn[c][:, ssl],
                    start=(c == 0), stop=(c == CT - 1)))
            for c in range(CT):
                P.op("tensor", lambda p, c=c, ht=ht, ssl=ssl, pb=pb: p.matmul(
                    pool[4 + pb][:], suv[:, c, ht * 128:(ht + 1) * 128], xn[c][:, ssl],
                    start=(c == 0), stop=(c == CT - 1)),
                    inc="s_tensor" if c == CT - 1 else None)
            mm = P.cur("s_tensor")
            P.wait("scalar", mm)
            st = P.op("scalar", lambda p, pb=pb: p.activation(sst[pb][:], pool[pb][:],
                                                              AF.Silu))
            P.wait("vector", st)
            ev_prev[pb] = P.op("vector", lambda p, ht=ht, ssl=ssl, pb=pb: p.tensor_tensor(
                aSH[ht][:, ssl], sst[pb][:], pool[4 + pb][:], ALU.mult))
    a_done = P.cur("s_vector")
    for sub in range(NCH):
        ssl = slice(sub * CH, (sub + 1) * CH)
        for ht in range(CT):
            if ht == 0:
                P.wait("tensor", a_done)
                P.wait("tensor", P.cur("s_scalar"))
                P.wait("tensor", P.cur("s_vector"))
            for oi in range(CT):
                P.op("tensor", lambda p, ht=ht, oi=oi, ssl=ssl: p.matmul(
                    pool[oi][:], sdv[:, ht, oi * 128:(oi + 1) * 128], aSH[ht][:, ssl],
                    start=(ht == 0), stop=(ht == CT - 1)),
                    inc="s_tensor" if (ht == CT - 1 and oi == CT - 1) else None)
        bdone = P.cur("s_tensor")
        P.wait("vector", bdone)
        for oi in range(CT):
            P.op("vector", lambda p, oi=oi, ssl=ssl: p.tensor_tensor(
                C.resid[oi][:, ssl], pool[oi][:], C.resid[oi][:, ssl], ALU.add))
    P.wait("tensor", P.cur("s_vector"))
    sh_es.close()

    ex_es = ExitStack()
    NE = 32
    wg = [C.sbuf(ex_es, [128, CT * 512], BF16, "wg") for _ in range(2)]
    wu = [C.sbuf(ex_es, [128, CT * 512], BF16, "wu") for _ in range(2)]
    wd = [C.sbuf(ex_es, [128, 4 * DM], BF16, "wd") for _ in range(2)]
    aE = [C.sbuf(ex_es, [128, T], BF16, "aE1") for _ in range(4)]
    wBs = [C.sbuf(ex_es, [128, CH], BF16, "wBs") for _ in range(NCH)]
    gst = [C.sbuf(ex_es, [128, CH], BF16, "gst") for _ in range(2)]
    ust = [C.sbuf(ex_es, [128, CH], BF16, "ust") for _ in range(2)]

    w_tok = [None, None]
    w_use = [None, None]

    def load_we(e, buf):
        P.wait("gpsimd", w_use[buf])
        P.dma("gpsimd", wg[buf][:].rearrange("p (a o) -> p a o", o=512),
              C.ins["l1_gwt"][e].rearrange("(a p) o -> p a o", p=128), f"d_we{buf}")
        P.dma("gpsimd", wu[buf][:].rearrange("p (a o) -> p a o", o=512),
              C.ins["l1_uwt"][e].rearrange("(a p) o -> p a o", p=128), f"d_we{buf}")
        w_tok[buf] = P.dma("gpsimd", wd[buf][:].rearrange("p (b o) -> p b o", o=DM),
                           C.ins["l1_dwt"][e].rearrange("(b p) o -> p b o", p=128),
                           f"d_we{buf}")

    P.wait("gpsimd", P.cur("s_tensor"))
    P.wait("gpsimd", P.cur("s_vector"))
    P.wait("gpsimd", P.cur("s_scalar"))
    load_we(0, 0)
    wB_done = [None] * NCH
    ev_prev = {}
    for e in range(NE):
        buf = e % 2
        if e + 1 < NE:
            load_we(e + 1, (e + 1) % 2)
        last_copy = wB_done[NCH - 1]
        for ch in range(NCH):
            P.wait("tensor", last_copy)
            t = P.op("tensor", lambda p, e=e, ch=ch: p.matmul(
                pool[2][:], esel[0:32, e * 128:(e + 1) * 128],
                routeT[0:32, ch * CH:(ch + 1) * CH], start=True, stop=True),
                inc="s_tensor")
            P.wait("vector", t)
            last_copy = wB_done[ch] = P.op("vector", lambda p, ch=ch: p.tensor_copy(
                wBs[ch][:], pool[2][:]))
        wv_tok = P.cur("s_vector")
        P.wait("tensor", w_tok[buf])
        wgv = wg[buf][:].rearrange("p (a o) -> p a o", o=512)
        wuv = wu[buf][:].rearrange("p (a o) -> p a o", o=512)
        wdv = wd[buf][:].rearrange("p (b o) -> p b o", o=DM)
        for ht in range(4):
            for sub in range(NCH):
                ssl = slice(sub * CH, (sub + 1) * CH)
                pb = (ht * NCH + sub) % 2
                P.wait("tensor", ev_prev.get(("A", pb)))
                for c in range(CT):
                    P.op("tensor", lambda p, c=c, ht=ht, ssl=ssl, pb=pb, wgv=wgv: p.matmul(
                        pool[pb][:], wgv[:, c, ht * 128:(ht + 1) * 128], xn[c][:, ssl],
                        start=(c == 0), stop=(c == CT - 1)))
                for c in range(CT):
                    P.op("tensor", lambda p, c=c, ht=ht, ssl=ssl, pb=pb, wuv=wuv: p.matmul(
                        pool[4 + pb][:], wuv[:, c, ht * 128:(ht + 1) * 128], xn[c][:, ssl],
                        start=(c == 0), stop=(c == CT - 1)),
                        inc="s_tensor" if c == CT - 1 else None)
                mm = P.cur("s_tensor")
                P.wait("scalar", mm)
                st = P.op("scalar", lambda p, pb=pb: p.activation(
                    gst[pb][:], pool[pb][:], AF.Silu))
                P.wait("vector", st)
                P.wait("vector", wv_tok)
                P.op("vector", lambda p, pb=pb: p.tensor_tensor(
                    ust[pb][:], gst[pb][:], pool[4 + pb][:], ALU.mult))
                ev_prev[("A", pb)] = P.op(
                    "vector", lambda p, ht=ht, ssl=ssl, pb=pb, sub=sub: p.tensor_tensor(
                        aE[ht][:, ssl], ust[pb][:], wBs[sub][:], ALU.mult))
        a_done_v = P.cur("s_vector")
        for sub in range(NCH):
            ssl = slice(sub * CH, (sub + 1) * CH)
            for w in range(2):
                if sub == 0:
                    P.wait("tensor", a_done_v)
                    P.wait("tensor", P.cur("s_scalar"))
                P.wait("tensor", ev_prev.get(("B", w)))
                for ht in range(4):
                    for oi4 in range(4):
                        oi = w * 4 + oi4
                        P.op("tensor", lambda p, ht=ht, oi=oi, ssl=ssl, wdv=wdv: p.matmul(
                            pool[oi][:], wdv[:, ht, oi * 128:(oi + 1) * 128], aE[ht][:, ssl],
                            start=(ht == 0), stop=(ht == 3)),
                            inc="s_tensor" if (ht == 3 and oi4 == 3) else None)
                bdone = P.cur("s_tensor")
                P.wait("vector", bdone)
                for oi4 in range(4):
                    oi = w * 4 + oi4
                    P.op("vector", lambda p, oi=oi, ssl=ssl: p.tensor_tensor(
                        C.resid[oi][:, ssl], pool[oi][:], C.resid[oi][:, ssl], ALU.add))
                ev_prev[("B", w)] = P.cur("s_vector")
        w_use[buf] = P.cur("s_tensor")
        P.wait("tensor", P.cur("s_vector"))
    ex_es.close()
    pool_es.close()



_ESEL = None


def _esel_host():
    global _ESEL
    if _ESEL is None:
        e = np.zeros((32, 32 * 128), dtype=np.float32)
        for k in range(32):
            e[k, k * 128:(k + 1) * 128] = 1.0
        _ESEL = e
    return _ESEL


GROUP_INPUTS = {
    "l0a": [("l0_wattn_t", [DM, 3072]), ("l0_wproj_t", [DM, DM]), ("l0_bproj_t", [128, CT])],
    "l0f": [("c_esel", [32, 32 * 128]), ("l0_gwt", [DM, 8]),
            ("l0_w1t", [9, DM, 4096]), ("l0_w2t", [9, 4096, DM])],
    "l1a": [("l1_bwt", [DM, 3072]), ("l1_swt", [8, DM, 3072]), ("l1_sct", [DM, 3072]),
            ("l1_grid", [DM, 12]), ("l1_rot_t", [64, 64]),
            ("l1_owt", [DM, DM]), ("l1_obt", [128, CT])],
    "l1f": [("c_esel", [32, 32 * 128]), ("l1_rwt", [DM, 32]), ("l1_rbias", [1, 32]),
            ("l1_gwt", [32, DM, 512]), ("l1_uwt", [32, DM, 512]), ("l1_dwt", [32, 512, DM]),
            ("l1_sgwt", [DM, DM]), ("l1_suwt", [DM, DM]), ("l1_sdwt", [DM, DM])],
}


def prep_host(inputs, groups):
    h = {"c_ident": np.eye(128, dtype=np.float32)}
    cat = np.concatenate
    if "l0a" in groups:
        h["l0_wattn_t"] = np.ascontiguousarray(inputs["L0_attn_W_attn"].T)
        h["l0_wproj_t"] = np.ascontiguousarray(inputs["L0_attn_W_proj"].T)
        h["l0_bproj_t"] = np.ascontiguousarray(inputs["L0_attn_b_proj"].reshape(CT, 128).T)
    if "l0f" in groups:
        h["c_esel"] = _esel_host()
        h["l0_gwt"] = np.ascontiguousarray(inputs["L0_ffn_gate_w"].T)
        w1 = cat([inputs["L0_ffn_shared_w1"][None], inputs["L0_ffn_experts_w1"]], 0)
        w2 = cat([inputs["L0_ffn_shared_w2"][None], inputs["L0_ffn_experts_w2"]], 0)
        h["l0_w1t"] = np.ascontiguousarray(w1.transpose(0, 2, 1))
        h["l0_w2t"] = np.ascontiguousarray(w2.transpose(0, 2, 1))
    if "l1a" in groups:
        h["l1_bwt"] = np.ascontiguousarray(inputs["L1_attn_base_weight"].T)
        h["l1_swt"] = np.ascontiguousarray(inputs["L1_attn_spline_weight"].transpose(2, 1, 0))
        h["l1_sct"] = np.ascontiguousarray(inputs["L1_attn_spline_scaler"].T)
        h["l1_grid"] = np.ascontiguousarray(inputs["L1_attn_grid"])
        h["l1_rot_t"] = np.ascontiguousarray(inputs["L1_attn_rotation_matrix"].T)
        h["l1_owt"] = np.ascontiguousarray(inputs["L1_attn_out_w"].T)
        h["l1_obt"] = np.ascontiguousarray(inputs["L1_attn_out_b"].reshape(CT, 128).T)
    if "l1f" in groups:
        h["c_esel"] = _esel_host()
        h["l1_rwt"] = np.ascontiguousarray(inputs["L1_ffn_router_weight"].T)
        h["l1_rbias"] = np.ascontiguousarray(inputs["L1_ffn_router_bias"].reshape(1, 32))
        h["l1_gwt"] = np.ascontiguousarray(inputs["L1_ffn_gate_w"].transpose(0, 2, 1))
        h["l1_uwt"] = np.ascontiguousarray(inputs["L1_ffn_up_w"].transpose(0, 2, 1))
        h["l1_dwt"] = np.ascontiguousarray(inputs["L1_ffn_down_w"].transpose(0, 2, 1))
        h["l1_sgwt"] = np.ascontiguousarray(inputs["L1_ffn_shared_gate_w"].T)
        h["l1_suwt"] = np.ascontiguousarray(inputs["L1_ffn_shared_up_w"].T)
        h["l1_sdwt"] = np.ascontiguousarray(inputs["L1_ffn_shared_down_w"].T)
    return h


def build_group(groups):
    nc = bass.Bass("TRN2", target_bir_lowering=False, debug=False)
    C = Ctx(nc)
    C.ins = {}
    seen = set()
    decls = [("resid_in", [DM, T]), ("c_ident", [128, 128])]
    for g in groups:
        decls += GROUP_INPUTS[g]
    for nm, shp in decls:
        if nm in seen:
            continue
        seen.add(nm)
        C.ins[nm] = nc.dram_tensor(nm, shp, F32, kind="ExternalInput").ap()
    out = nc.dram_tensor("resid_out", [DM, T], F32, kind="ExternalOutput").ap()
    if "l0a" in groups or "l1a" in groups:
        C.qkv_s = nc.dram_tensor("qkv_s", [3072, T], BF16, kind="Internal").ap()
    if "l1a" in groups:
        C.swb_s = nc.dram_tensor("swb_s", [8, DM, 3072], BF16, kind="Internal").ap()

    es = C.es
    emit_common_setup(C, es)
    C.P.barrier()
    for g in groups:
        if g == "l0a":
            xn_es = ExitStack()
            st_es = ExitStack()
            emit_ln(C, es, make_xn=True, xn_es=xn_es, stats_es=st_es)
            C.P.barrier()
            st_es.close()
            emit_dense_to_scratch(C, es, "l0_wattn_t", 3072, C.xn, C.qkv_s)
            xn_es.close()
            C.P.barrier()
            attn_es = ExitStack()
            emit_attention(C, es, layer=0, ctx_es=attn_es)
            C.P.barrier()
            emit_proj_residual(C, es, "l0_wproj_t", "l0_bproj_t")
            attn_es.close()
            C.P.barrier()
        elif g == "l0f":
            xn_es = ExitStack()
            st_es = ExitStack()
            emit_ln(C, es, make_xn=True, xn_es=xn_es, stats_es=st_es)
            C.P.barrier()
            st_es.close()
            route_es = ExitStack()
            emit_routing_l0(C, es, route_es)
            C.P.barrier()
            emit_moe_l0(C, es)
            route_es.close()
            xn_es.close()
            C.P.barrier()
        elif g == "l1a":
            st_es = ExitStack()
            emit_ln(C, es, make_xn=False, stats_es=st_es)
            C.P.barrier()
            emit_sw_spill(C, es)
            C.P.barrier()
            emit_spline_qkv(C, es)
            C.P.barrier()
            st_es.close()
            attn_es = ExitStack()
            emit_attention(C, es, layer=1, ctx_es=attn_es)
            C.P.barrier()
            emit_proj_residual(C, es, "l1_owt", "l1_obt")
            attn_es.close()
            C.P.barrier()
        elif g == "l1f":
            xn_es = ExitStack()
            st_es = ExitStack()
            emit_ln(C, es, make_xn=True, xn_es=xn_es, stats_es=st_es)
            C.P.barrier()
            st_es.close()
            route_es = ExitStack()
            emit_routing_l1(C, es, route_es)
            C.P.barrier()
            emit_moe_l1(C, es)
            route_es.close()
            xn_es.close()
            C.P.barrier()
    toks = [C.P.dma("sync", out[c * 128:(c + 1) * 128, :], C.resid[c][:], "d_out")
            for c in range(CT)]
    C.P.wait("sync", C.P.cur("d_out"))
    C.P.build()
    es.close()
    return nc


_BUILD_CACHE = {}


def _get_prog(groups):
    key = tuple(groups)
    if key not in _BUILD_CACHE:
        _BUILD_CACHE[key] = build_group(list(groups))
    return _BUILD_CACHE[key]


LAUNCH_PLAN = [["l0a", "l0f", "l1a", "l1f"]]


def kernel(**inputs):
    x = inputs["L0_attn_x"]
    resid = [np.ascontiguousarray(x[b].T) for b in range(NCORES)]
    for groups in LAUNCH_PLAN:
        h = prep_host(inputs, groups)
        nc = _get_prog(tuple(groups))
        in_maps = []
        for b in range(NCORES):
            m = dict(h)
            m["resid_in"] = resid[b]
            in_maps.append(m)
        res = run_bass_kernel_spmd(nc, in_maps, core_ids=list(range(NCORES)))
        resid = [r["resid_out"] for r in res.results]
    return np.stack([r.T for r in resid]).astype(np.float32)
